# version 15
# baseline (speedup 1.0000x reference)
import math
import numpy as np
import concourse.bass as bass
import concourse.mybir as mybir
from concourse.bass_utils import run_bass_kernel_spmd

F32 = mybir.dt.float32
BF16 = mybir.dt.bfloat16
I32 = mybir.dt.int32
AF = mybir.ActivationFunctionType
ALU = mybir.AluOpType
AX = mybir.AxisListType
ts = bass.ts

B, S, H = 2, 8192, 2048
NH, HD = 16, 128
NPAIR = 8
TOK = 2048
NSB = S // 512
NQB = TOK // 512
NKT = S // 128
E, G, GS, II = 64, 8, 8, 1408
IC = II // 128
MC = H // 128
SCALE = 1.0 / math.sqrt(HD)
EPS = 1e-5
NEG = -1.0e9


class Ctr:
    def __init__(self, es, nc, name):
        self.h = es.enter_context(nc.semaphore(name))
        self.n = 0

    def tick(self, ins, k=1):
        ins.then_inc(self.h, k)
        self.n += k
        return self.n

    def wait(self, eng, n=None):
        v = self.n if n is None else n
        if v > 0:
            eng.wait_ge(self.h, v)


def build(dbg=0, npairs=NPAIR, stages="full"):
    nc = bass.Bass("TRN2", target_bir_lowering=False, debug=False)
    dt = nc.dram_tensor

    xTb = dt("xTb", [H, S], F32, kind="ExternalInput").ap()
    xqT = dt("xqT", [H, TOK], F32, kind="ExternalInput").ap()
    xch = dt("xch", [TOK, H], F32, kind="ExternalInput").ap()
    wqT = dt("wqT", [H, H], F32, kind="ExternalInput").ap()
    wkT = dt("wkT", [H, H], F32, kind="ExternalInput").ap()
    wvT = dt("wvT", [H, H], F32, kind="ExternalInput").ap()
    woT = dt("woT", [H, H], F32, kind="ExternalInput").ap()
    gt2 = dt("gt2", [1, NH], F32, kind="ExternalInput").ap()
    csk = dt("csk", [1024, 2, S], F32, kind="ExternalInput").ap()
    csq = dt("csq", [1024, 2, TOK], F32, kind="ExternalInput").ap()
    gwT = dt("gwT", [H, E], F32, kind="ExternalInput").ap()
    muo = dt("muo", [G, H, GS], F32, kind="ExternalInput").ap()
    gpT = dt("gpT", [G, H, II], F32, kind="ExternalInput").ap()
    upT = dt("upT", [G, H, II], F32, kind="ExternalInput").ap()
    dnT = dt("dnT", [G, II, H], F32, kind="ExternalInput").ap()
    invm = dt("invm", [G, GS], I32, kind="ExternalInput").ap()
    iot = dt("iot", [E, 1], F32, kind="ExternalInput").ap()
    idn = dt("idn", [128, 128], F32, kind="ExternalInput").ap()

    out = dt("out", [TOK, H], F32, kind="ExternalOutput").ap()

    kfb = dt("kfb", [256, S], BF16, kind="Internal").ap()
    xblk = dt("xblk", [NSB, 128, MC, 512], BF16, kind="Internal").ap()
    xqblk = dt("xqblk", [NQB, 128, MC, 512], BF16, kind="Internal").ap()
    combT = dt("combT", [H, TOK], BF16, kind="Internal").ap()
    x2b16 = dt("x2b16", [TOK, H], BF16, kind="Internal").ap()
    x2f32 = dt("x2f32", [TOK, H], F32, kind="Internal").ap()
    y32 = dt("y32", [TOK, H], F32, kind="Internal").ap()
    sftd = dt("sftd", [G, GS, TOK], BF16, kind="Internal").ap()

    dbg_specs = {
        1: [("d_kt", [128, 2, S]), ("d_va", [128, 2, 64, 130]),
            ("d_mem", [128, 2, 132])],
        2: [("d_qt", [128, 2, TOK]), ("d_mo", [128, 2, 16, 132])],
        4: [("d_comb", [H, TOK])],
        8: [("d_x2", [TOK, H])],
        16: [("d_scl", [128, 16, 8]), ("d_sft", [G, GS, TOK])],
        32: [("d_y", [TOK, H])],
    }
    dout = {}
    for bit, specs in dbg_specs.items():
        if dbg & bit:
            for nm, shp in specs:
                dout[nm] = dt(nm, shp, F32, kind="ExternalOutput").ap()

    from contextlib import ExitStack
    es = ExitStack()
    C = es.enter_context
    with es:
        identf = C(nc.sbuf_tensor("identf", [128, 128], F32))
        ones1f = C(nc.sbuf_tensor("ones1f", [1, 128], F32))
        onescol = C(nc.sbuf_tensor("onescol", [128, 1], F32))
        gcols = C(nc.sbuf_tensor("gcols", [128, 32], F32))
        grow = C(nc.sbuf_tensor("grow", [1, 32], F32))
        iotc = C(nc.sbuf_tensor("iotc", [64, 1], F32))

        pS3 = [C(nc.psum_tensor(f"pS3_{i}", [128, 1536], F32)) for i in range(2)]
        pP = [C(nc.psum_tensor(f"pP{i}", [128, 512], F32)) for i in range(2)]
        pA = [pS3[0][:, 0:512], pS3[0][:, 512:1024], pS3[1][:, 0:512],
              pS3[1][:, 512:1024]]
        VB = [pS3[1][:, 0:512], pS3[1][:, 512:1024],
              pS3[0][:, 1024:1536], pS3[1][:, 1024:1536]]

        cXT = [Ctr(es, nc, f"cXT{i}") for i in range(2)]
        cBXL = [Ctr(es, nc, f"cBXL{i}") for i in range(2)]
        cBXS = [Ctr(es, nc, f"cBXS{i}") for i in range(2)]
        cCS = [Ctr(es, nc, f"cCS{i}") for i in range(2)]
        cWKV = [Ctr(es, nc, f"cWKV{i}") for i in range(2)]
        cWQ = Ctr(es, nc, "cWQ")
        cKFB = [Ctr(es, nc, f"cKFB{i}") for i in range(2)]
        cKFN = [Ctr(es, nc, f"cKFN{i}") for i in range(2)]
        cCB = [Ctr(es, nc, f"cCB{i}") for i in range(2)]
        cWO = Ctr(es, nc, "cWO")
        cCT = [Ctr(es, nc, f"cCT{i}") for i in range(2)]
        cXC = [Ctr(es, nc, f"cXC{i}") for i in range(2)]
        cX2W = Ctr(es, nc, "cX2W")
        cGW = Ctr(es, nc, "cGW")
        cX2T = Ctr(es, nc, "cX2T")
        cSFT = Ctr(es, nc, "cSFT")
        cMU2 = Ctr(es, nc, "cMU2")
        cGU = [Ctr(es, nc, f"cGU{i}") for i in range(2)]
        cDN = Ctr(es, nc, "cDN")
        cYL = [Ctr(es, nc, f"cYL{i}") for i in range(2)]
        cXL = [Ctr(es, nc, f"cXL{i}") for i in range(2)]
        cOB = [Ctr(es, nc, f"cOB{i}") for i in range(2)]
        cY = [Ctr(es, nc, f"cY{i}") for i in range(2)]
        cOUT = Ctr(es, nc, "cOUT")
        cDBG = Ctr(es, nc, "cDBG")
        cG2 = Ctr(es, nc, "cG2")
        cT = Ctr(es, nc, "cT")
        cV = Ctr(es, nc, "cV")
        cA = Ctr(es, nc, "cA")

        block = C(nc.Block())
        Tb, Vb, Ab, Yb, Gb = (block.tensor, block.vector, block.scalar,
                              block.sync, block.gpsimd)

        fr = {"xt0": 0, "xt1": 0, "cs0": 0, "cs1": 0, "pA01": 0,
              "pvb0": 0, "pvb1": 0, "pvb2": 0, "pvb3": 0,
              "kf0": 0, "kf1": 0, "wsl": 0,
              "mem": 0, "cb4": [0, 0], "rbev": 0, "pvfree": 0, "epi_done": 0}

        def _g0(g):
            cG2.tick(g.dma_start(out=identf[:, :], in_=idn), 16)
            cG2.tick(g.dma_start(out=iotc[:, :], in_=iot), 16)
            cG2.tick(g.dma_start(out=grow[0:1, 0:16], in_=gt2), 16)
        Gb(_g0)

        def _vones(v):
            v.memset(ones1f[:, :], 1.0)
            cV.tick(v.memset(onescol[:, :], 1.0))
        Vb(_vones)

        def _a0(s):
            cG2.wait(s)
            ins = s.activation(grow[0:1, 0:16], grow[0:1, 0:16], AF.Sigmoid)
            cA.tick(ins)
        Ab(_a0)

        def _v0(v):
            cA.wait(v)
            ins = v.tensor_scalar(grow[0:1, 16:32], grow[0:1, 0:16], -1.0, 1.0,
                                  ALU.mult, ALU.add)
            cV.tick(ins)
        Vb(_v0)

        def _t0(t):
            cV.wait(t)
            ins = t.matmul(pS3[0][:, 0:32], ones1f[0:1, :], grow[0:1, 0:32],
                           start=True, stop=True)
            cT.tick(ins)
        Tb(_t0)

        def _v1(v):
            cT.wait(v)
            ins = v.tensor_copy(gcols[:, :], pS3[0][:, 0:32])
            fr["mem"] = cV.tick(ins)
            fr["pA01"] = fr["mem"]
        Vb(_v1)

        p1 = ExitStack()

        class _NoAlloc:
            def __getitem__(self, k):
                raise RuntimeError("phase-1 tensor used in p2-only build")

        def P1(*args, **kw):
            if stages == "p2":
                return _NoAlloc()
            return p1.enter_context(nc.sbuf_tensor(*args, **kw))

        stg = [P1(f"stg{i}", [128, MC // 4, 512], BF16) for i in range(2)]
        kt = P1("kt", [128, 2, S], BF16)
        va = P1("va", [128, 2, 64, 130], BF16)
        qt = P1("qt", [128, 2, TOK], BF16)

        mem = P1("mem", [128, 2, 132], BF16)
        xt = [P1(f"xt{i}", [128, MC, 512], BF16) for i in range(2)]
        csb = [P1(f"csb{i}", [128, 2, 512], F32) for i in range(2)]
        wslkv = [P1(f"wslkv{i}", [128, MC, 2, 256], BF16) for i in range(2)]
        wslq = P1("wslq", [128, MC, 256], BF16)
        kf = [P1(f"kf{i}", [128, 2, 512], BF16) for i in range(2)]
        kfn = P1("kfn", [128, 8, 256], BF16)
        qft = P1("qft", [128, 2, TOK], BF16)
        rs = [P1(f"rs{i}", [128, 512], F32) for i in range(2)]
        esc = [P1(f"esc{i}", [128, 2, 512], BF16) for i in range(3)]
        pt = [P1(f"pt{i}", [128, 1536], BF16) for i in range(2)]
        cbt = P1("cbt", [128, 2, 512], BF16)
        sacc = P1("sacc", [128, 512], F32)
        rb = P1("rb", [128, 2, 512], BF16)

        if stages != "p2":
            def _vone(v):
                ins = v.memset(va[:, :, :, 128:130], 1.0)
                cV.tick(ins)
            Vb(_vone)
            fr["va1"] = cV.n

        xblk_w = {}
        bjc = [0]
        bxl_m = {}
        stg_rd = {}

        def emit_bounce(kind, bi, tgate=None):
            for qtr in range(4):
                j = bjc[0]
                bjc[0] += 1
                par = j % 2
                tg = tgate[qtr] if tgate is not None else 0

                def _gb(g, kind=kind, bi=bi, qtr=qtr, par=par, j=j, tg=tg):
                    cBXS[par].wait(g)
                    cT.wait(g, tg)
                    src = xTb if kind == "k" else xqT
                    ins = g.dma_start(
                        out=stg[par][:, :, :],
                        in_=src.rearrange("(c p) s -> p c s", p=128)
                        [:, ts(qtr, 4), ts(bi, 512)])
                    bxl_m[j] = cBXL[par].tick(ins, 16)
                Gb(_gb)

                def _yb(y, kind=kind, bi=bi, qtr=qtr, par=par):
                    cBXL[par].wait(y)
                    dst = xblk[bi] if kind == "k" else xqblk[bi]
                    ins = y.dma_start(out=dst[:, ts(qtr, 4), :],
                                      in_=stg[par][:, :, :])
                    xblk_w.setdefault((kind, bi), []).append(
                        (par, cBXS[par].tick(ins, 16)))
                Yb(_yb)

        def emit_rope(v, buf, dstA, dstB):
            v.tensor_tensor(rs[0][:, :], pA[0][:, 0:512], csb[buf][:, 0, :],
                            ALU.mult)
            ins = v.tensor_tensor(rs[1][:, :], pA[1][:, 0:512],
                                  csb[buf][:, 1, :], ALU.mult)
            v.wait_ge(cV.h, cV.tick(ins))
            ins = v.tensor_tensor(dstA, rs[0][:, :], rs[1][:, :], ALU.subtract)
            v.wait_ge(cV.h, cV.tick(ins))
            v.tensor_tensor(rs[0][:, :], pA[1][:, 0:512], csb[buf][:, 0, :],
                            ALU.mult)
            ins = v.tensor_tensor(rs[1][:, :], pA[0][:, 0:512],
                                  csb[buf][:, 1, :], ALU.mult)
            fr["pA01"] = cV.tick(ins)
            fr[f"cs{buf}"] = fr["pA01"]
            v.wait_ge(cV.h, fr["pA01"])
            ins = v.tensor_tensor(dstB, rs[0][:, :], rs[1][:, :], ALU.add)
            v.wait_ge(cV.h, cV.tick(ins))

        wsl_used = {}

        def emit_gw(hp, which="both"):
            def _gw(g, hp=hp, which=which):
                if which in ("both", "kv"):
                    cT.wait(g, wsl_used.get(hp - 2, 0))
                    for wi, w in ((0, wkT), (1, wvT)):
                        for d in range(2):
                            col = hp * 128 + d * 1024
                            ins = g.dma_start(
                                out=wslkv[hp % 2][:, :, wi, ts(d, 128)],
                                in_=w[:, col:col + 128].rearrange(
                                    "(c p) d -> p c d", p=128))
                            cWKV[hp % 2].tick(ins, 16)
                if which in ("both", "q"):
                    cT.wait(g, wsl_used.get(hp - 1, 0))
                    for d in range(2):
                        col = hp * 128 + d * 1024
                        ins = g.dma_start(
                            out=wslq[:, :, ts(d, 128)],
                            in_=wqT[:, col:col + 128].rearrange(
                                "(c p) d -> p c d", p=128))
                        cWQ.tick(ins, 16)
            Gb(_gw)

        def emit_kv_pair(hp):
            mm_used = {}
            kv_rdy = {}
            for sb in range(NSB):
                buf = sb % 2
                kmm = [0, 0]
                vmm = [0] * 4
                elu_mx = [0]
                kfn_rdy = [0]

                if sb < 2 and hp in prefetched_pairs:
                    pass
                else:
                    def _y(y, hp=hp, sb=sb, buf=buf):
                        if hp > 0:
                            cT.wait(y, fr[f"xt{buf}"])
                            for bpar, bval in xblk_w[("k", sb)]:
                                cBXS[bpar].wait(y, bval)
                            ins = y.dma_start(out=xt[buf][:, :, :],
                                              in_=xblk[sb])
                            cXT[buf].tick(ins, 16)
                        cV.wait(y, fr[f"cs{buf}"])
                        ins = y.dma_start(out=csb[buf][:, :, :],
                                          in_=csk[ts(hp, 128), :, ts(sb, 512)])
                        cCS[buf].tick(ins, 16)
                    Yb(_y)

                if hp == 0:
                    tb = cT.n
                    tgate = [stg_rd.get(4 * sb - 2, 0),
                             stg_rd.get(4 * sb - 1, 0), tb + 1, tb + 2]
                    emit_bounce("k", sb, tgate)
                    if sb == 8:
                        emit_gw(1, "kv")

                cv_base = [0]

                def _t(t, hp=hp, sb=sb, buf=buf, kmm=kmm, vmm=vmm,
                       cv_base=cv_base):
                    cv_base[0] = cV.n
                    cCS[buf].wait(t)
                    cWKV[hp % 2].wait(t)
                    if sb == 0:
                        cV.wait(t, fr["epi_done"])
                    cV.wait(t, fr["pA01"])
                    for st in range(4):
                        cV.wait(t, fr[f"pvb{st}"])
                    if hp == 0:
                        for qtr in range(4):
                            j = 4 * sb + qtr
                            cBXL[j % 2].wait(t, bxl_m[j])
                            ins = None
                            for q in range(4):
                                mc = 4 * qtr + q
                                for d in range(2):
                                    ins = t.matmul(
                                        pA[d][:, 0:512],
                                        wslkv[0][:, mc, 0, ts(d, 128)],
                                        stg[j % 2][:, q, :],
                                        start=(mc == 0), stop=(mc == MC - 1))
                                for st in range(4):
                                    ins = t.matmul(
                                        VB[st][:, 0:256],
                                        stg[j % 2][:, q, ts(st, 128)],
                                        wslkv[0][:, mc, 1, :],
                                        start=(mc == 0), stop=(mc == MC - 1))
                            stg_rd[j] = cT.tick(ins)
                        last = stg_rd[4 * sb + 3]
                        for d in range(2):
                            kmm[d] = last
                        for st in range(4):
                            vmm[st] = last
                    else:
                        cXT[buf].wait(t)
                        for d in range(2):
                            ins = None
                            for mc in range(MC):
                                ins = t.matmul(
                                    pA[d][:, 0:512],
                                    wslkv[hp % 2][:, mc, 0, ts(d, 128)],
                                    xt[buf][:, mc, :],
                                    start=(mc == 0), stop=(mc == MC - 1))
                            kmm[d] = cT.tick(ins)
                        for st in range(4):
                            ins = None
                            for mc in range(MC):
                                ins = t.matmul(
                                    VB[st][:, 0:256],
                                    xt[buf][:, mc, ts(st, 128)],
                                    wslkv[hp % 2][:, mc, 1, :],
                                    start=(mc == 0), stop=(mc == MC - 1))
                            vmm[st] = cT.tick(ins)
                        fr[f"xt{buf}"] = cT.n
                Tb(_t)

                if hp == 0:
                    assert stg_rd[4 * sb] == tgate[2] and \
                        stg_rd[4 * sb + 1] == tgate[3]

                def _v(v, hp=hp, sb=sb, buf=buf, kmm=kmm, vmm=vmm,
                       elu_mx=elu_mx, cv_base=cv_base):
                    sl = ts(sb, 512)
                    if dbg & 1 and hp == 1 and sb == 0:
                        cDBG.wait(v)
                    cT.wait(v, kmm[1])
                    emit_rope(v, buf, kt[:, 0, sl], kt[:, 1, sl])
                    assert fr["pA01"] == cv_base[0] + 3
                    cKFB[sb % 2].wait(v, fr[f"kf{sb % 2}"])
                    ins = None
                    for h in range(2):
                        v.tensor_scalar_min(esc[0][:, h, :], kt[:, h, sl], 0.0)
                        ins = v.tensor_scalar_max(esc[1][:, h, :], kt[:, h, sl],
                                                  0.0)
                    elu_mx[0] = cV.tick(ins)
                    for st in range(4):
                        cT.wait(v, vmm[st])
                        kti = sb * 4 + st
                        pv = VB[st]
                        v.tensor_copy(va[:, 0, kti, 0:128], pv[:, 0:128])
                        ins = v.tensor_copy(va[:, 1, kti, 0:128], pv[:, 128:256])
                        fr[f"pvb{st}"] = cV.tick(ins)
                Vb(_v)

                def _a(s, sb=sb, elu_mx=elu_mx):
                    cV.wait(s, elu_mx[0])
                    ins = s.activation(esc[2][:, :, :], esc[0][:, :, :], AF.Exp)
                    cA.tick(ins)
                Ab(_a)

                def _vf(v, sb=sb):
                    cA.wait(v)
                    ins = None
                    for h in range(2):
                        ins = v.tensor_tensor(kf[sb % 2][:, h, :],
                                              esc[2][:, h, :], esc[1][:, h, :],
                                              ALU.add)
                    cV.tick(ins)
                Vb(_vf)

                def _y2(g, hp=hp, sb=sb):
                    cV.wait(g)
                    ins = None
                    for h in range(2):
                        ins = g.dma_start(out=kfb[ts(h, 128), ts(sb, 512)],
                                          in_=kf[sb % 2][:, h, :])
                        cKFB[sb % 2].tick(ins, 16)
                    fr[f"kf{sb % 2}"] = cKFB[sb % 2].n
                Gb(_y2)

                def _al(s, sb=sb, rdy=kv_rdy, mm_used=mm_used):
                    cKFB[0].wait(s)
                    cKFB[1].wait(s)
                    if sb >= 2:
                        cT.wait(s, mm_used.get(sb - 2, 0))
                    ins = None
                    for st in range(4):
                        kti = sb * 4 + st
                        slot = (sb % 2) * 4 + st
                        ins = s.dma_start_transpose(
                            kfn[:, slot, :],
                            kfb[0:256, ts(kti, 128)])
                        cKFN[sb % 2].tick(ins, 16)
                    rdy[sb] = cKFN[sb % 2].n
                Ab(_al)

                def _tm(t, hp=hp, sb=sb - 1, kfn_rdy=None,
                        mm_used=mm_used, rdy=kv_rdy):
                    cKFN[sb % 2].wait(t, rdy[sb])
                    if sb == 0:
                        cV.wait(t, max(fr["mem"], fr["va1"]))
                    ins = None
                    for h in range(2):
                        reg = pP[h][:, 0:129]
                        for st in range(4):
                            kti = sb * 4 + st
                            slot = (sb % 2) * 4 + st
                            ins = t.matmul(reg, kfn[:, slot, ts(h, 128)],
                                           va[:, h, kti, 0:129],
                                           start=(sb == 0 and st == 0),
                                           stop=(sb == NSB - 1 and st == 3))
                        mm_used[sb] = cT.tick(ins)
                if sb >= 1:
                    Tb(_tm)

            kv_state[hp] = (mm_used, kv_rdy)

        def emit_kv_tail(hp):
            mm_used, kv_rdy = kv_state[hp]

            def _tml(t, mm_used=mm_used, rdy=kv_rdy):
                sb = NSB - 1
                cKFN[sb % 2].wait(t, rdy[sb])
                ins = None
                for h in range(2):
                    reg = pP[h][:, 0:129]
                    for st in range(4):
                        kti = sb * 4 + st
                        slot = (sb % 2) * 4 + st
                        ins = t.matmul(reg, kfn[:, slot, ts(h, 128)],
                                       va[:, h, kti, 0:129],
                                       start=False,
                                       stop=(st == 3))
                    mm_used[sb] = cT.tick(ins)
            Tb(_tml)

            def _vm(v):
                cT.wait(v)
                v.tensor_copy(mem[:, 0, 0:129], pP[0][:, 0:129])
                ins = v.tensor_copy(mem[:, 1, 0:129], pP[1][:, 0:129])
                fr["mem"] = cV.tick(ins)
            Vb(_vm)

            if dbg & 1 and hp == 0:
                def _dbgk(g):
                    cV.wait(g)
                    cDBG.tick(g.dma_start(out=dout["d_kt"], in_=kt[:, :, :]), 16)
                    cDBG.tick(g.dma_start(out=dout["d_va"], in_=va[:, :, :, :]), 16)
                    cDBG.tick(g.dma_start(out=dout["d_mem"][:, :, 0:129], in_=mem[:, :, 0:129]), 16)
                Gb(_dbgk)

        def emit_q_pair(hp):
            for qb in range(NQB):
                if hp == 0:
                    emit_bounce("q", qb)
                buf = qb % 2
                kmm = [0, 0]
                elu_mx = [0]

                def _y(y, hp=hp, qb=qb, buf=buf):
                    cT.wait(y, fr[f"xt{buf}"])
                    for bpar, bval in xblk_w[("q", qb)]:
                        cBXS[bpar].wait(y, bval)
                    ins = y.dma_start(out=xt[buf][:, :, :], in_=xqblk[qb])
                    cXT[buf].tick(ins, 16)
                    cV.wait(y, fr[f"cs{buf}"])
                    ins = y.dma_start(out=csb[buf][:, :, :],
                                      in_=csq[ts(hp, 128), :, ts(qb, 512)])
                    cCS[buf].tick(ins, 16)
                Yb(_y)

                def _t(t, hp=hp, qb=qb, buf=buf, kmm=kmm):
                    cXT[buf].wait(t)
                    cCS[buf].wait(t)
                    cWQ.wait(t)
                    cV.wait(t, fr["pA01"])
                    for d in range(2):
                        ins = None
                        for mc in range(MC):
                            ins = t.matmul(pA[d][:, 0:512],
                                           wslq[:, mc, ts(d, 128)],
                                           xt[buf][:, mc, :],
                                           start=(mc == 0), stop=(mc == MC - 1))
                        kmm[d] = cT.tick(ins)
                    fr[f"xt{buf}"] = cT.n
                Tb(_t)

                def _v(v, hp=hp, qb=qb, buf=buf, kmm=kmm, elu_mx=elu_mx):
                    sl = ts(qb, 512)
                    cT.wait(v, kmm[1])
                    emit_rope(v, buf, qt[:, 0, sl], qt[:, 1, sl])
                    ins = None
                    for h in range(2):
                        v.tensor_scalar_min(esc[0][:, h, :], qt[:, h, sl], 0.0)
                        ins = v.tensor_scalar_max(esc[1][:, h, :], qt[:, h, sl],
                                                  0.0)
                    elu_mx[0] = cV.tick(ins)
                Vb(_v)

                def _a(s, qb=qb, elu_mx=elu_mx):
                    cV.wait(s, elu_mx[0])
                    ins = s.activation(esc[2][:, :, :], esc[0][:, :, :], AF.Exp)
                    cA.tick(ins)
                Ab(_a)

                def _vq(v, qb=qb):
                    cA.wait(v)
                    ins = None
                    for h in range(2):
                        ins = v.tensor_tensor(qft[:, h, ts(qb, 512)],
                                              esc[2][:, h, :],
                                              esc[1][:, h, :], ALU.add)
                    cV.tick(ins)
                Vb(_vq)

            if dbg & 2 and hp == 0:
                def _dbgq(g):
                    cA.wait(g)
                    cV.wait(g)
                    cDBG.tick(g.dma_start(out=dout["d_qt"], in_=qt[:, :, :]), 16)
                Gb(_dbgq)

        def emit_attn_pair(hp):
            GRP = [3] * 21 + [1]
            NKG3 = len(GRP)
            goff = [0]
            for gsz in GRP:
                goff.append(goff[-1] + gsz)
            for h in range(2):
                hg = hp + 8 * h
                j0 = h * 1024 + hp * 128
                for qb in range(NQB):
                    sl = ts(qb, 512)
                    sc_done = {}
                    pv_done = {}
                    mm_marks = {}
                    cA_base = [0]
                    cV_base = [0]

                    def _t(t, hp=hp, h=h, qb=qb, sl=sl, cA_base=cA_base,
                           cV_base=cV_base, sc_done=sc_done, pv_done=pv_done,
                           mm_marks=mm_marks):
                        cA_base[0] = cA.n
                        cV_base[0] = cV.n
                        vb = cV_base[0]
                        for kg in range(NKG3):
                            ps = pS3[kg % 2]
                            cA.wait(t, cA_base[0] + max(0, kg - 1))
                            if kg == 0:
                                cV.wait(t, max(fr["rbev"], fr["pA01"],
                                               fr["pvb2"]))
                            elif kg == 1:
                                cV.wait(t, max(fr["pvb0"], fr["pvb1"],
                                               fr["pvb3"]))
                            ins = None
                            for j in range(GRP[kg]):
                                kti = goff[kg] + j
                                ins = t.matmul(ps[:, ts(j, 512)],
                                               kt[:, h, ts(kti, 128)],
                                               qt[:, h, sl],
                                               start=True, stop=True)
                            sc_done[kg] = cT.tick(ins)
                            if kg >= 1:
                                kgp = kg - 1
                                cA.wait(t, cA_base[0] + kgp + 1)
                                if kgp == 0:
                                    cV.wait(t, max(fr["pvfree"], fr["mem"]))
                                ins = None
                                for j in range(GRP[kgp]):
                                    kti = goff[kgp] + j
                                    ins = t.matmul(
                                        pP[0][:, 0:512],
                                        va[:, h, kti, 0:128],
                                        pt[kgp % 2][:, ts(j, 512)],
                                        start=(kti == 0),
                                        stop=(kti == NKT - 1))
                                pv_done[kgp] = cT.tick(ins)
                        kgp = NKG3 - 1
                        cA.wait(t, cA_base[0] + kgp + 1)
                        ins = t.matmul(pP[0][:, 0:512],
                                       va[:, h, goff[kgp], 0:128],
                                       pt[kgp % 2][:, 0:512],
                                       start=False, stop=True)
                        pv_done[kgp] = cT.tick(ins)
                        cV.wait(t, vb + 65)
                        cA.wait(t, cA_base[0] + NKG3 - 1)
                        ins = t.matmul(pS3[0][0:1, 0:512], onescol[:, 0:1],
                                       sacc[:, :], start=True, stop=True)
                        mm_marks["srow"] = cT.tick(ins)
                        ins = t.matmul(pS3[1][0:1, 0:512], mem[:, h, 128:129],
                                       qft[:, h, sl], start=True, stop=True)
                        mm_marks["lrow"] = cT.tick(ins)
                        cV.wait(t, max(fr["epi_done"], fr["mem"]))
                        ins = t.matmul(pP[1][:, 0:512], mem[:, h, 0:128],
                                       qft[:, h, sl], start=True, stop=True)
                        mm_marks["mo"] = cT.tick(ins)
                        cV.wait(t, vb + 66)
                        ins = t.matmul(pS3[0][:, 512:1024], ones1f[0:1, :],
                                       rs[0][0:1, 0:512], start=True, stop=True)
                        ins = t.matmul(pS3[1][:, 512:1024], ones1f[0:1, :],
                                       rs[1][0:1, 0:512], start=True,
                                       stop=True)
                        mm_marks["bc"] = cT.tick(ins)
                    Tb(_t)

                    def _ae(s, hp=hp, h=h, qb=qb, cA_base=cA_base,
                            cV_base=cV_base, sc_done=sc_done,
                            pv_done=pv_done):
                        for kg in range(NKG3):
                            cT.wait(s, sc_done[kg])
                            if kg >= 2:
                                cT.wait(s, pv_done[kg - 2])
                                cV.wait(s, cV_base[0] + 1 + 3 * (kg - 1))
                            ps = pS3[kg % 2]
                            w = GRP[kg] * 512
                            ins = s.activation(pt[kg % 2][:, 0:w], ps[:, 0:w],
                                               AF.Exp, scale=SCALE)
                            got = cA.tick(ins)
                            assert got == cA_base[0] + kg + 1
                    Ab(_ae)

                    def _v(v, hp=hp, h=h, qb=qb, hg=hg, cA_base=cA_base,
                           cV_base=cV_base, sc_done=sc_done, pv_done=pv_done,
                           mm_marks=mm_marks):
                        vb = cV_base[0]
                        tmp = esc[0][:, 0, :]
                        ins = v.memset(sacc[:, :], 0.0)
                        got = cV.tick(ins)
                        assert got == vb + 1
                        for kg in range(NKG3):
                            cA.wait(v, cA_base[0] + kg + 1)
                            pb = pt[kg % 2]
                            if GRP[kg] == 3:
                                v.wait_ge(cV.h, cV.n)
                                ins = v.tensor_tensor(tmp, pb[:, 0:512],
                                                      pb[:, 512:1024], ALU.add)
                                v.wait_ge(cV.h, cV.tick(ins))
                                ins = v.tensor_tensor(tmp, tmp,
                                                      pb[:, 1024:1536], ALU.add)
                                v.wait_ge(cV.h, cV.tick(ins))
                                ins = v.tensor_tensor(sacc[:, :], sacc[:, :],
                                                      tmp, ALU.add)
                                cV.tick(ins)
                            else:
                                v.wait_ge(cV.h, cV.n)
                                ins = v.tensor_tensor(sacc[:, :], sacc[:, :],
                                                      pb[:, 0:512], ALU.add)
                                cV.tick(ins)
                        assert cV.n == vb + 65
                        cT.wait(v, mm_marks["srow"])
                        cT.wait(v, mm_marks["lrow"])
                        v.reciprocal(rs[0][0:1, 0:512], pS3[0][0:1, 0:512])
                        ins = v.reciprocal(rs[1][0:1, 0:512],
                                           pS3[1][0:1, 0:512])
                        got = cV.tick(ins)
                        assert got == vb + 66
                        cT.wait(v, mm_marks["bc"])
                        v.tensor_copy(rb[:, 0, :], pS3[0][:, 512:1024])
                        ins = v.tensor_copy(rb[:, 1, :], pS3[1][:, 512:1024])
                        got = cV.tick(ins)
                        fr["rbev"] = got
                        assert got == vb + 67
                        cT.wait(v, pv_done[NKG3 - 1])
                        cT.wait(v, mm_marks["mo"])
                        cCB[qb % 2].wait(v, fr["cb4"][qb % 2])
                        v.wait_ge(cV.h, vb + 67)
                        v.scalar_tensor_tensor(rs[0][:, :], pP[0][:, 0:512],
                                               gcols[:, 16 + hg:17 + hg],
                                               rb[:, 0, :], ALU.mult, ALU.mult)
                        ins = v.scalar_tensor_tensor(rs[1][:, :],
                                                     pP[1][:, 0:512],
                                                     gcols[:, hg:hg + 1],
                                                     rb[:, 1, :], ALU.mult,
                                                     ALU.mult)
                        got = cV.tick(ins)
                        assert got == vb + 68
                        v.wait_ge(cV.h, vb + 68)
                        ins = v.tensor_tensor(cbt[:, qb % 2, :], rs[0][:, :],
                                              rs[1][:, :], ALU.add)
                        got = cV.tick(ins)
                        fr["pvfree"] = got
                        fr["epi_done"] = got
                        assert got == vb + 69
                    Vb(_v)

                    def _y(y, hp=hp, h=h, qb=qb, j0=j0):
                        cV.wait(y)
                        ins = y.dma_start(
                            out=combT[j0:j0 + 128, ts(qb, 512)],
                            in_=cbt[:, qb % 2, :])
                        fr["cb4"][qb % 2] = cCB[qb % 2].tick(ins, 16)
                    Yb(_y)

        kv_state = {}
        prefetched_pairs = set()

        def emit_xt_prefetch(hp):
            prefetched_pairs.add(hp)

            def _yp(y, hp=hp):
                for sb in (0, 1):
                    cT.wait(y, fr[f"xt{sb}"])
                    for bpar, bval in xblk_w[("k", sb)]:
                        cBXS[bpar].wait(y, bval)
                    ins = y.dma_start(out=xt[sb][:, :, :], in_=xblk[sb])
                    cXT[sb].tick(ins, 16)
                    cV.wait(y, fr[f"cs{sb}"])
                    ins = y.dma_start(out=csb[sb][:, :, :],
                                      in_=csk[ts(hp, 128), :, ts(sb, 512)])
                    cCS[sb].tick(ins, 16)
            Yb(_yp)

        if stages != "p2":
            emit_gw(0)
        for hp in range(npairs if stages != "p2" else 0):
            emit_kv_pair(hp)
            if stages == "kv":
                emit_kv_tail(hp)
                break
            emit_q_pair(hp)
            emit_kv_tail(hp)
            wsl_used[hp] = cT.n
            if hp + 1 < npairs:
                emit_gw(hp + 1, "q" if hp == 0 else "both")
            if stages == "kvq":
                break
            if hp + 1 < npairs:
                emit_xt_prefetch(hp + 1)
            emit_attn_pair(hp)

        if stages in ("kv", "kvq"):
            def _fin(g):
                cV.wait(g)
                cA.wait(g)
                cKFB[0].wait(g)
                cKFB[1].wait(g)
                cDBG.wait(g)
                ins = g.dma_start(out=out, in_=xch)
                cOUT.tick(ins, 16)
                cOUT.wait(g)
            Gb(_fin)
            p1.close()
            return nc, dout

        if dbg & 4:
            def _dbgc(g):
                cCB[0].wait(g)
                cCB[1].wait(g)
                for hp in range(npairs):
                    for h in range(2):
                        j0 = h * 1024 + hp * 128
                        ins = g.dma_start(out=dout["d_comb"][j0:j0 + 128, :],
                                          in_=combT[j0:j0 + 128, :])
                        cDBG.tick(ins, 16)
            Gb(_dbgc)

        if stages == "attn":
            def _fin(g):
                cCB[0].wait(g)
                cCB[1].wait(g)
                cDBG.wait(g)
                ins = g.dma_start(out=out, in_=xch)
                cOUT.tick(ins, 16)
                cOUT.wait(g)
            Gb(_fin)
            p1.close()
            return nc, dout

        p1.close()

        def emit_ln(nt, src_psum, xcol_ap, zs, s48, lncol, sqscr, x2f_ap,
                    x2b_ap, done_dict, war_ctr=None, war_val=0):
            vtick = [0]

            def _v(v):
                ins = None
                for mb in range(4):
                    ins = v.scalar_tensor_tensor(
                        zs[:, ts(mb, 512)], src_psum[mb][:, :], 1.0,
                        xcol_ap[:, ts(mb, 512)], ALU.mult, ALU.add,
                        accum_out=s48[:, mb:mb + 1])
                v.wait_ge(cV.h, cV.tick(ins))
                ins = v.tensor_reduce(
                    lncol[:, 0:1].rearrange("p (a o) -> p a o", o=1),
                    s48[:, 0:4].rearrange("p (a f) -> p a f", a=1),
                    AX.X, ALU.add)
                v.wait_ge(cV.h, cV.tick(ins))
                ins = v.tensor_scalar_mul(lncol[:, 1:2], lncol[:, 0:1], 1.0 / H)
                vtick[0] = cV.tick(ins)
            Vb(_v)

            def _aq(s):
                cV.wait(s, vtick[0])
                for mb in range(4):
                    ins = s.activation(sqscr[:, ts(mb, 512)],
                                       zs[:, ts(mb, 512)], AF.Square,
                                       accum_out=s48[:, 4 + mb:5 + mb])
                cA.tick(ins)
            Ab(_aq)

            def _v2(v):
                cA.wait(v)
                ins = v.tensor_reduce(
                    lncol[:, 2:3].rearrange("p (a o) -> p a o", o=1),
                    s48[:, 4:8].rearrange("p (a f) -> p a f", a=1),
                    AX.X, ALU.add)
                v.wait_ge(cV.h, cV.tick(ins))
                ins = v.tensor_scalar_mul(lncol[:, 2:3], lncol[:, 2:3], 1.0 / H)
                w = cV.tick(ins)
                ins = v.tensor_tensor(lncol[:, 3:4], lncol[:, 1:2],
                                      lncol[:, 1:2], ALU.mult)
                v.wait_ge(cV.h, cV.tick(ins))
                ins = v.tensor_tensor(lncol[:, 3:4], lncol[:, 2:3],
                                      lncol[:, 3:4], ALU.subtract)
                v.wait_ge(cV.h, cV.tick(ins))
                ins = v.tensor_scalar_add(lncol[:, 3:4], lncol[:, 3:4], EPS)
                cV.tick(ins)
            Vb(_v2)

            def _a2(s):
                cV.wait(s)
                ins = s.activation(lncol[:, 4:5], lncol[:, 3:4], AF.Sqrt)
                cA.tick(ins)
            Ab(_a2)

            def _v3(v):
                cA.wait(v)
                if war_ctr is not None:
                    war_ctr.wait(v, war_val)
                ins = v.reciprocal(lncol[:, 5:6], lncol[:, 4:5])
                v.wait_ge(cV.h, cV.tick(ins))
                ins = None
                for mb in range(4):
                    ins = v.tensor_scalar(x2f_ap[:, ts(mb, 512)],
                                          zs[:, ts(mb, 512)],
                                          lncol[:, 1:2], lncol[:, 5:6],
                                          ALU.subtract, ALU.mult)
                if x2b_ap is not None:
                    v.wait_ge(cV.h, cV.tick(ins))
                    ins = v.tensor_copy(x2b_ap[:, :], x2f_ap[:, :])
                done_dict[nt] = cV.tick(ins)
            Vb(_v3)

        w_es = ExitStack()
        wo_sb = w_es.enter_context(nc.sbuf_tensor("wo_sb", [128, MC, H], BF16))
        ct = w_es.enter_context(nc.sbuf_tensor("ct", [128, 2, MC, 512], BF16))
        xc = w_es.enter_context(nc.sbuf_tensor("xc", [128, 2, H], F32))
        zs = w_es.enter_context(nc.sbuf_tensor("zs", [128, H], F32))
        sqs = w_es.enter_context(nc.sbuf_tensor("sqs", [128, H], BF16))
        x2f = w_es.enter_context(nc.sbuf_tensor("x2f", [128, H], F32))
        x2b = w_es.enter_context(nc.sbuf_tensor("x2b", [128, H], BF16))
        lncol = w_es.enter_context(nc.sbuf_tensor("lncol", [128, 6], F32))
        s48 = w_es.enter_context(nc.sbuf_tensor("s48", [128, 8], F32))

        def _gwo(g):
            cT.wait(g)
            cV.wait(g)
            cA.wait(g)
            cCB[0].wait(g)
            cCB[1].wait(g)
            for mc in range(MC):
                ins = g.dma_start(out=wo_sb[:, mc, :], in_=woT[ts(mc, 128), :])
                cWO.tick(ins, 16)
        Gb(_gwo)

        ct_ready = {}
        wo_done = {}
        ln1_done = {}
        xc_ready = {}
        x2w_mark = {}

        for nb in range(NQB):
            def _yc(y, nb=nb):
                if nb == 0:
                    cCB[0].wait(y)
                    cCB[1].wait(y)
                    cT.wait(y)
                    cV.wait(y)
                    cA.wait(y)
                if nb >= 2:
                    cT.wait(y, wo_done.get((nb - 2, 3, 3), 0))
                ins = None
                for jc in range(MC):
                    ins = y.dma_start(
                        out=ct[:, nb % 2, jc, :],
                        in_=combT[ts(jc, 128), ts(nb, 512)])
                    cCT[nb % 2].tick(ins, 16)
                ct_ready[nb] = cCT[nb % 2].n
            Yb(_yc)

            for nt4 in range(4):
                nt = nb * 4 + nt4

                def _yx(y, nb=nb, nt=nt):
                    cV.wait(y, ln1_done.get(nt - 2, 0))
                    ins = y.dma_start(out=xc[:, nt % 2, :],
                                      in_=xch[ts(nt, 128), :])
                    xc_ready[nt] = cXC[nt % 2].tick(ins, 16)
                Yb(_yx)

                wb = [pS3[nt % 2][:, 0:512], pS3[nt % 2][:, 512:1024],
                      pS3[nt % 2][:, 1024:1536], pP[nt % 2][:, 0:512]]

                def _tw(t, nb=nb, nt=nt, nt4=nt4, wb=wb):
                    cCT[nb % 2].wait(t, ct_ready[nb])
                    cWO.wait(t)
                    if nt <= 1:
                        cV.wait(t)
                    cV.wait(t, ln1_done.get(nt - 2, 0))
                    for mb in range(4):
                        ins = None
                        for jc in range(MC):
                            ins = t.matmul(wb[mb][:, :],
                                           ct[:, nb % 2, jc, ts(nt4, 128)],
                                           wo_sb[:, jc, ts(mb, 512)],
                                           start=(jc == 0), stop=(jc == MC - 1))
                        wo_done[(nb, nt4, mb)] = cT.tick(ins)
                Tb(_tw)

                def _vpre(v, nb=nb, nt=nt, nt4=nt4):
                    cT.wait(v, wo_done[(nb, nt4, 3)])
                    cXC[nt % 2].wait(v, xc_ready[nt])
                Vb(_vpre)

                emit_ln(nt, wb, xc[:, nt % 2, :], zs, s48, lncol, sqs,
                        x2f, x2b, ln1_done, war_ctr=cX2W,
                        war_val=x2w_mark.get(nt - 1, 0))

                def _yw(g, nt=nt):
                    cV.wait(g, ln1_done[nt])
                    ins = g.dma_start(out=x2f32[ts(nt, 128), :], in_=x2f[:, :])
                    cX2W.tick(ins, 16)
                    ins = g.dma_start(out=x2b16[ts(nt, 128), :], in_=x2b[:, :])
                    x2w_mark[nt] = cX2W.tick(ins, 16)
                Gb(_yw)

        x2_written = cX2W.n
        w_es.close()

        if dbg & 8:
            def _dbgx(g):
                cX2W.wait(g, x2_written)
                ins = g.dma_start(out=dout["d_x2"], in_=x2f32)
                cDBG.tick(ins, 16)
            Gb(_dbgx)

        if stages == "ln1":
            def _fin(g):
                cX2W.wait(g, x2_written)
                cDBG.wait(g)
                ins = g.dma_start(out=out, in_=x2f32)
                cOUT.tick(ins, 16)
                cOUT.wait(g)
            Gb(_fin)
            return nc, dout

        scl = C(nc.sbuf_tensor("scl", [128, MC, 8], F32))
        x2T = C(nc.sbuf_tensor("x2T", [128, MC, TOK], BF16))

        g_es = ExitStack()
        gw_sb = g_es.enter_context(nc.sbuf_tensor("gw_sb", [128, MC, E], BF16))
        lgt = g_es.enter_context(nc.sbuf_tensor("lgt", [128, E], F32))
        gx = g_es.enter_context(nc.sbuf_tensor("gx", [128, E], F32))
        gsc = g_es.enter_context(nc.sbuf_tensor("gsc", [128, E], F32))
        m8 = g_es.enter_context(nc.sbuf_tensor("m8", [128, 8], F32))
        gcol2 = g_es.enter_context(nc.sbuf_tensor("gcol2", [128, 4], F32))
        wet = g_es.enter_context(nc.sbuf_tensor("wet", [64, 128], F32))
        ivb = g_es.enter_context(nc.sbuf_tensor("ivb", [64, 64], F32))
        poh = g_es.enter_context(nc.sbuf_tensor("poh", [64, 64], F32))
        ivr = g_es.enter_context(nc.sbuf_tensor("ivr", [1, 64], F32))
        ivri = g_es.enter_context(nc.sbuf_tensor("ivri", [1, 64], I32))
        fl = g_es.enter_context(nc.sbuf_tensor("fl", [128, E], F32))
        exs = g_es.enter_context(nc.sbuf_tensor("exs", [128, E], F32))
        sd8 = g_es.enter_context(nc.sbuf_tensor("sd8", [128, 8], F32))
        sfl = g_es.enter_context(nc.sbuf_tensor("sfl", [128, E], F32))
        sft_nt = g_es.enter_context(nc.sbuf_tensor("sft_nt", [8, G, 128], BF16))

        def _gg(g):
            cT.wait(g)
            cV.wait(g)
            cX2W.wait(g, x2_written)
            for mc in range(MC):
                ins = g.dma_start(out=gw_sb[:, mc, :], in_=gwT[ts(mc, 128), :])
                cGW.tick(ins, 16)
            ins = g.dma_start(out=ivri[:, :],
                              in_=invm.rearrange("(o a) b -> o (a b)", o=1))
            cGW.tick(ins, 16)
        Gb(_gg)

        def _yx2(y):
            cX2W.wait(y, x2_written)
            cT.wait(y)
            cV.wait(y)
            ins = None
            for nb in range(NQB):
                for mc in range(MC):
                    ins = y.dma_start_transpose(
                        x2T[:, mc, ts(nb, 512)],
                        x2b16[ts(nb, 512), ts(mc, 128)])
                    cX2T.tick(ins, 16)
        Yb(_yx2)
        x2T_ready = cX2T.n

        def _vp(v):
            cGW.wait(v)
            ins = v.tensor_copy(ivr[:, :], ivri[:, :])
            cV.tick(ins)
        Vb(_vp)

        def _tp(t):
            cV.wait(t)
            ins = t.matmul(pS3[0][0:64, 0:64], ones1f[0:1, 0:64], ivr[0:1, :],
                           start=True, stop=True)
            cT.tick(ins)
        Tb(_tp)

        def _vp2(v):
            cT.wait(v)
            ins = v.tensor_copy(ivb[0:64, :], pS3[0][0:64, 0:64])
            v.wait_ge(cV.h, cV.tick(ins))
            ins = v.tensor_scalar(poh[0:64, :], ivb[0:64, :], iotc[0:64, 0:1],
                                  None, ALU.is_equal)
            cV.tick(ins)
        Vb(_vp2)

        sft_w = {}
        for nt in range(MC):
            nb, nt4 = nt // 4, nt % 4
            lgmm = [0]
            flmm = [0]
            wemm = [0]
            sftr = {}

            def _tg(t, nt=nt, nb=nb, nt4=nt4, lgmm=lgmm):
                cX2T.wait(t, x2T_ready)
                cGW.wait(t)
                cV.wait(t)
                ins = None
                for mc in range(MC):
                    ins = t.matmul(pP[0][:, 0:64],
                                   x2T[:, mc, ts(nt, 128)],
                                   gw_sb[:, mc, :],
                                   start=(mc == 0), stop=(mc == MC - 1))
                lgmm[0] = cT.tick(ins)
            Tb(_tg)

            def _vg(v, nt=nt, lgmm=lgmm):
                cT.wait(v, lgmm[0])
                ins = v.tensor_copy(lgt[:, :], pP[0][:, 0:64])
                v.wait_ge(cV.h, cV.tick(ins))
                ins = v.reduce_max(
                    gcol2[:, 0:1].rearrange("p (a o) -> p a o", o=1),
                    lgt[:, :].rearrange("p (a f) -> p a f", a=1), AX.X)
                v.wait_ge(cV.h, cV.tick(ins))
                ins = v.tensor_scalar_mul(gcol2[:, 1:2], gcol2[:, 0:1], -1.0)
                cV.tick(ins)
            Vb(_vg)

            def _ag(s, nt=nt):
                cV.wait(s)
                ins = s.activation(gx[:, :], lgt[:, :], AF.Exp,
                                   bias=gcol2[:, 1:2],
                                   accum_out=gcol2[:, 2:3])
                cA.tick(ins)
            Ab(_ag)

            def _vg2(v, nt=nt):
                cA.wait(v)
                ins = v.reciprocal(gcol2[:, 3:4], gcol2[:, 2:3])
                v.wait_ge(cV.h, cV.tick(ins))
                ins = v.tensor_scalar(gsc[:, :], gx[:, :], gcol2[:, 3:4], None,
                                      ALU.mult)
                v.wait_ge(cV.h, cV.tick(ins))
                ins = v.max(m8[:, :], gsc[:, :])
                v.wait_ge(cV.h, cV.tick(ins))
                ins = v.tensor_scalar(lgt[:, :], gsc[:, :], m8[:, 7:8], None,
                                      ALU.is_ge)
                v.wait_ge(cV.h, cV.tick(ins))
                ins = v.tensor_tensor(gx[:, :], gsc[:, :], lgt[:, :], ALU.mult)
                cV.tick(ins)
            Vb(_vg2)

            def _tg2(t, nt=nt, wemm=wemm):
                cV.wait(t)
                ins = t.transpose(pS3[0][0:64, 0:128], gx[:, :], identf[:, :])
                wemm[0] = cT.tick(ins)
            Tb(_tg2)

            def _vg3(v, nt=nt, wemm=wemm):
                cT.wait(v, wemm[0])
                ins = v.tensor_copy(wet[0:64, :], pS3[0][0:64, 0:128])
                cV.tick(ins)
            Vb(_vg3)

            def _tg3(t, nt=nt, flmm=flmm):
                cV.wait(t)
                ins = t.matmul(pP[1][:, 0:64], wet[0:64, :], poh[0:64, :],
                               start=True, stop=True)
                flmm[0] = cT.tick(ins)
            Tb(_tg3)

            def _vg4(v, nt=nt, flmm=flmm):
                cT.wait(v, flmm[0])
                ins = v.tensor_copy(fl[:, :], pP[1][:, 0:64])
                v.wait_ge(cV.h, cV.tick(ins))
                v.tensor_reduce(
                    scl[:, nt, :].rearrange("p (g o) -> p g o", o=1),
                    fl[:, :].rearrange("p (g s) -> p g s", g=G),
                    AX.X, ALU.add)
                ins = v.tensor_scalar(lgt[:, :], fl[:, :], 0.0, None,
                                      ALU.is_equal)
                v.wait_ge(cV.h, cV.tick(ins))
                ins = v.scalar_tensor_tensor(lgt[:, :], lgt[:, :], NEG,
                                             fl[:, :], ALU.mult, ALU.add)
                cV.tick(ins)
            Vb(_vg4)

            def _ag2(s, nt=nt):
                cV.wait(s)
                ins = s.activation(exs[:, :], lgt[:, :], AF.Exp)
                cA.tick(ins)
            Ab(_ag2)

            def _vg5(v, nt=nt):
                cA.wait(v)
                ins = v.tensor_reduce(
                    sd8[:, :].rearrange("p (g o) -> p g o", o=1),
                    exs[:, :].rearrange("p (g s) -> p g s", g=G),
                    AX.X, ALU.add)
                v.wait_ge(cV.h, cV.tick(ins))
                ins = v.tensor_scalar_max(sd8[:, :], sd8[:, :], 1e-30)
                v.wait_ge(cV.h, cV.tick(ins))
                ins = v.reciprocal(sd8[:, :], sd8[:, :])
                v.wait_ge(cV.h, cV.tick(ins))
                ins = None
                for g in range(G):
                    ins = v.tensor_scalar(sfl[:, ts(g, GS)], exs[:, ts(g, GS)],
                                          sd8[:, g:g + 1], None, ALU.mult)
                cV.tick(ins)
            Vb(_vg5)

            vg6_base = cV.n
            sftw_prev = sft_w.get(nt - 1, 0)

            def _tg4(t, nt=nt, sftr=sftr, vg6_base=vg6_base):
                cV.wait(t, vg6_base)
                for g in range(G):
                    reg = (pS3[1][0:8, 0:128] if g % 2 == 0
                           else pS3[1][0:8, 512:640])
                    if g >= 2:
                        cV.wait(t, vg6_base + g - 1)
                    ins = t.transpose(reg, sfl[:, ts(g, GS)], identf[:, :])
                    sftr[g] = cT.tick(ins)
            Tb(_tg4)

            def _vg6(v, nt=nt, sftr=sftr, vg6_base=vg6_base,
                     sftw_prev=sftw_prev):
                cSFT.wait(v, sftw_prev)
                for g in range(G):
                    reg = (pS3[1][0:8, 0:128] if g % 2 == 0
                           else pS3[1][0:8, 512:640])
                    cT.wait(v, sftr[g])
                    ins = v.tensor_copy(sft_nt[0:8, g, :], reg)
                    got = cV.tick(ins)
                    assert got == vg6_base + g + 1
            Vb(_vg6)

            def _yg6(y, nt=nt):
                cV.wait(y)
                ins = y.dma_start(
                    out=sftd[:, :, ts(nt, 128)].rearrange("g s c -> s g c"),
                    in_=sft_nt[0:8, :, :])
                sft_w[nt] = cSFT.tick(ins, 16)
            Yb(_yg6)

        sft_written = cSFT.n
        g_es.close()

        if dbg & 16:
            def _dbgg(g):
                cV.wait(g)
                cSFT.wait(g, sft_written)
                cDBG.tick(g.dma_start(out=dout["d_scl"], in_=scl[:, :, :]), 16)
                cDBG.tick(g.dma_start(out=dout["d_sft"], in_=sftd), 16)
            Gb(_dbgg)

        f_es = ExitStack()
        aT = f_es.enter_context(nc.sbuf_tensor("aT", [128, IC, TOK], BF16))
        dn_sb = f_es.enter_context(nc.sbuf_tensor("dn_sb", [128, IC, H], BF16))
        guw = [f_es.enter_context(
            nc.sbuf_tensor(f"guw{i}", [128, MC, 2, 128], BF16))
            for i in range(2)]
        mun = f_es.enter_context(nc.sbuf_tensor("mun", [128, MC, GS], BF16))
        sft_g = f_es.enter_context(nc.sbuf_tensor("sft_g", [8, TOK], BF16))
        wmg = f_es.enter_context(nc.sbuf_tensor("wmg", [8, 128], BF16))
        wmu = f_es.enter_context(nc.sbuf_tensor("wmu", [8, 128], BF16))
        sil = f_es.enter_context(nc.sbuf_tensor("sil", [128, 512], BF16))
        dsc = f_es.enter_context(nc.sbuf_tensor("dsc", [128, 2, H], F32))

        guw_ready = {}
        guw_used = {}
        dn_ready = {}
        dn_used = {}
        mun_ready = {}
        yacc = {}
        yfin = {}

        for g in range(G):
            tgu_base = [0]

            def _gf(gg, g=g, tgu_base=tgu_base):
                tgu_base[0] = cT.n
                cSFT.wait(gg, sft_written)
                if g == 0:
                    cT.wait(gg)
                    cV.wait(gg)
                if g >= 1:
                    cT.wait(gg, guw_used.get((g - 1, "mun"), 0))
                    cT.wait(gg, guw_used[(g - 1, IC - 1)])
                ins = gg.dma_start(out=sft_g[0:8, :], in_=sftd[g])
                cMU2.tick(ins, 16)
                ins = gg.dma_start(
                    out=mun[:, :, :],
                    in_=muo[g].rearrange("(c p) s -> p c s", p=128))
                mun_ready[g] = cMU2.tick(ins, 16)
                for ic in range(IC):
                    slot = ic % 2
                    if ic >= 2:
                        cT.wait(gg, tgu_base[0] + 9 * (ic - 1))
                    elif g >= 1:
                        cT.wait(gg, guw_used[(g - 1, IC - 2 + ic)])
                    for half, w in ((0, gpT), (1, upT)):
                        for q in range(4):
                            ins = gg.dma_start(
                                out=guw[slot][:, ts(q, 4), half, :],
                                in_=w[g][:, ts(ic, 128)].rearrange(
                                    "(c p) i -> p c i", p=128)[:, ts(q, 4), :])
                            cGU[slot].tick(ins, 16)
                    guw_ready[(g, ic)] = cGU[slot].n
                if g >= 1:
                    cT.wait(gg, dn_used[g - 1])
                for q in range(4):
                    sl = bass.ds(q * 3, 3) if q < 3 else bass.ds(9, 2)
                    ins = gg.dma_start(
                        out=dn_sb[:, sl, :],
                        in_=dnT[g].rearrange("(c p) h -> p c h",
                                             p=128)[:, sl, :])
                    cDN.tick(ins, 16)
                dn_ready[g] = cDN.n
            Gb(_gf)

            gu_mm = {}
            wm_mm = {}
            agu_base = [0]
            vgu_base = [0]

            def _tgu(t, g=g, gu_mm=gu_mm, wm_mm=wm_mm, agu_base=agu_base,
                     vgu_base=vgu_base):
                agu_base[0] = cA.n
                vgu_base[0] = cV.n
                cMU2.wait(t, mun_ready[g])
                for ic in range(IC):
                    cGU[ic % 2].wait(t, guw_ready[(g, ic)])
                    if ic >= 1:
                        cV.wait(t, vgu_base[0] + (ic - 1) * 5 + 1)
                    ins = None
                    for half in range(2):
                        reg = (pS3[0][0:8, 0:128] if half == 0
                               else pS3[0][0:8, 512:640])
                        for mc in range(MC):
                            ins = t.matmul(reg, mun[:, mc, :],
                                           guw[ic % 2][:, mc, half, :],
                                           start=(mc == 0), stop=(mc == MC - 1))
                    wm_mm[ic] = cT.tick(ins)
                    for nb in range(NQB):
                        cA.wait(t, agu_base[0] + ic * 4 + nb)
                        cV.wait(t, vgu_base[0] + ic * 5 + (1 if nb == 0
                                                           else nb + 1))
                        ins = None
                        for mc in range(MC):
                            ins = t.matmul(pP[0][:, :],
                                           guw[ic % 2][:, mc, 0, :],
                                           x2T[:, mc, ts(nb, 512)],
                                           start=(mc == 0), stop=False)
                        ins = t.matmul(pP[0][:, :], wmg[0:8, :],
                                       sft_g[0:8, ts(nb, 512)],
                                       start=False, stop=True)
                        gmm = cT.tick(ins)
                        ins = None
                        for mc in range(MC):
                            ins = t.matmul(pP[1][:, :],
                                           guw[ic % 2][:, mc, 1, :],
                                           x2T[:, mc, ts(nb, 512)],
                                           start=(mc == 0), stop=False)
                        ins = t.matmul(pP[1][:, :], wmu[0:8, :],
                                       sft_g[0:8, ts(nb, 512)],
                                       start=False, stop=True)
                        umm = cT.tick(ins)
                        gu_mm[(ic, nb)] = (gmm, umm)
                    guw_used[(g, ic)] = cT.n
                    assert cT.n == tgu_base[0] + 9 * (ic + 1)
                guw_used[(g, "mun")] = cT.n
            Tb(_tgu)

            def _agu(s, g=g, gu_mm=gu_mm, agu_base=agu_base,
                     vgu_base=vgu_base):
                for ic in range(IC):
                    for nb in range(NQB):
                        gmm, umm = gu_mm[(ic, nb)]
                        cT.wait(s, gmm)
                        cV.wait(s, vgu_base[0] + (ic * 5 if nb == 0
                                                  else ic * 5 + nb + 1))
                        ins = s.activation(sil[:, :], pP[0][:, :], AF.Silu)
                        got = cA.tick(ins)
                        assert got == agu_base[0] + ic * 4 + nb + 1
            Ab(_agu)

            def _vgu(v, g=g, gu_mm=gu_mm, wm_mm=wm_mm, agu_base=agu_base,
                     vgu_base=vgu_base):
                if g >= 1:
                    cT.wait(v, dn_used[g - 1])
                for ic in range(IC):
                    cT.wait(v, wm_mm[ic])
                    if ic >= 1:
                        cT.wait(v, gu_mm[(ic - 1, NQB - 1)][1])
                    v.tensor_copy(wmg[0:8, :], pS3[0][0:8, 0:128])
                    ins = v.tensor_copy(wmu[0:8, :], pS3[0][0:8, 512:640])
                    got = cV.tick(ins)
                    assert got == vgu_base[0] + ic * 5 + 1, (got, vgu_base, ic)
                    for nb in range(NQB):
                        gmm, umm = gu_mm[(ic, nb)]
                        cT.wait(v, umm)
                        cA.wait(v, agu_base[0] + ic * 4 + nb + 1)
                        ins = v.tensor_tensor(aT[:, ic, ts(nb, 512)],
                                              sil[:, :], pP[1][:, :], ALU.mult)
                        got = cV.tick(ins)
                        assert got == vgu_base[0] + ic * 5 + nb + 2
            Vb(_vgu)


            dmm = {}
            vdn_base = [0]

            def _tdn(t, g=g, dmm=dmm, vdn_base=vdn_base):
                vdn_base[0] = cV.n
                cDN.wait(t, dn_ready[g])
                cV.wait(t)
                for nt in range(MC):
                    for hb in range(4):
                        cV.wait(t, vdn_base[0] + max(0, (nt - 1) * 4 + hb + 1))
                        dreg = (pS3[1][:, ts(hb, 512)] if hb < 3
                                else pS3[0][:, 1024:1536])
                        ins = None
                        for ic in range(IC):
                            ins = t.matmul(dreg,
                                           aT[:, ic, ts(nt, 128)],
                                           dn_sb[:, ic, ts(hb, 512)],
                                           start=(ic == 0), stop=(ic == IC - 1))
                        dmm[(nt, hb)] = cT.tick(ins)
                dn_used[g] = cT.n
            Tb(_tdn)

            def _vdn(v, g=g, dmm=dmm, vdn_base=vdn_base):
                ybase = (cY[0].n, cY[1].n)
                for nt in range(MC):
                    par = nt % 2
                    if nt >= 2:
                        cY[par].wait(v, ybase[par] + 16 * (nt // 2))
                    else:
                        cY[par].wait(v, ybase[par])
                    for hb in range(4):
                        cT.wait(v, dmm[(nt, hb)])
                        dreg = (pS3[1][:, ts(hb, 512)] if hb < 3
                                else pS3[0][:, 1024:1536])
                        ins = v.tensor_scalar(dsc[:, nt % 2, ts(hb, 512)],
                                              dreg,
                                              scl[:, nt, g:g + 1], None,
                                              ALU.mult)
                        got = cV.tick(ins)
                        assert got == vdn_base[0] + nt * 4 + hb + 1
            Vb(_vdn)

            def _gdn(gg, g=g, vdn_base=vdn_base):
                ybase2 = (cY[0].n, cY[1].n)
                for nt in range(MC):
                    cV.wait(gg, vdn_base[0] + nt * 4 + 4)
                    if g >= 1:
                        cY[nt % 2].wait(gg, yfin[g - 1][nt % 2])
                    ins = gg.dma_start(out=y32[ts(nt, 128), :],
                                       in_=dsc[:, nt % 2, :],
                                       accum_op=(ALU.bypass if g == 0
                                                 else ALU.add))
                    yacc[(g, nt)] = cY[nt % 2].tick(ins, 16)
                    assert yacc[(g, nt)] == ybase2[nt % 2] + 16 * (nt // 2 + 1)
                yfin[g] = (cY[0].n, cY[1].n)
            Gb(_gdn)

        f_es.close()

        if dbg & 32:
            def _dbgy(g):
                cY[0].wait(g)
                cY[1].wait(g)
                ins = g.dma_start(out=dout["d_y"], in_=y32)
                cDBG.tick(ins, 16)
            Gb(_dbgy)

        with (nc.sbuf_tensor("yl", [128, 2, H], F32) as yl,
              nc.sbuf_tensor("x2l", [128, 2, H], F32) as x2l,
              nc.sbuf_tensor("z2", [128, 2, H], F32) as z2,
              nc.sbuf_tensor("sq2", [128, 2, H], BF16) as sq2,
              nc.sbuf_tensor("ocol", [128, 2, 6], F32) as ocol,
              nc.sbuf_tensor("o48", [128, 2, 8], F32) as o48,
              nc.sbuf_tensor("ob", [128, 2, H], F32) as ob):
            yload = {}
            xload = {}
            oln = {}
            out_mark = {}
            amark = {}
            a2mark = {}

            for nt in range(MC):
                pb = nt % 2

                def _yl(y, nt=nt, pb=pb):
                    if nt == 0:
                        cT.wait(y)
                        cV.wait(y)
                    cY[0].wait(y)
                    cY[1].wait(y)
                    cV.wait(y, oln.get(nt - 2, 0))
                    ins = y.dma_start(out=yl[:, pb, :],
                                      in_=y32[ts(nt, 128), :])
                    yload[nt] = cYL[pb].tick(ins, 16)
                Yb(_yl)

                def _axl(a, nt=nt, pb=pb):
                    if nt == 0:
                        cV.wait(a)
                    cV.wait(a, oln.get(nt - 2, 0))
                    ins = a.dma_start(out=x2l[:, pb, :],
                                      in_=x2f32[ts(nt, 128), :])
                    xload[nt] = cXL[pb].tick(ins, 16)
                Ab(_axl)

                vt = [0]

                def _vl(v, nt=nt, pb=pb, vt=vt):
                    cYL[pb].wait(v, yload[nt])
                    cXL[pb].wait(v, xload[nt])
                    cA.wait(v, a2mark.get(nt - 2, 0))
                    ins = None
                    for mb in range(4):
                        ins = v.scalar_tensor_tensor(
                            z2[:, pb, ts(mb, 512)],
                            yl[:, pb, ts(mb, 512)], 1.0,
                            x2l[:, pb, ts(mb, 512)], ALU.mult, ALU.add,
                            accum_out=o48[:, pb, mb:mb + 1])
                    v.wait_ge(cV.h, cV.tick(ins))
                    ins = v.tensor_reduce(
                        ocol[:, pb, 0:1].rearrange("p (a o) -> p a o", o=1),
                        o48[:, pb, 0:4].rearrange("p (a f) -> p a f", a=1),
                        AX.X, ALU.add)
                    v.wait_ge(cV.h, cV.tick(ins))
                    ins = v.tensor_scalar_mul(ocol[:, pb, 1:2],
                                              ocol[:, pb, 0:1], 1.0 / H)
                    vt[0] = cV.tick(ins)
                Vb(_vl)

                def _al(s, nt=nt, pb=pb, vt=vt):
                    cV.wait(s, vt[0])
                    ins = None
                    for mb in range(4):
                        ins = s.activation(sq2[:, pb, ts(mb, 512)],
                                           z2[:, pb, ts(mb, 512)], AF.Square,
                                           accum_out=o48[:, pb, 4 + mb:5 + mb])
                    amark[nt] = cA.tick(ins)
                Ab(_al)

                def _vl2(v, nt=nt, pb=pb):
                    cA.wait(v, amark[nt])
                    ins = v.tensor_reduce(
                        ocol[:, pb, 2:3].rearrange("p (a o) -> p a o", o=1),
                        o48[:, pb, 4:8].rearrange("p (a f) -> p a f", a=1),
                        AX.X, ALU.add)
                    v.wait_ge(cV.h, cV.tick(ins))
                    ins = v.tensor_scalar_mul(ocol[:, pb, 2:3],
                                              ocol[:, pb, 2:3], 1.0 / H)
                    cV.tick(ins)
                    ins = v.tensor_tensor(ocol[:, pb, 3:4], ocol[:, pb, 1:2],
                                          ocol[:, pb, 1:2], ALU.mult)
                    v.wait_ge(cV.h, cV.tick(ins))
                    ins = v.tensor_tensor(ocol[:, pb, 3:4], ocol[:, pb, 2:3],
                                          ocol[:, pb, 3:4], ALU.subtract)
                    v.wait_ge(cV.h, cV.tick(ins))
                    ins = v.tensor_scalar_add(ocol[:, pb, 3:4],
                                              ocol[:, pb, 3:4], EPS)
                    cV.tick(ins)
                Vb(_vl2)

                def _al2(s, nt=nt, pb=pb):
                    cV.wait(s)
                    ins = s.activation(ocol[:, pb, 4:5], ocol[:, pb, 3:4],
                                       AF.Sqrt)
                    a2mark[nt] = cA.tick(ins)
                Ab(_al2)

                def _vl3(v, nt=nt, pb=pb):
                    cA.wait(v, a2mark[nt])
                    cOB[pb].wait(v, out_mark.get(nt - 2, 0))
                    ins = v.reciprocal(ocol[:, pb, 5:6], ocol[:, pb, 4:5])
                    v.wait_ge(cV.h, cV.tick(ins))
                    ins = None
                    for mb in range(4):
                        ins = v.tensor_scalar(ob[:, pb, ts(mb, 512)],
                                              z2[:, pb, ts(mb, 512)],
                                              ocol[:, pb, 1:2],
                                              ocol[:, pb, 5:6],
                                              ALU.subtract, ALU.mult)
                    oln[nt] = cV.tick(ins)
                Vb(_vl3)

                def _yo(g, nt=nt, pb=pb):
                    cV.wait(g, oln[nt])
                    ins = g.dma_start(out=out[ts(nt, 128), :], in_=ob[:, pb, :])
                    out_mark[nt] = cOB[pb].tick(ins, 16)
                Gb(_yo)

            def _fin(y):
                cOB[0].wait(y)
                cOB[1].wait(y)
                if dbg:
                    cDBG.wait(y)
            Yb(_fin)

    return nc, dout



_NC_CACHE = {}


def _get_nc(dbg=0, npairs=NPAIR, stages="full"):
    key = (dbg, npairs, stages)
    if key not in _NC_CACHE:
        _NC_CACHE[key] = build(dbg, npairs, stages)
    return _NC_CACHE[key]


def host_prep(inputs):
    f32 = np.float32
    x = np.asarray(inputs["L0_attn_hidden_states"], dtype=f32)
    wq = np.asarray(inputs["L0_attn_Wq"], dtype=f32)
    wk = np.asarray(inputs["L0_attn_Wk"], dtype=f32)
    wv = np.asarray(inputs["L0_attn_Wv"], dtype=f32)
    wo = np.asarray(inputs["L0_attn_Wo"], dtype=f32)
    gate = np.asarray(inputs["L0_attn_gate"], dtype=f32).reshape(1, NH)
    gw = np.asarray(inputs["L0_ffn_gate_w"], dtype=f32)
    mu = np.asarray(inputs["L0_ffn_mask_up_w"], dtype=f32)
    gp = np.asarray(inputs["L0_ffn_gate_proj_w"], dtype=f32)
    up = np.asarray(inputs["L0_ffn_up_proj_w"], dtype=f32)
    dn = np.asarray(inputs["L0_ffn_down_proj_w"], dtype=f32)
    inv = np.asarray(inputs["L0_ffn_inv_map"], dtype=np.int32)

    ac = np.ascontiguousarray
    xT = ac(x.transpose(0, 2, 1))
    shared = dict(
        wqT=ac(wq.T), wkT=ac(wk.T), wvT=ac(wv.T), woT=ac(wo.T),
        gt2=ac(gate), gwT=ac(gw.T), muo=ac(mu),
        gpT=ac(gp.transpose(0, 2, 1)), upT=ac(up.transpose(0, 2, 1)),
        dnT=ac(dn.transpose(0, 2, 1)), invm=ac(inv),
        iot=np.arange(E, dtype=f32).reshape(E, 1),
        idn=np.eye(128, dtype=f32),
    )
    inv_freq = (1.0 / np.float32(10000.0) ** (np.arange(0, H, 2,
                dtype=f32) / np.float32(H))).astype(f32)
    t = np.arange(S, dtype=f32)
    ang = (inv_freq[:, None] * t[None, :]).astype(f32)
    cosk = np.cos(ang).astype(f32)
    sink = np.sin(ang).astype(f32)
    shared["csk"] = ac(np.stack([cosk, sink], axis=1))

    in_maps = []
    for c in range(8):
        b, ch = c // 4, c % 4
        m = dict(shared)
        m["xTb"] = ac(xT[b])
        m["xqT"] = ac(xT[b][:, ch * TOK:(ch + 1) * TOK])
        m["xch"] = ac(x[b, ch * TOK:(ch + 1) * TOK, :])
        m["csq"] = ac(shared["csk"][:, :, ch * TOK:(ch + 1) * TOK])
        in_maps.append(m)
    return in_maps


def kernel(**inputs):
    nc, _ = _get_nc()
    in_maps = host_prep(inputs)
    res = run_bass_kernel_spmd(nc, in_maps, core_ids=list(range(8)))
    outp = np.empty((B, S, H), dtype=np.float32)
    for c in range(8):
        b, ch = c // 4, c % 4
        outp[b, ch * TOK:(ch + 1) * TOK, :] = res.results[c]["out"]
    return outp


# version 16
# speedup vs baseline: 1.0179x; 1.0179x over previous
import math
import numpy as np
import concourse.bass as bass
import concourse.mybir as mybir
from concourse.bass_utils import run_bass_kernel_spmd

F32 = mybir.dt.float32
BF16 = mybir.dt.bfloat16
I32 = mybir.dt.int32
AF = mybir.ActivationFunctionType
ALU = mybir.AluOpType
AX = mybir.AxisListType
ts = bass.ts

B, S, H = 2, 8192, 2048
NH, HD = 16, 128
NPAIR = 8
TOK = 2048
NSB = S // 512
NQB = TOK // 512
NKT = S // 128
E, G, GS, II = 64, 8, 8, 1408
IC = II // 128
MC = H // 128
SCALE = 1.0 / math.sqrt(HD)
EPS = 1e-5
NEG = -1.0e9


class Ctr:
    def __init__(self, es, nc, name):
        self.h = es.enter_context(nc.semaphore(name))
        self.n = 0

    def tick(self, ins, k=1):
        ins.then_inc(self.h, k)
        self.n += k
        return self.n

    def wait(self, eng, n=None):
        v = self.n if n is None else n
        if v > 0:
            eng.wait_ge(self.h, v)


def build(dbg=0, npairs=NPAIR, stages="full"):
    nc = bass.Bass("TRN2", target_bir_lowering=False, debug=False)
    dt = nc.dram_tensor

    xTb = dt("xTb", [H, S], F32, kind="ExternalInput").ap()
    xqT = dt("xqT", [H, TOK], F32, kind="ExternalInput").ap()
    xch = dt("xch", [TOK, H], F32, kind="ExternalInput").ap()
    wqT = dt("wqT", [H, H], F32, kind="ExternalInput").ap()
    wkT = dt("wkT", [H, H], F32, kind="ExternalInput").ap()
    wvT = dt("wvT", [H, H], F32, kind="ExternalInput").ap()
    woT = dt("woT", [H, H], F32, kind="ExternalInput").ap()
    gt2 = dt("gt2", [1, NH], F32, kind="ExternalInput").ap()
    csk = dt("csk", [1024, 2, S], F32, kind="ExternalInput").ap()
    csq = dt("csq", [1024, 2, TOK], F32, kind="ExternalInput").ap()
    gwT = dt("gwT", [H, E], F32, kind="ExternalInput").ap()
    muo = dt("muo", [G, H, GS], F32, kind="ExternalInput").ap()
    gpT = dt("gpT", [G, H, II], F32, kind="ExternalInput").ap()
    upT = dt("upT", [G, H, II], F32, kind="ExternalInput").ap()
    dnT = dt("dnT", [G, II, H], F32, kind="ExternalInput").ap()
    invm = dt("invm", [G, GS], I32, kind="ExternalInput").ap()
    iot = dt("iot", [E, 1], F32, kind="ExternalInput").ap()
    idn = dt("idn", [128, 128], F32, kind="ExternalInput").ap()

    out = dt("out", [TOK, H], F32, kind="ExternalOutput").ap()

    kfb = dt("kfb", [256, S], BF16, kind="Internal").ap()
    xblk = dt("xblk", [NSB, 128, MC, 512], BF16, kind="Internal").ap()
    xqblk = dt("xqblk", [NQB, 128, MC, 512], BF16, kind="Internal").ap()
    combT = dt("combT", [H, TOK], BF16, kind="Internal").ap()
    x2b16 = dt("x2b16", [TOK, H], BF16, kind="Internal").ap()
    x2f32 = dt("x2f32", [TOK, H], F32, kind="Internal").ap()
    y32 = dt("y32", [TOK, H], F32, kind="Internal").ap()
    sftd = dt("sftd", [G, GS, TOK], BF16, kind="Internal").ap()

    dbg_specs = {
        1: [("d_kt", [128, 2, S]), ("d_va", [128, 2, 64, 130]),
            ("d_mem", [128, 2, 132])],
        2: [("d_qt", [128, 2, TOK]), ("d_mo", [128, 2, 16, 132])],
        4: [("d_comb", [H, TOK])],
        8: [("d_x2", [TOK, H])],
        16: [("d_scl", [128, 16, 8]), ("d_sft", [G, GS, TOK])],
        32: [("d_y", [TOK, H])],
    }
    dout = {}
    for bit, specs in dbg_specs.items():
        if dbg & bit:
            for nm, shp in specs:
                dout[nm] = dt(nm, shp, F32, kind="ExternalOutput").ap()

    from contextlib import ExitStack
    es = ExitStack()
    C = es.enter_context
    with es:
        identf = C(nc.sbuf_tensor("identf", [128, 128], F32))
        ones1f = C(nc.sbuf_tensor("ones1f", [1, 128], F32))
        onescol = C(nc.sbuf_tensor("onescol", [128, 1], F32))
        gcols = C(nc.sbuf_tensor("gcols", [128, 32], F32))
        grow = C(nc.sbuf_tensor("grow", [1, 32], F32))
        iotc = C(nc.sbuf_tensor("iotc", [64, 1], F32))

        pS3 = [C(nc.psum_tensor(f"pS3_{i}", [128, 1536], F32)) for i in range(2)]
        pP = [C(nc.psum_tensor(f"pP{i}", [128, 512], F32)) for i in range(2)]
        pA = [pS3[0][:, 0:512], pS3[0][:, 512:1024], pS3[1][:, 0:512],
              pS3[1][:, 512:1024]]
        VB = [pS3[1][:, 0:512], pS3[1][:, 512:1024],
              pS3[0][:, 1024:1536], pS3[1][:, 1024:1536]]

        cXT = [Ctr(es, nc, f"cXT{i}") for i in range(2)]
        cBXL = [Ctr(es, nc, f"cBXL{i}") for i in range(2)]
        cBXS = [Ctr(es, nc, f"cBXS{i}") for i in range(2)]
        cCS = [Ctr(es, nc, f"cCS{i}") for i in range(2)]
        cWKV = [Ctr(es, nc, f"cWKV{i}") for i in range(2)]
        cWQ = Ctr(es, nc, "cWQ")
        cKFB = [Ctr(es, nc, f"cKFB{i}") for i in range(2)]
        cKFN = [Ctr(es, nc, f"cKFN{i}") for i in range(3)]
        cCB = [Ctr(es, nc, f"cCB{i}") for i in range(2)]
        cWO = Ctr(es, nc, "cWO")
        cCT = [Ctr(es, nc, f"cCT{i}") for i in range(2)]
        cXC = [Ctr(es, nc, f"cXC{i}") for i in range(2)]
        cX2W = Ctr(es, nc, "cX2W")
        cGW = Ctr(es, nc, "cGW")
        cX2T = Ctr(es, nc, "cX2T")
        cSFT = Ctr(es, nc, "cSFT")
        cMU2 = Ctr(es, nc, "cMU2")
        cGU = [Ctr(es, nc, f"cGU{i}") for i in range(2)]
        cDN = Ctr(es, nc, "cDN")
        cYL = [Ctr(es, nc, f"cYL{i}") for i in range(2)]
        cXL = [Ctr(es, nc, f"cXL{i}") for i in range(2)]
        cOB = [Ctr(es, nc, f"cOB{i}") for i in range(2)]
        cY = [Ctr(es, nc, f"cY{i}") for i in range(2)]
        cOUT = Ctr(es, nc, "cOUT")
        cDBG = Ctr(es, nc, "cDBG")
        cG2 = Ctr(es, nc, "cG2")
        cT = Ctr(es, nc, "cT")
        cV = Ctr(es, nc, "cV")
        cA = Ctr(es, nc, "cA")

        block = C(nc.Block())
        Tb, Vb, Ab, Yb, Gb = (block.tensor, block.vector, block.scalar,
                              block.sync, block.gpsimd)

        fr = {"xt0": 0, "xt1": 0, "cs0": 0, "cs1": 0, "pA01": 0,
              "pvb0": 0, "pvb1": 0, "pvb2": 0, "pvb3": 0,
              "kf0": 0, "kf1": 0, "wsl": 0,
              "mem": 0, "cb4": [0, 0], "rbev": 0, "pvfree": 0, "epi_done": 0}

        def _g0(g):
            cG2.tick(g.dma_start(out=identf[:, :], in_=idn), 16)
            cG2.tick(g.dma_start(out=iotc[:, :], in_=iot), 16)
            cG2.tick(g.dma_start(out=grow[0:1, 0:16], in_=gt2), 16)
        Gb(_g0)

        def _vones(v):
            v.memset(ones1f[:, :], 1.0)
            cV.tick(v.memset(onescol[:, :], 1.0))
        Vb(_vones)

        def _a0(s):
            cG2.wait(s)
            ins = s.activation(grow[0:1, 0:16], grow[0:1, 0:16], AF.Sigmoid)
            cA.tick(ins)
        Ab(_a0)

        def _v0(v):
            cA.wait(v)
            ins = v.tensor_scalar(grow[0:1, 16:32], grow[0:1, 0:16], -1.0, 1.0,
                                  ALU.mult, ALU.add)
            cV.tick(ins)
        Vb(_v0)

        def _t0(t):
            cV.wait(t)
            ins = t.matmul(pS3[0][:, 0:32], ones1f[0:1, :], grow[0:1, 0:32],
                           start=True, stop=True)
            cT.tick(ins)
        Tb(_t0)

        def _v1(v):
            cT.wait(v)
            ins = v.tensor_copy(gcols[:, :], pS3[0][:, 0:32])
            fr["mem"] = cV.tick(ins)
            fr["pA01"] = fr["mem"]
        Vb(_v1)

        p1 = ExitStack()

        class _NoAlloc:
            def __getitem__(self, k):
                raise RuntimeError("phase-1 tensor used in p2-only build")

        def P1(*args, **kw):
            if stages == "p2":
                return _NoAlloc()
            return p1.enter_context(nc.sbuf_tensor(*args, **kw))

        stg = [P1(f"stg{i}", [128, MC // 4, 512], BF16) for i in range(2)]
        kt = P1("kt", [128, 2, S], BF16)
        va = P1("va", [128, 2, 64, 130], BF16)
        qt = P1("qt", [128, 2, TOK], BF16)

        mem = P1("mem", [128, 2, 132], BF16)
        xt = [P1(f"xt{i}", [128, MC, 512], BF16) for i in range(2)]
        csb = [P1(f"csb{i}", [128, 2, 512], F32) for i in range(2)]
        wslkv = [P1(f"wslkv{i}", [128, MC, 2, 256], BF16) for i in range(2)]
        wslq = P1("wslq", [128, MC, 256], BF16)
        kf = [P1(f"kf{i}", [128, 2, 512], BF16) for i in range(2)]
        kfn = P1("kfn", [128, 12, 256], BF16)
        qft = P1("qft", [128, 2, TOK], BF16)
        rs = [P1(f"rs{i}", [128, 512], F32) for i in range(2)]
        esc = [P1(f"esc{i}", [128, 2, 512], BF16) for i in range(3)]
        pt = [P1(f"pt{i}", [128, 1536], BF16) for i in range(2)]
        cbt = P1("cbt", [128, 2, 512], BF16)
        sacc = P1("sacc", [128, 512], F32)
        rb = P1("rb", [128, 2, 512], BF16)

        if stages != "p2":
            def _vone(v):
                ins = v.memset(va[:, :, :, 128:130], 1.0)
                cV.tick(ins)
            Vb(_vone)
            fr["va1"] = cV.n

        xblk_w = {}
        bjc = [0]
        bxl_m = {}
        stg_rd = {}

        def emit_bounce(kind, bi, tgate=None):
            for qtr in range(4):
                j = bjc[0]
                bjc[0] += 1
                par = j % 2
                tg = tgate[qtr] if tgate is not None else 0

                def _gb(g, kind=kind, bi=bi, qtr=qtr, par=par, j=j, tg=tg):
                    cBXS[par].wait(g)
                    cT.wait(g, tg)
                    src = xTb if kind == "k" else xqT
                    ins = g.dma_start(
                        out=stg[par][:, :, :],
                        in_=src.rearrange("(c p) s -> p c s", p=128)
                        [:, ts(qtr, 4), ts(bi, 512)])
                    bxl_m[j] = cBXL[par].tick(ins, 16)
                Gb(_gb)

                def _yb(y, kind=kind, bi=bi, qtr=qtr, par=par):
                    cBXL[par].wait(y)
                    dst = xblk[bi] if kind == "k" else xqblk[bi]
                    ins = y.dma_start(out=dst[:, ts(qtr, 4), :],
                                      in_=stg[par][:, :, :])
                    xblk_w.setdefault((kind, bi), []).append(
                        (par, cBXS[par].tick(ins, 16)))
                Yb(_yb)

        def emit_rope(v, buf, dstA, dstB):
            v.tensor_tensor(rs[0][:, :], pA[0][:, 0:512], csb[buf][:, 0, :],
                            ALU.mult)
            ins = v.tensor_tensor(rs[1][:, :], pA[1][:, 0:512],
                                  csb[buf][:, 1, :], ALU.mult)
            v.wait_ge(cV.h, cV.tick(ins))
            ins = v.tensor_tensor(dstA, rs[0][:, :], rs[1][:, :], ALU.subtract)
            v.wait_ge(cV.h, cV.tick(ins))
            v.tensor_tensor(rs[0][:, :], pA[1][:, 0:512], csb[buf][:, 0, :],
                            ALU.mult)
            ins = v.tensor_tensor(rs[1][:, :], pA[0][:, 0:512],
                                  csb[buf][:, 1, :], ALU.mult)
            fr["pA01"] = cV.tick(ins)
            fr[f"cs{buf}"] = fr["pA01"]
            v.wait_ge(cV.h, fr["pA01"])
            ins = v.tensor_tensor(dstB, rs[0][:, :], rs[1][:, :], ALU.add)
            v.wait_ge(cV.h, cV.tick(ins))

        wsl_used = {}

        def emit_gw(hp, which="both"):
            def _gw(g, hp=hp, which=which):
                if which in ("both", "kv"):
                    cT.wait(g, wsl_used.get(hp - 2, 0))
                    for wi, w in ((0, wkT), (1, wvT)):
                        for d in range(2):
                            col = hp * 128 + d * 1024
                            ins = g.dma_start(
                                out=wslkv[hp % 2][:, :, wi, ts(d, 128)],
                                in_=w[:, col:col + 128].rearrange(
                                    "(c p) d -> p c d", p=128))
                            cWKV[hp % 2].tick(ins, 16)
                if which in ("both", "q"):
                    cT.wait(g, wsl_used.get(hp - 1, 0))
                    for d in range(2):
                        col = hp * 128 + d * 1024
                        ins = g.dma_start(
                            out=wslq[:, :, ts(d, 128)],
                            in_=wqT[:, col:col + 128].rearrange(
                                "(c p) d -> p c d", p=128))
                        cWQ.tick(ins, 16)
            Gb(_gw)

        def emit_kv_pair(hp):
            mm_used = {}
            kv_rdy = {}
            for sb in range(NSB):
                buf = sb % 2
                kmm = [0, 0]
                vmm = [0] * 4
                elu_mx = [0]
                kfn_rdy = [0]

                if sb < 2 and hp in prefetched_pairs:
                    pass
                else:
                    def _y(y, hp=hp, sb=sb, buf=buf):
                        if hp > 0:
                            cT.wait(y, fr[f"xt{buf}"])
                            for bpar, bval in xblk_w[("k", sb)]:
                                cBXS[bpar].wait(y, bval)
                            ins = y.dma_start(out=xt[buf][:, :, :],
                                              in_=xblk[sb])
                            cXT[buf].tick(ins, 16)
                        cV.wait(y, fr[f"cs{buf}"])
                        ins = y.dma_start(out=csb[buf][:, :, :],
                                          in_=csk[ts(hp, 128), :, ts(sb, 512)])
                        cCS[buf].tick(ins, 16)
                    Yb(_y)

                if hp == 0:
                    tb = cT.n
                    tgate = [stg_rd.get(4 * sb - 2, 0),
                             stg_rd.get(4 * sb - 1, 0), tb + 1, tb + 2]
                    emit_bounce("k", sb, tgate)
                    if sb == 8:
                        emit_gw(1, "kv")

                cv_base = [0]

                def _t(t, hp=hp, sb=sb, buf=buf, kmm=kmm, vmm=vmm,
                       cv_base=cv_base):
                    cv_base[0] = cV.n
                    cCS[buf].wait(t)
                    cWKV[hp % 2].wait(t)
                    if sb == 0:
                        cV.wait(t, fr["epi_done"])
                    cV.wait(t, fr["pA01"])
                    for st in range(4):
                        cV.wait(t, fr[f"pvb{st}"])
                    if hp == 0:
                        for qtr in range(4):
                            j = 4 * sb + qtr
                            cBXL[j % 2].wait(t, bxl_m[j])
                            ins = None
                            for q in range(4):
                                mc = 4 * qtr + q
                                for d in range(2):
                                    ins = t.matmul(
                                        pA[d][:, 0:512],
                                        wslkv[0][:, mc, 0, ts(d, 128)],
                                        stg[j % 2][:, q, :],
                                        start=(mc == 0), stop=(mc == MC - 1))
                                for st in range(4):
                                    ins = t.matmul(
                                        VB[st][:, 0:256],
                                        stg[j % 2][:, q, ts(st, 128)],
                                        wslkv[0][:, mc, 1, :],
                                        start=(mc == 0), stop=(mc == MC - 1))
                            stg_rd[j] = cT.tick(ins)
                        last = stg_rd[4 * sb + 3]
                        for d in range(2):
                            kmm[d] = last
                        for st in range(4):
                            vmm[st] = last
                    else:
                        cXT[buf].wait(t)
                        for d in range(2):
                            ins = None
                            for mc in range(MC):
                                ins = t.matmul(
                                    pA[d][:, 0:512],
                                    wslkv[hp % 2][:, mc, 0, ts(d, 128)],
                                    xt[buf][:, mc, :],
                                    start=(mc == 0), stop=(mc == MC - 1))
                            kmm[d] = cT.tick(ins)
                        for st in range(4):
                            ins = None
                            for mc in range(MC):
                                ins = t.matmul(
                                    VB[st][:, 0:256],
                                    xt[buf][:, mc, ts(st, 128)],
                                    wslkv[hp % 2][:, mc, 1, :],
                                    start=(mc == 0), stop=(mc == MC - 1))
                            vmm[st] = cT.tick(ins)
                        fr[f"xt{buf}"] = cT.n
                Tb(_t)

                if hp == 0:
                    assert stg_rd[4 * sb] == tgate[2] and \
                        stg_rd[4 * sb + 1] == tgate[3]

                def _v(v, hp=hp, sb=sb, buf=buf, kmm=kmm, vmm=vmm,
                       elu_mx=elu_mx, cv_base=cv_base):
                    sl = ts(sb, 512)
                    if dbg & 1 and hp == 1 and sb == 0:
                        cDBG.wait(v)
                    cT.wait(v, kmm[1])
                    emit_rope(v, buf, kt[:, 0, sl], kt[:, 1, sl])
                    assert fr["pA01"] == cv_base[0] + 3
                    cKFB[sb % 2].wait(v, fr[f"kf{sb % 2}"])
                    ins = None
                    for h in range(2):
                        v.tensor_scalar_min(esc[0][:, h, :], kt[:, h, sl], 0.0)
                        ins = v.tensor_scalar_max(esc[1][:, h, :], kt[:, h, sl],
                                                  0.0)
                    elu_mx[0] = cV.tick(ins)
                    for st in range(4):
                        cT.wait(v, vmm[st])
                        kti = sb * 4 + st
                        pv = VB[st]
                        v.tensor_copy(va[:, 0, kti, 0:128], pv[:, 0:128])
                        ins = v.tensor_copy(va[:, 1, kti, 0:128], pv[:, 128:256])
                        fr[f"pvb{st}"] = cV.tick(ins)
                Vb(_v)

                def _a(s, sb=sb, elu_mx=elu_mx):
                    cV.wait(s, elu_mx[0])
                    ins = s.activation(esc[2][:, :, :], esc[0][:, :, :], AF.Exp)
                    cA.tick(ins)
                Ab(_a)

                def _vf(v, sb=sb):
                    cA.wait(v)
                    ins = None
                    for h in range(2):
                        ins = v.tensor_tensor(kf[sb % 2][:, h, :],
                                              esc[2][:, h, :], esc[1][:, h, :],
                                              ALU.add)
                    cV.tick(ins)
                Vb(_vf)

                def _y2(g, hp=hp, sb=sb):
                    cV.wait(g)
                    ins = None
                    for h in range(2):
                        ins = g.dma_start(out=kfb[ts(h, 128), ts(sb, 512)],
                                          in_=kf[sb % 2][:, h, :])
                        cKFB[sb % 2].tick(ins, 16)
                    fr[f"kf{sb % 2}"] = cKFB[sb % 2].n
                Gb(_y2)

                def _al(s, sb=sb, rdy=kv_rdy, mm_used=mm_used):
                    cKFB[0].wait(s)
                    cKFB[1].wait(s)
                    if sb >= 3:
                        cT.wait(s, mm_used.get(sb - 3, 0))
                    ins = None
                    for st in range(4):
                        kti = sb * 4 + st
                        slot = (sb % 3) * 4 + st
                        ins = s.dma_start_transpose(
                            kfn[:, slot, :],
                            kfb[0:256, ts(kti, 128)])
                        cKFN[sb % 3].tick(ins, 16)
                    rdy[sb] = cKFN[sb % 3].n
                Ab(_al)

                def _tm(t, hp=hp, sb=sb - 2, kfn_rdy=None,
                        mm_used=mm_used, rdy=kv_rdy):
                    cKFN[sb % 3].wait(t, rdy[sb])
                    if sb == 0:
                        cV.wait(t, max(fr["mem"], fr["va1"]))
                    ins = None
                    for h in range(2):
                        reg = pP[h][:, 0:129]
                        for st in range(4):
                            kti = sb * 4 + st
                            slot = (sb % 3) * 4 + st
                            ins = t.matmul(reg, kfn[:, slot, ts(h, 128)],
                                           va[:, h, kti, 0:129],
                                           start=(sb == 0 and st == 0),
                                           stop=(sb == NSB - 1 and st == 3))
                        mm_used[sb] = cT.tick(ins)
                if sb >= 2:
                    Tb(_tm)

            kv_state[hp] = (mm_used, kv_rdy)

        def emit_kv_tail(hp):
            mm_used, kv_rdy = kv_state[hp]

            def _tml(t, mm_used=mm_used, rdy=kv_rdy):
                for sb in (NSB - 2, NSB - 1):
                    cKFN[sb % 3].wait(t, rdy[sb])
                    ins = None
                    for h in range(2):
                        reg = pP[h][:, 0:129]
                        for st in range(4):
                            kti = sb * 4 + st
                            slot = (sb % 3) * 4 + st
                            ins = t.matmul(reg, kfn[:, slot, ts(h, 128)],
                                           va[:, h, kti, 0:129],
                                           start=False,
                                           stop=(sb == NSB - 1 and st == 3))
                        mm_used[sb] = cT.tick(ins)
            Tb(_tml)

            def _vm(v):
                cT.wait(v)
                v.tensor_copy(mem[:, 0, 0:129], pP[0][:, 0:129])
                ins = v.tensor_copy(mem[:, 1, 0:129], pP[1][:, 0:129])
                fr["mem"] = cV.tick(ins)
            Vb(_vm)

            if dbg & 1 and hp == 0:
                def _dbgk(g):
                    cV.wait(g)
                    cDBG.tick(g.dma_start(out=dout["d_kt"], in_=kt[:, :, :]), 16)
                    cDBG.tick(g.dma_start(out=dout["d_va"], in_=va[:, :, :, :]), 16)
                    cDBG.tick(g.dma_start(out=dout["d_mem"][:, :, 0:129], in_=mem[:, :, 0:129]), 16)
                Gb(_dbgk)

        def emit_q_pair(hp):
            for qb in range(NQB):
                if hp == 0:
                    emit_bounce("q", qb)
                buf = qb % 2
                kmm = [0, 0]
                elu_mx = [0]

                def _y(y, hp=hp, qb=qb, buf=buf):
                    cT.wait(y, fr[f"xt{buf}"])
                    for bpar, bval in xblk_w[("q", qb)]:
                        cBXS[bpar].wait(y, bval)
                    ins = y.dma_start(out=xt[buf][:, :, :], in_=xqblk[qb])
                    cXT[buf].tick(ins, 16)
                    cV.wait(y, fr[f"cs{buf}"])
                    ins = y.dma_start(out=csb[buf][:, :, :],
                                      in_=csq[ts(hp, 128), :, ts(qb, 512)])
                    cCS[buf].tick(ins, 16)
                Yb(_y)

                def _t(t, hp=hp, qb=qb, buf=buf, kmm=kmm):
                    cXT[buf].wait(t)
                    cCS[buf].wait(t)
                    cWQ.wait(t)
                    cV.wait(t, fr["pA01"])
                    for d in range(2):
                        ins = None
                        for mc in range(MC):
                            ins = t.matmul(pA[d][:, 0:512],
                                           wslq[:, mc, ts(d, 128)],
                                           xt[buf][:, mc, :],
                                           start=(mc == 0), stop=(mc == MC - 1))
                        kmm[d] = cT.tick(ins)
                    fr[f"xt{buf}"] = cT.n
                Tb(_t)

                def _v(v, hp=hp, qb=qb, buf=buf, kmm=kmm, elu_mx=elu_mx):
                    sl = ts(qb, 512)
                    cT.wait(v, kmm[1])
                    emit_rope(v, buf, qt[:, 0, sl], qt[:, 1, sl])
                    ins = None
                    for h in range(2):
                        v.tensor_scalar_min(esc[0][:, h, :], qt[:, h, sl], 0.0)
                        ins = v.tensor_scalar_max(esc[1][:, h, :], qt[:, h, sl],
                                                  0.0)
                    elu_mx[0] = cV.tick(ins)
                Vb(_v)

                def _a(s, qb=qb, elu_mx=elu_mx):
                    cV.wait(s, elu_mx[0])
                    ins = s.activation(esc[2][:, :, :], esc[0][:, :, :], AF.Exp)
                    cA.tick(ins)
                Ab(_a)

                def _vq(v, qb=qb):
                    cA.wait(v)
                    ins = None
                    for h in range(2):
                        ins = v.tensor_tensor(qft[:, h, ts(qb, 512)],
                                              esc[2][:, h, :],
                                              esc[1][:, h, :], ALU.add)
                    cV.tick(ins)
                Vb(_vq)

            if dbg & 2 and hp == 0:
                def _dbgq(g):
                    cA.wait(g)
                    cV.wait(g)
                    cDBG.tick(g.dma_start(out=dout["d_qt"], in_=qt[:, :, :]), 16)
                Gb(_dbgq)

        def emit_attn_pair(hp):
            GRP = [3] * 21 + [1]
            NKG3 = len(GRP)
            goff = [0]
            for gsz in GRP:
                goff.append(goff[-1] + gsz)
            for h in range(2):
                hg = hp + 8 * h
                j0 = h * 1024 + hp * 128
                for qb in range(NQB):
                    sl = ts(qb, 512)
                    sc_done = {}
                    pv_done = {}
                    mm_marks = {}
                    cA_base = [0]
                    cV_base = [0]

                    def _t(t, hp=hp, h=h, qb=qb, sl=sl, cA_base=cA_base,
                           cV_base=cV_base, sc_done=sc_done, pv_done=pv_done,
                           mm_marks=mm_marks):
                        cA_base[0] = cA.n
                        cV_base[0] = cV.n
                        vb = cV_base[0]
                        for kg in range(NKG3):
                            ps = pS3[kg % 2]
                            cA.wait(t, cA_base[0] + max(0, kg - 1))
                            if kg == 0:
                                cV.wait(t, max(fr["rbev"], fr["pA01"],
                                               fr["pvb2"]))
                            elif kg == 1:
                                cV.wait(t, max(fr["pvb0"], fr["pvb1"],
                                               fr["pvb3"]))
                            ins = None
                            for j in range(GRP[kg]):
                                kti = goff[kg] + j
                                ins = t.matmul(ps[:, ts(j, 512)],
                                               kt[:, h, ts(kti, 128)],
                                               qt[:, h, sl],
                                               start=True, stop=True)
                            sc_done[kg] = cT.tick(ins)
                            if kg >= 1:
                                kgp = kg - 1
                                cA.wait(t, cA_base[0] + kgp + 1)
                                if kgp == 0:
                                    cV.wait(t, max(fr["pvfree"], fr["mem"]))
                                ins = None
                                for j in range(GRP[kgp]):
                                    kti = goff[kgp] + j
                                    ins = t.matmul(
                                        pP[0][:, 0:512],
                                        va[:, h, kti, 0:128],
                                        pt[kgp % 2][:, ts(j, 512)],
                                        start=(kti == 0),
                                        stop=(kti == NKT - 1))
                                pv_done[kgp] = cT.tick(ins)
                        kgp = NKG3 - 1
                        cA.wait(t, cA_base[0] + kgp + 1)
                        ins = t.matmul(pP[0][:, 0:512],
                                       va[:, h, goff[kgp], 0:128],
                                       pt[kgp % 2][:, 0:512],
                                       start=False, stop=True)
                        pv_done[kgp] = cT.tick(ins)
                        cV.wait(t, vb + 65)
                        cA.wait(t, cA_base[0] + NKG3 - 1)
                        ins = t.matmul(pS3[0][0:1, 0:512], onescol[:, 0:1],
                                       sacc[:, :], start=True, stop=True)
                        mm_marks["srow"] = cT.tick(ins)
                        ins = t.matmul(pS3[1][0:1, 0:512], mem[:, h, 128:129],
                                       qft[:, h, sl], start=True, stop=True)
                        mm_marks["lrow"] = cT.tick(ins)
                        cV.wait(t, max(fr["epi_done"], fr["mem"]))
                        ins = t.matmul(pP[1][:, 0:512], mem[:, h, 0:128],
                                       qft[:, h, sl], start=True, stop=True)
                        mm_marks["mo"] = cT.tick(ins)
                        cV.wait(t, vb + 66)
                        ins = t.matmul(pS3[0][:, 512:1024], ones1f[0:1, :],
                                       rs[0][0:1, 0:512], start=True, stop=True)
                        ins = t.matmul(pS3[1][:, 512:1024], ones1f[0:1, :],
                                       rs[1][0:1, 0:512], start=True,
                                       stop=True)
                        mm_marks["bc"] = cT.tick(ins)
                    Tb(_t)

                    def _ae(s, hp=hp, h=h, qb=qb, cA_base=cA_base,
                            cV_base=cV_base, sc_done=sc_done,
                            pv_done=pv_done):
                        for kg in range(NKG3):
                            cT.wait(s, sc_done[kg])
                            if kg >= 2:
                                cT.wait(s, pv_done[kg - 2])
                                cV.wait(s, cV_base[0] + 1 + 3 * (kg - 1))
                            ps = pS3[kg % 2]
                            w = GRP[kg] * 512
                            ins = s.activation(pt[kg % 2][:, 0:w], ps[:, 0:w],
                                               AF.Exp, scale=SCALE)
                            got = cA.tick(ins)
                            assert got == cA_base[0] + kg + 1
                    Ab(_ae)

                    def _v(v, hp=hp, h=h, qb=qb, hg=hg, cA_base=cA_base,
                           cV_base=cV_base, sc_done=sc_done, pv_done=pv_done,
                           mm_marks=mm_marks):
                        vb = cV_base[0]
                        tmp = esc[0][:, 0, :]
                        ins = v.memset(sacc[:, :], 0.0)
                        got = cV.tick(ins)
                        assert got == vb + 1
                        for kg in range(NKG3):
                            cA.wait(v, cA_base[0] + kg + 1)
                            pb = pt[kg % 2]
                            if GRP[kg] == 3:
                                v.wait_ge(cV.h, cV.n)
                                ins = v.tensor_tensor(tmp, pb[:, 0:512],
                                                      pb[:, 512:1024], ALU.add)
                                v.wait_ge(cV.h, cV.tick(ins))
                                ins = v.tensor_tensor(tmp, tmp,
                                                      pb[:, 1024:1536], ALU.add)
                                v.wait_ge(cV.h, cV.tick(ins))
                                ins = v.tensor_tensor(sacc[:, :], sacc[:, :],
                                                      tmp, ALU.add)
                                cV.tick(ins)
                            else:
                                v.wait_ge(cV.h, cV.n)
                                ins = v.tensor_tensor(sacc[:, :], sacc[:, :],
                                                      pb[:, 0:512], ALU.add)
                                cV.tick(ins)
                        assert cV.n == vb + 65
                        cT.wait(v, mm_marks["srow"])
                        cT.wait(v, mm_marks["lrow"])
                        v.reciprocal(rs[0][0:1, 0:512], pS3[0][0:1, 0:512])
                        ins = v.reciprocal(rs[1][0:1, 0:512],
                                           pS3[1][0:1, 0:512])
                        got = cV.tick(ins)
                        assert got == vb + 66
                        cT.wait(v, mm_marks["bc"])
                        v.tensor_copy(rb[:, 0, :], pS3[0][:, 512:1024])
                        ins = v.tensor_copy(rb[:, 1, :], pS3[1][:, 512:1024])
                        got = cV.tick(ins)
                        fr["rbev"] = got
                        assert got == vb + 67
                        cT.wait(v, pv_done[NKG3 - 1])
                        cT.wait(v, mm_marks["mo"])
                        cCB[qb % 2].wait(v, fr["cb4"][qb % 2])
                        v.wait_ge(cV.h, vb + 67)
                        v.scalar_tensor_tensor(rs[0][:, :], pP[0][:, 0:512],
                                               gcols[:, 16 + hg:17 + hg],
                                               rb[:, 0, :], ALU.mult, ALU.mult)
                        ins = v.scalar_tensor_tensor(rs[1][:, :],
                                                     pP[1][:, 0:512],
                                                     gcols[:, hg:hg + 1],
                                                     rb[:, 1, :], ALU.mult,
                                                     ALU.mult)
                        got = cV.tick(ins)
                        assert got == vb + 68
                        v.wait_ge(cV.h, vb + 68)
                        ins = v.tensor_tensor(cbt[:, qb % 2, :], rs[0][:, :],
                                              rs[1][:, :], ALU.add)
                        got = cV.tick(ins)
                        fr["pvfree"] = got
                        fr["epi_done"] = got
                        assert got == vb + 69
                    Vb(_v)

                    def _y(y, hp=hp, h=h, qb=qb, j0=j0):
                        cV.wait(y)
                        ins = y.dma_start(
                            out=combT[j0:j0 + 128, ts(qb, 512)],
                            in_=cbt[:, qb % 2, :])
                        fr["cb4"][qb % 2] = cCB[qb % 2].tick(ins, 16)
                    Yb(_y)

        kv_state = {}
        prefetched_pairs = set()

        def emit_xt_prefetch(hp):
            prefetched_pairs.add(hp)

            def _yp(y, hp=hp):
                for sb in (0, 1):
                    cT.wait(y, fr[f"xt{sb}"])
                    for bpar, bval in xblk_w[("k", sb)]:
                        cBXS[bpar].wait(y, bval)
                    ins = y.dma_start(out=xt[sb][:, :, :], in_=xblk[sb])
                    cXT[sb].tick(ins, 16)
                    cV.wait(y, fr[f"cs{sb}"])
                    ins = y.dma_start(out=csb[sb][:, :, :],
                                      in_=csk[ts(hp, 128), :, ts(sb, 512)])
                    cCS[sb].tick(ins, 16)
            Yb(_yp)

        if stages != "p2":
            emit_gw(0)
        for hp in range(npairs if stages != "p2" else 0):
            emit_kv_pair(hp)
            if stages == "kv":
                emit_kv_tail(hp)
                break
            emit_q_pair(hp)
            emit_kv_tail(hp)
            wsl_used[hp] = cT.n
            if hp + 1 < npairs:
                emit_gw(hp + 1, "q" if hp == 0 else "both")
            if stages == "kvq":
                break
            if hp + 1 < npairs:
                emit_xt_prefetch(hp + 1)
            emit_attn_pair(hp)

        if stages in ("kv", "kvq"):
            def _fin(g):
                cV.wait(g)
                cA.wait(g)
                cKFB[0].wait(g)
                cKFB[1].wait(g)
                cDBG.wait(g)
                ins = g.dma_start(out=out, in_=xch)
                cOUT.tick(ins, 16)
                cOUT.wait(g)
            Gb(_fin)
            p1.close()
            return nc, dout

        if dbg & 4:
            def _dbgc(g):
                cCB[0].wait(g)
                cCB[1].wait(g)
                for hp in range(npairs):
                    for h in range(2):
                        j0 = h * 1024 + hp * 128
                        ins = g.dma_start(out=dout["d_comb"][j0:j0 + 128, :],
                                          in_=combT[j0:j0 + 128, :])
                        cDBG.tick(ins, 16)
            Gb(_dbgc)

        if stages == "attn":
            def _fin(g):
                cCB[0].wait(g)
                cCB[1].wait(g)
                cDBG.wait(g)
                ins = g.dma_start(out=out, in_=xch)
                cOUT.tick(ins, 16)
                cOUT.wait(g)
            Gb(_fin)
            p1.close()
            return nc, dout

        p1.close()

        def emit_ln(nt, src_psum, xcol_ap, zs, s48, lncol, sqscr, x2f_ap,
                    x2b_ap, done_dict, war_ctr=None, war_val=0):
            vtick = [0]

            def _v(v):
                ins = None
                for mb in range(4):
                    ins = v.scalar_tensor_tensor(
                        zs[:, ts(mb, 512)], src_psum[mb][:, :], 1.0,
                        xcol_ap[:, ts(mb, 512)], ALU.mult, ALU.add,
                        accum_out=s48[:, mb:mb + 1])
                v.wait_ge(cV.h, cV.tick(ins))
                ins = v.tensor_reduce(
                    lncol[:, 0:1].rearrange("p (a o) -> p a o", o=1),
                    s48[:, 0:4].rearrange("p (a f) -> p a f", a=1),
                    AX.X, ALU.add)
                v.wait_ge(cV.h, cV.tick(ins))
                ins = v.tensor_scalar_mul(lncol[:, 1:2], lncol[:, 0:1], 1.0 / H)
                vtick[0] = cV.tick(ins)
            Vb(_v)

            def _aq(s):
                cV.wait(s, vtick[0])
                for mb in range(4):
                    ins = s.activation(sqscr[:, ts(mb, 512)],
                                       zs[:, ts(mb, 512)], AF.Square,
                                       accum_out=s48[:, 4 + mb:5 + mb])
                cA.tick(ins)
            Ab(_aq)

            def _v2(v):
                cA.wait(v)
                ins = v.tensor_reduce(
                    lncol[:, 2:3].rearrange("p (a o) -> p a o", o=1),
                    s48[:, 4:8].rearrange("p (a f) -> p a f", a=1),
                    AX.X, ALU.add)
                v.wait_ge(cV.h, cV.tick(ins))
                ins = v.tensor_scalar_mul(lncol[:, 2:3], lncol[:, 2:3], 1.0 / H)
                w = cV.tick(ins)
                ins = v.tensor_tensor(lncol[:, 3:4], lncol[:, 1:2],
                                      lncol[:, 1:2], ALU.mult)
                v.wait_ge(cV.h, cV.tick(ins))
                ins = v.tensor_tensor(lncol[:, 3:4], lncol[:, 2:3],
                                      lncol[:, 3:4], ALU.subtract)
                v.wait_ge(cV.h, cV.tick(ins))
                ins = v.tensor_scalar_add(lncol[:, 3:4], lncol[:, 3:4], EPS)
                cV.tick(ins)
            Vb(_v2)

            def _a2(s):
                cV.wait(s)
                ins = s.activation(lncol[:, 4:5], lncol[:, 3:4], AF.Sqrt)
                cA.tick(ins)
            Ab(_a2)

            def _v3(v):
                cA.wait(v)
                if war_ctr is not None:
                    war_ctr.wait(v, war_val)
                ins = v.reciprocal(lncol[:, 5:6], lncol[:, 4:5])
                v.wait_ge(cV.h, cV.tick(ins))
                ins = None
                for mb in range(4):
                    ins = v.tensor_scalar(x2f_ap[:, ts(mb, 512)],
                                          zs[:, ts(mb, 512)],
                                          lncol[:, 1:2], lncol[:, 5:6],
                                          ALU.subtract, ALU.mult)
                if x2b_ap is not None:
                    v.wait_ge(cV.h, cV.tick(ins))
                    ins = v.tensor_copy(x2b_ap[:, :], x2f_ap[:, :])
                done_dict[nt] = cV.tick(ins)
            Vb(_v3)

        w_es = ExitStack()
        wo_sb = w_es.enter_context(nc.sbuf_tensor("wo_sb", [128, MC, H], BF16))
        ct = w_es.enter_context(nc.sbuf_tensor("ct", [128, 2, MC, 512], BF16))
        xc = w_es.enter_context(nc.sbuf_tensor("xc", [128, 2, H], F32))
        zs = w_es.enter_context(nc.sbuf_tensor("zs", [128, H], F32))
        sqs = w_es.enter_context(nc.sbuf_tensor("sqs", [128, H], BF16))
        x2f = w_es.enter_context(nc.sbuf_tensor("x2f", [128, H], F32))
        x2b = w_es.enter_context(nc.sbuf_tensor("x2b", [128, H], BF16))
        lncol = w_es.enter_context(nc.sbuf_tensor("lncol", [128, 6], F32))
        s48 = w_es.enter_context(nc.sbuf_tensor("s48", [128, 8], F32))

        def _gwo(g):
            cT.wait(g)
            cV.wait(g)
            cA.wait(g)
            cCB[0].wait(g)
            cCB[1].wait(g)
            for mc in range(MC):
                ins = g.dma_start(out=wo_sb[:, mc, :], in_=woT[ts(mc, 128), :])
                cWO.tick(ins, 16)
        Gb(_gwo)

        ct_ready = {}
        wo_done = {}
        ln1_done = {}
        xc_ready = {}
        x2w_mark = {}

        for nb in range(NQB):
            def _yc(y, nb=nb):
                if nb == 0:
                    cCB[0].wait(y)
                    cCB[1].wait(y)
                    cT.wait(y)
                    cV.wait(y)
                    cA.wait(y)
                if nb >= 2:
                    cT.wait(y, wo_done.get((nb - 2, 3, 3), 0))
                ins = None
                for jc in range(MC):
                    ins = y.dma_start(
                        out=ct[:, nb % 2, jc, :],
                        in_=combT[ts(jc, 128), ts(nb, 512)])
                    cCT[nb % 2].tick(ins, 16)
                ct_ready[nb] = cCT[nb % 2].n
            Yb(_yc)

            for nt4 in range(4):
                nt = nb * 4 + nt4

                def _yx(y, nb=nb, nt=nt):
                    cV.wait(y, ln1_done.get(nt - 2, 0))
                    ins = y.dma_start(out=xc[:, nt % 2, :],
                                      in_=xch[ts(nt, 128), :])
                    xc_ready[nt] = cXC[nt % 2].tick(ins, 16)
                Yb(_yx)

                wb = [pS3[nt % 2][:, 0:512], pS3[nt % 2][:, 512:1024],
                      pS3[nt % 2][:, 1024:1536], pP[nt % 2][:, 0:512]]

                def _tw(t, nb=nb, nt=nt, nt4=nt4, wb=wb):
                    cCT[nb % 2].wait(t, ct_ready[nb])
                    cWO.wait(t)
                    if nt <= 1:
                        cV.wait(t)
                    cV.wait(t, ln1_done.get(nt - 2, 0))
                    for mb in range(4):
                        ins = None
                        for jc in range(MC):
                            ins = t.matmul(wb[mb][:, :],
                                           ct[:, nb % 2, jc, ts(nt4, 128)],
                                           wo_sb[:, jc, ts(mb, 512)],
                                           start=(jc == 0), stop=(jc == MC - 1))
                        wo_done[(nb, nt4, mb)] = cT.tick(ins)
                Tb(_tw)

                def _vpre(v, nb=nb, nt=nt, nt4=nt4):
                    cT.wait(v, wo_done[(nb, nt4, 3)])
                    cXC[nt % 2].wait(v, xc_ready[nt])
                Vb(_vpre)

                emit_ln(nt, wb, xc[:, nt % 2, :], zs, s48, lncol, sqs,
                        x2f, x2b, ln1_done, war_ctr=cX2W,
                        war_val=x2w_mark.get(nt - 1, 0))

                def _yw(g, nt=nt):
                    cV.wait(g, ln1_done[nt])
                    ins = g.dma_start(out=x2f32[ts(nt, 128), :], in_=x2f[:, :])
                    cX2W.tick(ins, 16)
                    ins = g.dma_start(out=x2b16[ts(nt, 128), :], in_=x2b[:, :])
                    x2w_mark[nt] = cX2W.tick(ins, 16)
                Gb(_yw)

        x2_written = cX2W.n
        w_es.close()

        if dbg & 8:
            def _dbgx(g):
                cX2W.wait(g, x2_written)
                ins = g.dma_start(out=dout["d_x2"], in_=x2f32)
                cDBG.tick(ins, 16)
            Gb(_dbgx)

        if stages == "ln1":
            def _fin(g):
                cX2W.wait(g, x2_written)
                cDBG.wait(g)
                ins = g.dma_start(out=out, in_=x2f32)
                cOUT.tick(ins, 16)
                cOUT.wait(g)
            Gb(_fin)
            return nc, dout

        scl = C(nc.sbuf_tensor("scl", [128, MC, 8], F32))
        x2T = C(nc.sbuf_tensor("x2T", [128, MC, TOK], BF16))

        g_es = ExitStack()
        gw_sb = g_es.enter_context(nc.sbuf_tensor("gw_sb", [128, MC, E], BF16))
        lgt = g_es.enter_context(nc.sbuf_tensor("lgt", [128, E], F32))
        gx = g_es.enter_context(nc.sbuf_tensor("gx", [128, E], F32))
        gsc = g_es.enter_context(nc.sbuf_tensor("gsc", [128, E], F32))
        m8 = g_es.enter_context(nc.sbuf_tensor("m8", [128, 8], F32))
        gcol2 = g_es.enter_context(nc.sbuf_tensor("gcol2", [128, 4], F32))
        wet = g_es.enter_context(nc.sbuf_tensor("wet", [64, 128], F32))
        ivb = g_es.enter_context(nc.sbuf_tensor("ivb", [64, 64], F32))
        poh = g_es.enter_context(nc.sbuf_tensor("poh", [64, 64], F32))
        ivr = g_es.enter_context(nc.sbuf_tensor("ivr", [1, 64], F32))
        ivri = g_es.enter_context(nc.sbuf_tensor("ivri", [1, 64], I32))
        fl = g_es.enter_context(nc.sbuf_tensor("fl", [128, E], F32))
        exs = g_es.enter_context(nc.sbuf_tensor("exs", [128, E], F32))
        sd8 = g_es.enter_context(nc.sbuf_tensor("sd8", [128, 8], F32))
        sfl = g_es.enter_context(nc.sbuf_tensor("sfl", [128, E], F32))
        sft_nt = g_es.enter_context(nc.sbuf_tensor("sft_nt", [8, G, 128], BF16))

        def _gg(g):
            cT.wait(g)
            cV.wait(g)
            cX2W.wait(g, x2_written)
            for mc in range(MC):
                ins = g.dma_start(out=gw_sb[:, mc, :], in_=gwT[ts(mc, 128), :])
                cGW.tick(ins, 16)
            ins = g.dma_start(out=ivri[:, :],
                              in_=invm.rearrange("(o a) b -> o (a b)", o=1))
            cGW.tick(ins, 16)
        Gb(_gg)

        def _yx2(y):
            cX2W.wait(y, x2_written)
            cT.wait(y)
            cV.wait(y)
            ins = None
            for nb in range(NQB):
                for mc in range(MC):
                    ins = y.dma_start_transpose(
                        x2T[:, mc, ts(nb, 512)],
                        x2b16[ts(nb, 512), ts(mc, 128)])
                    cX2T.tick(ins, 16)
        Yb(_yx2)
        x2T_ready = cX2T.n

        def _vp(v):
            cGW.wait(v)
            ins = v.tensor_copy(ivr[:, :], ivri[:, :])
            cV.tick(ins)
        Vb(_vp)

        def _tp(t):
            cV.wait(t)
            ins = t.matmul(pS3[0][0:64, 0:64], ones1f[0:1, 0:64], ivr[0:1, :],
                           start=True, stop=True)
            cT.tick(ins)
        Tb(_tp)

        def _vp2(v):
            cT.wait(v)
            ins = v.tensor_copy(ivb[0:64, :], pS3[0][0:64, 0:64])
            v.wait_ge(cV.h, cV.tick(ins))
            ins = v.tensor_scalar(poh[0:64, :], ivb[0:64, :], iotc[0:64, 0:1],
                                  None, ALU.is_equal)
            cV.tick(ins)
        Vb(_vp2)

        sft_w = {}
        for nt in range(MC):
            nb, nt4 = nt // 4, nt % 4
            lgmm = [0]
            flmm = [0]
            wemm = [0]
            sftr = {}

            def _tg(t, nt=nt, nb=nb, nt4=nt4, lgmm=lgmm):
                cX2T.wait(t, x2T_ready)
                cGW.wait(t)
                cV.wait(t)
                ins = None
                for mc in range(MC):
                    ins = t.matmul(pP[0][:, 0:64],
                                   x2T[:, mc, ts(nt, 128)],
                                   gw_sb[:, mc, :],
                                   start=(mc == 0), stop=(mc == MC - 1))
                lgmm[0] = cT.tick(ins)
            Tb(_tg)

            def _vg(v, nt=nt, lgmm=lgmm):
                cT.wait(v, lgmm[0])
                ins = v.tensor_copy(lgt[:, :], pP[0][:, 0:64])
                v.wait_ge(cV.h, cV.tick(ins))
                ins = v.reduce_max(
                    gcol2[:, 0:1].rearrange("p (a o) -> p a o", o=1),
                    lgt[:, :].rearrange("p (a f) -> p a f", a=1), AX.X)
                v.wait_ge(cV.h, cV.tick(ins))
                ins = v.tensor_scalar_mul(gcol2[:, 1:2], gcol2[:, 0:1], -1.0)
                cV.tick(ins)
            Vb(_vg)

            def _ag(s, nt=nt):
                cV.wait(s)
                ins = s.activation(gx[:, :], lgt[:, :], AF.Exp,
                                   bias=gcol2[:, 1:2],
                                   accum_out=gcol2[:, 2:3])
                cA.tick(ins)
            Ab(_ag)

            def _vg2(v, nt=nt):
                cA.wait(v)
                ins = v.reciprocal(gcol2[:, 3:4], gcol2[:, 2:3])
                v.wait_ge(cV.h, cV.tick(ins))
                ins = v.tensor_scalar(gsc[:, :], gx[:, :], gcol2[:, 3:4], None,
                                      ALU.mult)
                v.wait_ge(cV.h, cV.tick(ins))
                ins = v.max(m8[:, :], gsc[:, :])
                v.wait_ge(cV.h, cV.tick(ins))
                ins = v.tensor_scalar(lgt[:, :], gsc[:, :], m8[:, 7:8], None,
                                      ALU.is_ge)
                v.wait_ge(cV.h, cV.tick(ins))
                ins = v.tensor_tensor(gx[:, :], gsc[:, :], lgt[:, :], ALU.mult)
                cV.tick(ins)
            Vb(_vg2)

            def _tg2(t, nt=nt, wemm=wemm):
                cV.wait(t)
                ins = t.transpose(pS3[0][0:64, 0:128], gx[:, :], identf[:, :])
                wemm[0] = cT.tick(ins)
            Tb(_tg2)

            def _vg3(v, nt=nt, wemm=wemm):
                cT.wait(v, wemm[0])
                ins = v.tensor_copy(wet[0:64, :], pS3[0][0:64, 0:128])
                cV.tick(ins)
            Vb(_vg3)

            def _tg3(t, nt=nt, flmm=flmm):
                cV.wait(t)
                ins = t.matmul(pP[1][:, 0:64], wet[0:64, :], poh[0:64, :],
                               start=True, stop=True)
                flmm[0] = cT.tick(ins)
            Tb(_tg3)

            def _vg4(v, nt=nt, flmm=flmm):
                cT.wait(v, flmm[0])
                ins = v.tensor_copy(fl[:, :], pP[1][:, 0:64])
                v.wait_ge(cV.h, cV.tick(ins))
                v.tensor_reduce(
                    scl[:, nt, :].rearrange("p (g o) -> p g o", o=1),
                    fl[:, :].rearrange("p (g s) -> p g s", g=G),
                    AX.X, ALU.add)
                ins = v.tensor_scalar(lgt[:, :], fl[:, :], 0.0, None,
                                      ALU.is_equal)
                v.wait_ge(cV.h, cV.tick(ins))
                ins = v.scalar_tensor_tensor(lgt[:, :], lgt[:, :], NEG,
                                             fl[:, :], ALU.mult, ALU.add)
                cV.tick(ins)
            Vb(_vg4)

            def _ag2(s, nt=nt):
                cV.wait(s)
                ins = s.activation(exs[:, :], lgt[:, :], AF.Exp)
                cA.tick(ins)
            Ab(_ag2)

            def _vg5(v, nt=nt):
                cA.wait(v)
                ins = v.tensor_reduce(
                    sd8[:, :].rearrange("p (g o) -> p g o", o=1),
                    exs[:, :].rearrange("p (g s) -> p g s", g=G),
                    AX.X, ALU.add)
                v.wait_ge(cV.h, cV.tick(ins))
                ins = v.tensor_scalar_max(sd8[:, :], sd8[:, :], 1e-30)
                v.wait_ge(cV.h, cV.tick(ins))
                ins = v.reciprocal(sd8[:, :], sd8[:, :])
                v.wait_ge(cV.h, cV.tick(ins))
                ins = None
                for g in range(G):
                    ins = v.tensor_scalar(sfl[:, ts(g, GS)], exs[:, ts(g, GS)],
                                          sd8[:, g:g + 1], None, ALU.mult)
                cV.tick(ins)
            Vb(_vg5)

            vg6_base = cV.n
            sftw_prev = sft_w.get(nt - 1, 0)

            def _tg4(t, nt=nt, sftr=sftr, vg6_base=vg6_base):
                cV.wait(t, vg6_base)
                for g in range(G):
                    reg = (pS3[1][0:8, 0:128] if g % 2 == 0
                           else pS3[1][0:8, 512:640])
                    if g >= 2:
                        cV.wait(t, vg6_base + g - 1)
                    ins = t.transpose(reg, sfl[:, ts(g, GS)], identf[:, :])
                    sftr[g] = cT.tick(ins)
            Tb(_tg4)

            def _vg6(v, nt=nt, sftr=sftr, vg6_base=vg6_base,
                     sftw_prev=sftw_prev):
                cSFT.wait(v, sftw_prev)
                for g in range(G):
                    reg = (pS3[1][0:8, 0:128] if g % 2 == 0
                           else pS3[1][0:8, 512:640])
                    cT.wait(v, sftr[g])
                    ins = v.tensor_copy(sft_nt[0:8, g, :], reg)
                    got = cV.tick(ins)
                    assert got == vg6_base + g + 1
            Vb(_vg6)

            def _yg6(y, nt=nt):
                cV.wait(y)
                ins = y.dma_start(
                    out=sftd[:, :, ts(nt, 128)].rearrange("g s c -> s g c"),
                    in_=sft_nt[0:8, :, :])
                sft_w[nt] = cSFT.tick(ins, 16)
            Yb(_yg6)

        sft_written = cSFT.n
        g_es.close()

        if dbg & 16:
            def _dbgg(g):
                cV.wait(g)
                cSFT.wait(g, sft_written)
                cDBG.tick(g.dma_start(out=dout["d_scl"], in_=scl[:, :, :]), 16)
                cDBG.tick(g.dma_start(out=dout["d_sft"], in_=sftd), 16)
            Gb(_dbgg)

        f_es = ExitStack()
        aT = f_es.enter_context(nc.sbuf_tensor("aT", [128, IC, TOK], BF16))
        dn_sb = f_es.enter_context(nc.sbuf_tensor("dn_sb", [128, IC, H], BF16))
        guw = [f_es.enter_context(
            nc.sbuf_tensor(f"guw{i}", [128, MC, 2, 128], BF16))
            for i in range(2)]
        mun = f_es.enter_context(nc.sbuf_tensor("mun", [128, MC, GS], BF16))
        sft_g = f_es.enter_context(nc.sbuf_tensor("sft_g", [8, TOK], BF16))
        wmg = f_es.enter_context(nc.sbuf_tensor("wmg", [8, 128], BF16))
        wmu = f_es.enter_context(nc.sbuf_tensor("wmu", [8, 128], BF16))
        sil = f_es.enter_context(nc.sbuf_tensor("sil", [128, 512], BF16))
        dsc = f_es.enter_context(nc.sbuf_tensor("dsc", [128, 2, H], F32))

        guw_ready = {}
        guw_used = {}
        dn_ready = {}
        dn_used = {}
        mun_ready = {}
        yacc = {}
        yfin = {}

        for g in range(G):
            tgu_base = [0]

            def _gf(gg, g=g, tgu_base=tgu_base):
                tgu_base[0] = cT.n
                cSFT.wait(gg, sft_written)
                if g == 0:
                    cT.wait(gg)
                    cV.wait(gg)
                if g >= 1:
                    cT.wait(gg, guw_used.get((g - 1, "mun"), 0))
                    cT.wait(gg, guw_used[(g - 1, IC - 1)])
                ins = gg.dma_start(out=sft_g[0:8, :], in_=sftd[g])
                cMU2.tick(ins, 16)
                ins = gg.dma_start(
                    out=mun[:, :, :],
                    in_=muo[g].rearrange("(c p) s -> p c s", p=128))
                mun_ready[g] = cMU2.tick(ins, 16)
                for ic in range(IC):
                    slot = ic % 2
                    if ic >= 2:
                        cT.wait(gg, tgu_base[0] + 9 * (ic - 1))
                    elif g >= 1:
                        cT.wait(gg, guw_used[(g - 1, IC - 2 + ic)])
                    for half, w in ((0, gpT), (1, upT)):
                        for q in range(4):
                            ins = gg.dma_start(
                                out=guw[slot][:, ts(q, 4), half, :],
                                in_=w[g][:, ts(ic, 128)].rearrange(
                                    "(c p) i -> p c i", p=128)[:, ts(q, 4), :])
                            cGU[slot].tick(ins, 16)
                    guw_ready[(g, ic)] = cGU[slot].n
                if g >= 1:
                    cT.wait(gg, dn_used[g - 1])
                for q in range(4):
                    sl = bass.ds(q * 3, 3) if q < 3 else bass.ds(9, 2)
                    ins = gg.dma_start(
                        out=dn_sb[:, sl, :],
                        in_=dnT[g].rearrange("(c p) h -> p c h",
                                             p=128)[:, sl, :])
                    cDN.tick(ins, 16)
                dn_ready[g] = cDN.n
            Gb(_gf)

            gu_mm = {}
            wm_mm = {}
            agu_base = [0]
            vgu_base = [0]

            def _tgu(t, g=g, gu_mm=gu_mm, wm_mm=wm_mm, agu_base=agu_base,
                     vgu_base=vgu_base):
                agu_base[0] = cA.n
                vgu_base[0] = cV.n
                cMU2.wait(t, mun_ready[g])
                for ic in range(IC):
                    cGU[ic % 2].wait(t, guw_ready[(g, ic)])
                    if ic >= 1:
                        cV.wait(t, vgu_base[0] + (ic - 1) * 5 + 1)
                    ins = None
                    for half in range(2):
                        reg = (pS3[0][0:8, 0:128] if half == 0
                               else pS3[0][0:8, 512:640])
                        for mc in range(MC):
                            ins = t.matmul(reg, mun[:, mc, :],
                                           guw[ic % 2][:, mc, half, :],
                                           start=(mc == 0), stop=(mc == MC - 1))
                    wm_mm[ic] = cT.tick(ins)
                    for nb in range(NQB):
                        cA.wait(t, agu_base[0] + ic * 4 + nb)
                        cV.wait(t, vgu_base[0] + ic * 5 + (1 if nb == 0
                                                           else nb + 1))
                        ins = None
                        for mc in range(MC):
                            ins = t.matmul(pP[0][:, :],
                                           guw[ic % 2][:, mc, 0, :],
                                           x2T[:, mc, ts(nb, 512)],
                                           start=(mc == 0), stop=False)
                        ins = t.matmul(pP[0][:, :], wmg[0:8, :],
                                       sft_g[0:8, ts(nb, 512)],
                                       start=False, stop=True)
                        gmm = cT.tick(ins)
                        ins = None
                        for mc in range(MC):
                            ins = t.matmul(pP[1][:, :],
                                           guw[ic % 2][:, mc, 1, :],
                                           x2T[:, mc, ts(nb, 512)],
                                           start=(mc == 0), stop=False)
                        ins = t.matmul(pP[1][:, :], wmu[0:8, :],
                                       sft_g[0:8, ts(nb, 512)],
                                       start=False, stop=True)
                        umm = cT.tick(ins)
                        gu_mm[(ic, nb)] = (gmm, umm)
                    guw_used[(g, ic)] = cT.n
                    assert cT.n == tgu_base[0] + 9 * (ic + 1)
                guw_used[(g, "mun")] = cT.n
            Tb(_tgu)

            def _agu(s, g=g, gu_mm=gu_mm, agu_base=agu_base,
                     vgu_base=vgu_base):
                for ic in range(IC):
                    for nb in range(NQB):
                        gmm, umm = gu_mm[(ic, nb)]
                        cT.wait(s, gmm)
                        cV.wait(s, vgu_base[0] + (ic * 5 if nb == 0
                                                  else ic * 5 + nb + 1))
                        ins = s.activation(sil[:, :], pP[0][:, :], AF.Silu)
                        got = cA.tick(ins)
                        assert got == agu_base[0] + ic * 4 + nb + 1
            Ab(_agu)

            def _vgu(v, g=g, gu_mm=gu_mm, wm_mm=wm_mm, agu_base=agu_base,
                     vgu_base=vgu_base):
                if g >= 1:
                    cT.wait(v, dn_used[g - 1])
                for ic in range(IC):
                    cT.wait(v, wm_mm[ic])
                    if ic >= 1:
                        cT.wait(v, gu_mm[(ic - 1, NQB - 1)][1])
                    v.tensor_copy(wmg[0:8, :], pS3[0][0:8, 0:128])
                    ins = v.tensor_copy(wmu[0:8, :], pS3[0][0:8, 512:640])
                    got = cV.tick(ins)
                    assert got == vgu_base[0] + ic * 5 + 1, (got, vgu_base, ic)
                    for nb in range(NQB):
                        gmm, umm = gu_mm[(ic, nb)]
                        cT.wait(v, umm)
                        cA.wait(v, agu_base[0] + ic * 4 + nb + 1)
                        ins = v.tensor_tensor(aT[:, ic, ts(nb, 512)],
                                              sil[:, :], pP[1][:, :], ALU.mult)
                        got = cV.tick(ins)
                        assert got == vgu_base[0] + ic * 5 + nb + 2
            Vb(_vgu)


            dmm = {}
            vdn_base = [0]

            def _tdn(t, g=g, dmm=dmm, vdn_base=vdn_base):
                vdn_base[0] = cV.n
                cDN.wait(t, dn_ready[g])
                cV.wait(t)
                for nt in range(MC):
                    for hb in range(4):
                        cV.wait(t, vdn_base[0] + max(0, (nt - 1) * 4 + hb + 1))
                        dreg = (pS3[1][:, ts(hb, 512)] if hb < 3
                                else pS3[0][:, 1024:1536])
                        ins = None
                        for ic in range(IC):
                            ins = t.matmul(dreg,
                                           aT[:, ic, ts(nt, 128)],
                                           dn_sb[:, ic, ts(hb, 512)],
                                           start=(ic == 0), stop=(ic == IC - 1))
                        dmm[(nt, hb)] = cT.tick(ins)
                dn_used[g] = cT.n
            Tb(_tdn)

            def _vdn(v, g=g, dmm=dmm, vdn_base=vdn_base):
                ybase = (cY[0].n, cY[1].n)
                for nt in range(MC):
                    par = nt % 2
                    if nt >= 2:
                        cY[par].wait(v, ybase[par] + 16 * (nt // 2))
                    else:
                        cY[par].wait(v, ybase[par])
                    for hb in range(4):
                        cT.wait(v, dmm[(nt, hb)])
                        dreg = (pS3[1][:, ts(hb, 512)] if hb < 3
                                else pS3[0][:, 1024:1536])
                        ins = v.tensor_scalar(dsc[:, nt % 2, ts(hb, 512)],
                                              dreg,
                                              scl[:, nt, g:g + 1], None,
                                              ALU.mult)
                        got = cV.tick(ins)
                        assert got == vdn_base[0] + nt * 4 + hb + 1
            Vb(_vdn)

            def _gdn(gg, g=g, vdn_base=vdn_base):
                ybase2 = (cY[0].n, cY[1].n)
                for nt in range(MC):
                    cV.wait(gg, vdn_base[0] + nt * 4 + 4)
                    if g >= 1:
                        cY[nt % 2].wait(gg, yfin[g - 1][nt % 2])
                    ins = gg.dma_start(out=y32[ts(nt, 128), :],
                                       in_=dsc[:, nt % 2, :],
                                       accum_op=(ALU.bypass if g == 0
                                                 else ALU.add))
                    yacc[(g, nt)] = cY[nt % 2].tick(ins, 16)
                    assert yacc[(g, nt)] == ybase2[nt % 2] + 16 * (nt // 2 + 1)
                yfin[g] = (cY[0].n, cY[1].n)
            Gb(_gdn)

        f_es.close()

        if dbg & 32:
            def _dbgy(g):
                cY[0].wait(g)
                cY[1].wait(g)
                ins = g.dma_start(out=dout["d_y"], in_=y32)
                cDBG.tick(ins, 16)
            Gb(_dbgy)

        with (nc.sbuf_tensor("yl", [128, 2, H], F32) as yl,
              nc.sbuf_tensor("x2l", [128, 2, H], F32) as x2l,
              nc.sbuf_tensor("z2", [128, 2, H], F32) as z2,
              nc.sbuf_tensor("sq2", [128, 2, H], BF16) as sq2,
              nc.sbuf_tensor("ocol", [128, 2, 6], F32) as ocol,
              nc.sbuf_tensor("o48", [128, 2, 8], F32) as o48,
              nc.sbuf_tensor("ob", [128, 2, H], F32) as ob):
            yload = {}
            xload = {}
            oln = {}
            out_mark = {}
            amark = {}
            a2mark = {}

            for nt in range(MC):
                pb = nt % 2

                def _yl(y, nt=nt, pb=pb):
                    if nt == 0:
                        cT.wait(y)
                        cV.wait(y)
                    cY[0].wait(y)
                    cY[1].wait(y)
                    cV.wait(y, oln.get(nt - 2, 0))
                    ins = y.dma_start(out=yl[:, pb, :],
                                      in_=y32[ts(nt, 128), :])
                    yload[nt] = cYL[pb].tick(ins, 16)
                Yb(_yl)

                def _axl(a, nt=nt, pb=pb):
                    if nt == 0:
                        cV.wait(a)
                    cV.wait(a, oln.get(nt - 2, 0))
                    ins = a.dma_start(out=x2l[:, pb, :],
                                      in_=x2f32[ts(nt, 128), :])
                    xload[nt] = cXL[pb].tick(ins, 16)
                Ab(_axl)

                vt = [0]

                def _vl(v, nt=nt, pb=pb, vt=vt):
                    cYL[pb].wait(v, yload[nt])
                    cXL[pb].wait(v, xload[nt])
                    cA.wait(v, a2mark.get(nt - 2, 0))
                    ins = None
                    for mb in range(4):
                        ins = v.scalar_tensor_tensor(
                            z2[:, pb, ts(mb, 512)],
                            yl[:, pb, ts(mb, 512)], 1.0,
                            x2l[:, pb, ts(mb, 512)], ALU.mult, ALU.add,
                            accum_out=o48[:, pb, mb:mb + 1])
                    v.wait_ge(cV.h, cV.tick(ins))
                    ins = v.tensor_reduce(
                        ocol[:, pb, 0:1].rearrange("p (a o) -> p a o", o=1),
                        o48[:, pb, 0:4].rearrange("p (a f) -> p a f", a=1),
                        AX.X, ALU.add)
                    v.wait_ge(cV.h, cV.tick(ins))
                    ins = v.tensor_scalar_mul(ocol[:, pb, 1:2],
                                              ocol[:, pb, 0:1], 1.0 / H)
                    vt[0] = cV.tick(ins)
                Vb(_vl)

                def _al(s, nt=nt, pb=pb, vt=vt):
                    cV.wait(s, vt[0])
                    ins = None
                    for mb in range(4):
                        ins = s.activation(sq2[:, pb, ts(mb, 512)],
                                           z2[:, pb, ts(mb, 512)], AF.Square,
                                           accum_out=o48[:, pb, 4 + mb:5 + mb])
                    amark[nt] = cA.tick(ins)
                Ab(_al)

                def _vl2(v, nt=nt, pb=pb):
                    cA.wait(v, amark[nt])
                    ins = v.tensor_reduce(
                        ocol[:, pb, 2:3].rearrange("p (a o) -> p a o", o=1),
                        o48[:, pb, 4:8].rearrange("p (a f) -> p a f", a=1),
                        AX.X, ALU.add)
                    v.wait_ge(cV.h, cV.tick(ins))
                    ins = v.tensor_scalar_mul(ocol[:, pb, 2:3],
                                              ocol[:, pb, 2:3], 1.0 / H)
                    cV.tick(ins)
                    ins = v.tensor_tensor(ocol[:, pb, 3:4], ocol[:, pb, 1:2],
                                          ocol[:, pb, 1:2], ALU.mult)
                    v.wait_ge(cV.h, cV.tick(ins))
                    ins = v.tensor_tensor(ocol[:, pb, 3:4], ocol[:, pb, 2:3],
                                          ocol[:, pb, 3:4], ALU.subtract)
                    v.wait_ge(cV.h, cV.tick(ins))
                    ins = v.tensor_scalar_add(ocol[:, pb, 3:4],
                                              ocol[:, pb, 3:4], EPS)
                    cV.tick(ins)
                Vb(_vl2)

                def _al2(s, nt=nt, pb=pb):
                    cV.wait(s)
                    ins = s.activation(ocol[:, pb, 4:5], ocol[:, pb, 3:4],
                                       AF.Sqrt)
                    a2mark[nt] = cA.tick(ins)
                Ab(_al2)

                def _vl3(v, nt=nt, pb=pb):
                    cA.wait(v, a2mark[nt])
                    cOB[pb].wait(v, out_mark.get(nt - 2, 0))
                    ins = v.reciprocal(ocol[:, pb, 5:6], ocol[:, pb, 4:5])
                    v.wait_ge(cV.h, cV.tick(ins))
                    ins = None
                    for mb in range(4):
                        ins = v.tensor_scalar(ob[:, pb, ts(mb, 512)],
                                              z2[:, pb, ts(mb, 512)],
                                              ocol[:, pb, 1:2],
                                              ocol[:, pb, 5:6],
                                              ALU.subtract, ALU.mult)
                    oln[nt] = cV.tick(ins)
                Vb(_vl3)

                def _yo(g, nt=nt, pb=pb):
                    cV.wait(g, oln[nt])
                    ins = g.dma_start(out=out[ts(nt, 128), :], in_=ob[:, pb, :])
                    out_mark[nt] = cOB[pb].tick(ins, 16)
                Gb(_yo)

            def _fin(y):
                cOB[0].wait(y)
                cOB[1].wait(y)
                if dbg:
                    cDBG.wait(y)
            Yb(_fin)

    return nc, dout



_NC_CACHE = {}


def _get_nc(dbg=0, npairs=NPAIR, stages="full"):
    key = (dbg, npairs, stages)
    if key not in _NC_CACHE:
        _NC_CACHE[key] = build(dbg, npairs, stages)
    return _NC_CACHE[key]


def host_prep(inputs):
    f32 = np.float32
    x = np.asarray(inputs["L0_attn_hidden_states"], dtype=f32)
    wq = np.asarray(inputs["L0_attn_Wq"], dtype=f32)
    wk = np.asarray(inputs["L0_attn_Wk"], dtype=f32)
    wv = np.asarray(inputs["L0_attn_Wv"], dtype=f32)
    wo = np.asarray(inputs["L0_attn_Wo"], dtype=f32)
    gate = np.asarray(inputs["L0_attn_gate"], dtype=f32).reshape(1, NH)
    gw = np.asarray(inputs["L0_ffn_gate_w"], dtype=f32)
    mu = np.asarray(inputs["L0_ffn_mask_up_w"], dtype=f32)
    gp = np.asarray(inputs["L0_ffn_gate_proj_w"], dtype=f32)
    up = np.asarray(inputs["L0_ffn_up_proj_w"], dtype=f32)
    dn = np.asarray(inputs["L0_ffn_down_proj_w"], dtype=f32)
    inv = np.asarray(inputs["L0_ffn_inv_map"], dtype=np.int32)

    ac = np.ascontiguousarray
    xT = ac(x.transpose(0, 2, 1))
    shared = dict(
        wqT=ac(wq.T), wkT=ac(wk.T), wvT=ac(wv.T), woT=ac(wo.T),
        gt2=ac(gate), gwT=ac(gw.T), muo=ac(mu),
        gpT=ac(gp.transpose(0, 2, 1)), upT=ac(up.transpose(0, 2, 1)),
        dnT=ac(dn.transpose(0, 2, 1)), invm=ac(inv),
        iot=np.arange(E, dtype=f32).reshape(E, 1),
        idn=np.eye(128, dtype=f32),
    )
    inv_freq = (1.0 / np.float32(10000.0) ** (np.arange(0, H, 2,
                dtype=f32) / np.float32(H))).astype(f32)
    t = np.arange(S, dtype=f32)
    ang = (inv_freq[:, None] * t[None, :]).astype(f32)
    cosk = np.cos(ang).astype(f32)
    sink = np.sin(ang).astype(f32)
    shared["csk"] = ac(np.stack([cosk, sink], axis=1))

    in_maps = []
    for c in range(8):
        b, ch = c // 4, c % 4
        m = dict(shared)
        m["xTb"] = ac(xT[b])
        m["xqT"] = ac(xT[b][:, ch * TOK:(ch + 1) * TOK])
        m["xch"] = ac(x[b, ch * TOK:(ch + 1) * TOK, :])
        m["csq"] = ac(shared["csk"][:, :, ch * TOK:(ch + 1) * TOK])
        in_maps.append(m)
    return in_maps


def kernel(**inputs):
    nc, _ = _get_nc()
    in_maps = host_prep(inputs)
    res = run_bass_kernel_spmd(nc, in_maps, core_ids=list(range(8)))
    outp = np.empty((B, S, H), dtype=np.float32)
    for c in range(8):
        b, ch = c // 4, c % 4
        outp[b, ch * TOK:(ch + 1) * TOK, :] = res.results[c]["out"]
    return outp


# version 17 (final)
# speedup vs baseline: 1.0882x; 1.0690x over previous
import math
import numpy as np
import concourse.bass as bass
import concourse.mybir as mybir
from concourse.bass_utils import run_bass_kernel_spmd

F32 = mybir.dt.float32
BF16 = mybir.dt.bfloat16
I32 = mybir.dt.int32
AF = mybir.ActivationFunctionType
ALU = mybir.AluOpType
AX = mybir.AxisListType
ts = bass.ts

B, S, H = 2, 8192, 2048
NH, HD = 16, 128
NPAIR = 8
TOK = 2048
NSB = S // 512
NQB = TOK // 512
NKT = S // 128
E, G, GS, II = 64, 8, 8, 1408
IC = II // 128
MC = H // 128
SCALE = 1.0 / math.sqrt(HD)
EPS = 1e-5
NEG = -1.0e9


class Ctr:
    def __init__(self, es, nc, name):
        self.h = es.enter_context(nc.semaphore(name))
        self.n = 0

    def tick(self, ins, k=1):
        ins.then_inc(self.h, k)
        self.n += k
        return self.n

    def wait(self, eng, n=None):
        v = self.n if n is None else n
        if v > 0:
            eng.wait_ge(self.h, v)


def build(dbg=0, npairs=NPAIR, stages="full"):
    nc = bass.Bass("TRN2", target_bir_lowering=False, debug=False)
    dt = nc.dram_tensor

    xTb = dt("xTb", [H, S], F32, kind="ExternalInput").ap()
    xqT = dt("xqT", [H, TOK], F32, kind="ExternalInput").ap()
    xch = dt("xch", [TOK, H], F32, kind="ExternalInput").ap()
    wqT = dt("wqT", [H, H], F32, kind="ExternalInput").ap()
    wkT = dt("wkT", [H, H], F32, kind="ExternalInput").ap()
    wvT = dt("wvT", [H, H], F32, kind="ExternalInput").ap()
    woT = dt("woT", [H, H], F32, kind="ExternalInput").ap()
    gt2 = dt("gt2", [1, NH], F32, kind="ExternalInput").ap()
    csk = dt("csk", [1024, 2, S], F32, kind="ExternalInput").ap()
    csq = dt("csq", [1024, 2, TOK], F32, kind="ExternalInput").ap()
    gwT = dt("gwT", [H, E], F32, kind="ExternalInput").ap()
    muo = dt("muo", [G, H, GS], F32, kind="ExternalInput").ap()
    gpT = dt("gpT", [G, H, II], F32, kind="ExternalInput").ap()
    upT = dt("upT", [G, H, II], F32, kind="ExternalInput").ap()
    dnT = dt("dnT", [G, II, H], F32, kind="ExternalInput").ap()
    invm = dt("invm", [G, GS], I32, kind="ExternalInput").ap()
    iot = dt("iot", [E, 1], F32, kind="ExternalInput").ap()
    idn = dt("idn", [128, 128], F32, kind="ExternalInput").ap()

    out = dt("out", [TOK, H], F32, kind="ExternalOutput").ap()

    kfb = dt("kfb", [256, S], BF16, kind="Internal").ap()
    xblk = dt("xblk", [NSB, 128, MC, 512], BF16, kind="Internal").ap()
    xqblk = dt("xqblk", [NQB, 128, MC, 512], BF16, kind="Internal").ap()
    combT = dt("combT", [H, TOK], BF16, kind="Internal").ap()
    x2b16 = dt("x2b16", [TOK, H], BF16, kind="Internal").ap()
    x2f32 = dt("x2f32", [TOK, H], F32, kind="Internal").ap()
    y32 = dt("y32", [TOK, H], F32, kind="Internal").ap()
    sftd = dt("sftd", [G, GS, TOK], BF16, kind="Internal").ap()

    dbg_specs = {
        1: [("d_kt", [128, 2, S]), ("d_va", [128, 2, 64, 130]),
            ("d_mem", [128, 2, 132])],
        2: [("d_qt", [128, 2, TOK]), ("d_mo", [128, 2, 16, 132])],
        4: [("d_comb", [H, TOK])],
        8: [("d_x2", [TOK, H])],
        16: [("d_scl", [128, 16, 8]), ("d_sft", [G, GS, TOK])],
        32: [("d_y", [TOK, H])],
    }
    dout = {}
    for bit, specs in dbg_specs.items():
        if dbg & bit:
            for nm, shp in specs:
                dout[nm] = dt(nm, shp, F32, kind="ExternalOutput").ap()

    from contextlib import ExitStack
    es = ExitStack()
    C = es.enter_context
    with es:
        identf = C(nc.sbuf_tensor("identf", [128, 128], F32))
        ones1f = C(nc.sbuf_tensor("ones1f", [1, 128], F32))
        onescol = C(nc.sbuf_tensor("onescol", [128, 1], F32))
        gcols = C(nc.sbuf_tensor("gcols", [128, 32], F32))
        grow = C(nc.sbuf_tensor("grow", [1, 32], F32))
        iotc = C(nc.sbuf_tensor("iotc", [64, 1], F32))

        pS3 = [C(nc.psum_tensor(f"pS3_{i}", [128, 1536], F32)) for i in range(2)]
        pP = [C(nc.psum_tensor(f"pP{i}", [128, 512], F32)) for i in range(2)]
        pA = [pS3[0][:, 0:512], pS3[0][:, 512:1024], pS3[1][:, 0:512],
              pS3[1][:, 512:1024]]
        VB = [pS3[1][:, 0:512], pS3[1][:, 512:1024],
              pS3[0][:, 1024:1536], pS3[1][:, 1024:1536]]

        cXT = [Ctr(es, nc, f"cXT{i}") for i in range(2)]
        cBXL = [Ctr(es, nc, f"cBXL{i}") for i in range(2)]
        cBXS = [Ctr(es, nc, f"cBXS{i}") for i in range(2)]
        cCS = [Ctr(es, nc, f"cCS{i}") for i in range(2)]
        cWKV = [Ctr(es, nc, f"cWKV{i}") for i in range(2)]
        cWQ = Ctr(es, nc, "cWQ")
        cKFB = [Ctr(es, nc, f"cKFB{i}") for i in range(2)]
        cKFN = [Ctr(es, nc, f"cKFN{i}") for i in range(3)]
        cCB = [Ctr(es, nc, f"cCB{i}") for i in range(2)]
        cWO = Ctr(es, nc, "cWO")
        cCT = [Ctr(es, nc, f"cCT{i}") for i in range(2)]
        cXC = [Ctr(es, nc, f"cXC{i}") for i in range(2)]
        cX2W = Ctr(es, nc, "cX2W")
        cGW = Ctr(es, nc, "cGW")
        cX2T = Ctr(es, nc, "cX2T")
        cSFT = Ctr(es, nc, "cSFT")
        cMU2 = Ctr(es, nc, "cMU2")
        cGU = [Ctr(es, nc, f"cGU{i}") for i in range(2)]
        cDN = Ctr(es, nc, "cDN")
        cYL = [Ctr(es, nc, f"cYL{i}") for i in range(2)]
        cXL = [Ctr(es, nc, f"cXL{i}") for i in range(2)]
        cOB = [Ctr(es, nc, f"cOB{i}") for i in range(2)]
        cY = [Ctr(es, nc, f"cY{i}") for i in range(2)]
        cOUT = Ctr(es, nc, "cOUT")
        cDBG = Ctr(es, nc, "cDBG")
        cG2 = Ctr(es, nc, "cG2")
        cT = Ctr(es, nc, "cT")
        cV = Ctr(es, nc, "cV")
        cA = Ctr(es, nc, "cA")

        block = C(nc.Block())
        Tb, Vb, Ab, Yb, Gb = (block.tensor, block.vector, block.scalar,
                              block.sync, block.gpsimd)

        fr = {"xt0": 0, "xt1": 0, "cs0": 0, "cs1": 0, "pA01": 0, "pA23": 0,
              "pvb0": 0, "pvb1": 0, "pvb2": 0, "pvb3": 0,
              "kf0": 0, "kf1": 0, "wsl": 0,
              "mem": 0, "cb4": [0, 0], "rbev": 0, "pvfree": 0, "epi_done": 0}

        def _g0(g):
            cG2.tick(g.dma_start(out=identf[:, :], in_=idn), 16)
            cG2.tick(g.dma_start(out=iotc[:, :], in_=iot), 16)
            cG2.tick(g.dma_start(out=grow[0:1, 0:16], in_=gt2), 16)
        Gb(_g0)

        def _vones(v):
            v.memset(ones1f[:, :], 1.0)
            cV.tick(v.memset(onescol[:, :], 1.0))
        Vb(_vones)

        def _a0(s):
            cG2.wait(s)
            ins = s.activation(grow[0:1, 0:16], grow[0:1, 0:16], AF.Sigmoid)
            cA.tick(ins)
        Ab(_a0)

        def _v0(v):
            cA.wait(v)
            ins = v.tensor_scalar(grow[0:1, 16:32], grow[0:1, 0:16], -1.0, 1.0,
                                  ALU.mult, ALU.add)
            cV.tick(ins)
        Vb(_v0)

        def _t0(t):
            cV.wait(t)
            ins = t.matmul(pS3[0][:, 0:32], ones1f[0:1, :], grow[0:1, 0:32],
                           start=True, stop=True)
            cT.tick(ins)
        Tb(_t0)

        def _v1(v):
            cT.wait(v)
            ins = v.tensor_copy(gcols[:, :], pS3[0][:, 0:32])
            fr["mem"] = cV.tick(ins)
            fr["pA01"] = fr["mem"]
        Vb(_v1)

        p1 = ExitStack()

        class _NoAlloc:
            def __getitem__(self, k):
                raise RuntimeError("phase-1 tensor used in p2-only build")

        def P1(*args, **kw):
            if stages == "p2":
                return _NoAlloc()
            return p1.enter_context(nc.sbuf_tensor(*args, **kw))

        stg = [P1(f"stg{i}", [128, MC // 4, 512], BF16) for i in range(2)]
        kt = P1("kt", [128, 2, S], BF16)
        va = P1("va", [128, 2, 64, 130], BF16)
        qt = P1("qt", [128, 2, TOK], BF16)

        mem = P1("mem", [128, 2, 132], BF16)
        xt = [P1(f"xt{i}", [128, MC, 512], BF16) for i in range(2)]
        csb = [P1(f"csb{i}", [128, 2, 512], F32) for i in range(2)]
        wslkv = [P1(f"wslkv{i}", [128, MC, 2, 256], BF16) for i in range(2)]
        wslq = P1("wslq", [128, MC, 256], BF16)
        kf = [P1(f"kf{i}", [128, 2, 512], BF16) for i in range(2)]
        kfn = P1("kfn", [128, 12, 256], BF16)
        qft = P1("qft", [128, 2, TOK], BF16)
        rs = [P1(f"rs{i}", [128, 512], F32) for i in range(2)]
        esc = [P1(f"esc{i}", [128, 2, 512], BF16) for i in range(3)]
        pt = [P1(f"pt{i}", [128, 1536], BF16) for i in range(2)]
        cbt = P1("cbt", [128, 2, 512], BF16)
        sacc = P1("sacc", [128, 512], F32)
        rb = P1("rb", [128, 2, 512], BF16)

        if stages != "p2":
            def _vone(v):
                ins = v.memset(va[:, :, :, 128:130], 1.0)
                cV.tick(ins)
            Vb(_vone)
            fr["va1"] = cV.n

        xblk_w = {}
        bjc = [0]
        bxl_m = {}
        stg_rd = {}

        def emit_bounce(kind, bi, tgate=None):
            for qtr in range(4):
                j = bjc[0]
                bjc[0] += 1
                par = j % 2
                tg = tgate[qtr] if tgate is not None else 0

                def _gb(g, kind=kind, bi=bi, qtr=qtr, par=par, j=j, tg=tg):
                    cBXS[par].wait(g)
                    cT.wait(g, tg)
                    src = xTb if kind == "k" else xqT
                    ins = g.dma_start(
                        out=stg[par][:, :, :],
                        in_=src.rearrange("(c p) s -> p c s", p=128)
                        [:, ts(qtr, 4), ts(bi, 512)])
                    bxl_m[j] = cBXL[par].tick(ins, 16)
                Gb(_gb)

                def _yb(y, kind=kind, bi=bi, qtr=qtr, par=par):
                    cBXL[par].wait(y)
                    dst = xblk[bi] if kind == "k" else xqblk[bi]
                    ins = y.dma_start(out=dst[:, ts(qtr, 4), :],
                                      in_=stg[par][:, :, :])
                    xblk_w.setdefault((kind, bi), []).append(
                        (par, cBXS[par].tick(ins, 16)))
                Yb(_yb)

        def emit_rope(v, buf, dstA, dstB, bk0=None, bk1=None, mark="pA01"):
            if bk0 is None:
                bk0, bk1 = pA[0], pA[1]
            v.tensor_tensor(rs[0][:, :], bk0[:, 0:512], csb[buf][:, 0, :],
                            ALU.mult)
            ins = v.tensor_tensor(rs[1][:, :], bk1[:, 0:512],
                                  csb[buf][:, 1, :], ALU.mult)
            v.wait_ge(cV.h, cV.tick(ins))
            ins = v.tensor_tensor(dstA, rs[0][:, :], rs[1][:, :], ALU.subtract)
            v.wait_ge(cV.h, cV.tick(ins))
            v.tensor_tensor(rs[0][:, :], bk1[:, 0:512], csb[buf][:, 0, :],
                            ALU.mult)
            ins = v.tensor_tensor(rs[1][:, :], bk0[:, 0:512],
                                  csb[buf][:, 1, :], ALU.mult)
            fr[mark] = cV.tick(ins)
            fr[f"cs{buf}"] = fr[mark]
            v.wait_ge(cV.h, fr[mark])
            ins = v.tensor_tensor(dstB, rs[0][:, :], rs[1][:, :], ALU.add)
            v.wait_ge(cV.h, cV.tick(ins))

        wsl_used = {}

        def emit_gw(hp, which="both"):
            def _gw(g, hp=hp, which=which):
                if which in ("both", "kv"):
                    cT.wait(g, wsl_used.get(hp - 2, 0))
                    for wi, w in ((0, wkT), (1, wvT)):
                        for d in range(2):
                            col = hp * 128 + d * 1024
                            ins = g.dma_start(
                                out=wslkv[hp % 2][:, :, wi, ts(d, 128)],
                                in_=w[:, col:col + 128].rearrange(
                                    "(c p) d -> p c d", p=128))
                            cWKV[hp % 2].tick(ins, 16)
                if which in ("both", "q"):
                    cT.wait(g, wsl_used.get(hp - 1, 0))
                    for d in range(2):
                        col = hp * 128 + d * 1024
                        ins = g.dma_start(
                            out=wslq[:, :, ts(d, 128)],
                            in_=wqT[:, col:col + 128].rearrange(
                                "(c p) d -> p c d", p=128))
                        cWQ.tick(ins, 16)
            Gb(_gw)

        def emit_kv_pair(hp):
            mm_used = {}
            kv_rdy = {}
            for sb in range(NSB):
                buf = sb % 2
                kmm = [0, 0]
                vmm = [0] * 4
                elu_mx = [0]
                kfn_rdy = [0]

                if sb < 2 and hp in prefetched_pairs:
                    pass
                else:
                    def _y(y, hp=hp, sb=sb, buf=buf):
                        if hp > 0:
                            cT.wait(y, fr[f"xt{buf}"])
                            for bpar, bval in xblk_w[("k", sb)]:
                                cBXS[bpar].wait(y, bval)
                            ins = y.dma_start(out=xt[buf][:, :, :],
                                              in_=xblk[sb])
                            cXT[buf].tick(ins, 16)
                        cV.wait(y, fr[f"cs{buf}"])
                        ins = y.dma_start(out=csb[buf][:, :, :],
                                          in_=csk[ts(hp, 128), :, ts(sb, 512)])
                        cCS[buf].tick(ins, 16)
                    Yb(_y)

                if hp == 0:
                    tb = cT.n
                    tgate = [stg_rd.get(4 * sb - 2, 0),
                             stg_rd.get(4 * sb - 1, 0), tb + 1, tb + 2]
                    emit_bounce("k", sb, tgate)
                    if sb == 8:
                        emit_gw(1, "kv")

                cv_base = [0]

                def _t(t, hp=hp, sb=sb, buf=buf, kmm=kmm, vmm=vmm,
                       cv_base=cv_base):
                    cv_base[0] = cV.n
                    cCS[buf].wait(t)
                    cWKV[hp % 2].wait(t)
                    if sb == 0:
                        cV.wait(t, fr["epi_done"])
                    cV.wait(t, fr["pA01"])
                    cV.wait(t, fr["pA23"])
                    for st in range(4):
                        cV.wait(t, fr[f"pvb{st}"])
                    if hp == 0:
                        for qtr in range(4):
                            j = 4 * sb + qtr
                            cBXL[j % 2].wait(t, bxl_m[j])
                            ins = None
                            for q in range(4):
                                mc = 4 * qtr + q
                                for d in range(2):
                                    ins = t.matmul(
                                        pA[d][:, 0:512],
                                        wslkv[0][:, mc, 0, ts(d, 128)],
                                        stg[j % 2][:, q, :],
                                        start=(mc == 0), stop=(mc == MC - 1))
                                for st in range(4):
                                    ins = t.matmul(
                                        VB[st][:, 0:256],
                                        stg[j % 2][:, q, ts(st, 128)],
                                        wslkv[0][:, mc, 1, :],
                                        start=(mc == 0), stop=(mc == MC - 1))
                            stg_rd[j] = cT.tick(ins)
                        last = stg_rd[4 * sb + 3]
                        for d in range(2):
                            kmm[d] = last
                        for st in range(4):
                            vmm[st] = last
                    else:
                        cXT[buf].wait(t)
                        for d in range(2):
                            ins = None
                            for mc in range(MC):
                                ins = t.matmul(
                                    pA[d][:, 0:512],
                                    wslkv[hp % 2][:, mc, 0, ts(d, 128)],
                                    xt[buf][:, mc, :],
                                    start=(mc == 0), stop=(mc == MC - 1))
                            kmm[d] = cT.tick(ins)
                        for st in range(4):
                            ins = None
                            for mc in range(MC):
                                ins = t.matmul(
                                    VB[st][:, 0:256],
                                    xt[buf][:, mc, ts(st, 128)],
                                    wslkv[hp % 2][:, mc, 1, :],
                                    start=(mc == 0), stop=(mc == MC - 1))
                            vmm[st] = cT.tick(ins)
                        fr[f"xt{buf}"] = cT.n
                Tb(_t)

                if hp == 0:
                    assert stg_rd[4 * sb] == tgate[2] and \
                        stg_rd[4 * sb + 1] == tgate[3]

                def _v(v, hp=hp, sb=sb, buf=buf, kmm=kmm, vmm=vmm,
                       elu_mx=elu_mx, cv_base=cv_base):
                    sl = ts(sb, 512)
                    if dbg & 1 and hp == 1 and sb == 0:
                        cDBG.wait(v)
                    cT.wait(v, kmm[1])
                    emit_rope(v, buf, kt[:, 0, sl], kt[:, 1, sl])
                    assert fr["pA01"] == cv_base[0] + 3
                    cKFB[sb % 2].wait(v, fr[f"kf{sb % 2}"])
                    ins = None
                    for h in range(2):
                        v.tensor_scalar_min(esc[0][:, h, :], kt[:, h, sl], 0.0)
                        ins = v.tensor_scalar_max(esc[1][:, h, :], kt[:, h, sl],
                                                  0.0)
                    elu_mx[0] = cV.tick(ins)
                    for st in range(4):
                        cT.wait(v, vmm[st])
                        kti = sb * 4 + st
                        pv = VB[st]
                        v.tensor_copy(va[:, 0, kti, 0:128], pv[:, 0:128])
                        ins = v.tensor_copy(va[:, 1, kti, 0:128], pv[:, 128:256])
                        fr[f"pvb{st}"] = cV.tick(ins)
                Vb(_v)

                def _a(s, sb=sb, elu_mx=elu_mx):
                    cV.wait(s, elu_mx[0])
                    ins = s.activation(esc[2][:, :, :], esc[0][:, :, :], AF.Exp)
                    cA.tick(ins)
                Ab(_a)

                def _vf(v, sb=sb):
                    cA.wait(v)
                    ins = None
                    for h in range(2):
                        ins = v.tensor_tensor(kf[sb % 2][:, h, :],
                                              esc[2][:, h, :], esc[1][:, h, :],
                                              ALU.add)
                    cV.tick(ins)
                Vb(_vf)

                def _y2(g, hp=hp, sb=sb):
                    cV.wait(g)
                    ins = None
                    for h in range(2):
                        ins = g.dma_start(out=kfb[ts(h, 128), ts(sb, 512)],
                                          in_=kf[sb % 2][:, h, :])
                        cKFB[sb % 2].tick(ins, 16)
                    fr[f"kf{sb % 2}"] = cKFB[sb % 2].n
                Gb(_y2)

                def _al(s, sb=sb, rdy=kv_rdy, mm_used=mm_used):
                    cKFB[0].wait(s)
                    cKFB[1].wait(s)
                    if sb >= 3:
                        cT.wait(s, mm_used.get(sb - 3, 0))
                    ins = None
                    for st in range(4):
                        kti = sb * 4 + st
                        slot = (sb % 3) * 4 + st
                        ins = s.dma_start_transpose(
                            kfn[:, slot, :],
                            kfb[0:256, ts(kti, 128)])
                        cKFN[sb % 3].tick(ins, 16)
                    rdy[sb] = cKFN[sb % 3].n
                Ab(_al)

                def _tm(t, hp=hp, sb=sb - 2, kfn_rdy=None,
                        mm_used=mm_used, rdy=kv_rdy):
                    cKFN[sb % 3].wait(t, rdy[sb])
                    if sb == 0:
                        cV.wait(t, max(fr["mem"], fr["va1"]))
                    ins = None
                    for h in range(2):
                        reg = pP[h][:, 0:129]
                        for st in range(4):
                            kti = sb * 4 + st
                            slot = (sb % 3) * 4 + st
                            ins = t.matmul(reg, kfn[:, slot, ts(h, 128)],
                                           va[:, h, kti, 0:129],
                                           start=(sb == 0 and st == 0),
                                           stop=(sb == NSB - 1 and st == 3))
                        mm_used[sb] = cT.tick(ins)
                if sb >= 2:
                    Tb(_tm)

            kv_state[hp] = (mm_used, kv_rdy)

        def emit_kv_tail(hp):
            mm_used, kv_rdy = kv_state[hp]

            def _tml(t, mm_used=mm_used, rdy=kv_rdy):
                for sb in (NSB - 2, NSB - 1):
                    cKFN[sb % 3].wait(t, rdy[sb])
                    ins = None
                    for h in range(2):
                        reg = pP[h][:, 0:129]
                        for st in range(4):
                            kti = sb * 4 + st
                            slot = (sb % 3) * 4 + st
                            ins = t.matmul(reg, kfn[:, slot, ts(h, 128)],
                                           va[:, h, kti, 0:129],
                                           start=False,
                                           stop=(sb == NSB - 1 and st == 3))
                        mm_used[sb] = cT.tick(ins)
            Tb(_tml)

            def _vm(v):
                cT.wait(v)
                v.tensor_copy(mem[:, 0, 0:129], pP[0][:, 0:129])
                ins = v.tensor_copy(mem[:, 1, 0:129], pP[1][:, 0:129])
                fr["mem"] = cV.tick(ins)
            Vb(_vm)

            if dbg & 1 and hp == 0:
                def _dbgk(g):
                    cV.wait(g)
                    cDBG.tick(g.dma_start(out=dout["d_kt"], in_=kt[:, :, :]), 16)
                    cDBG.tick(g.dma_start(out=dout["d_va"], in_=va[:, :, :, :]), 16)
                    cDBG.tick(g.dma_start(out=dout["d_mem"][:, :, 0:129], in_=mem[:, :, 0:129]), 16)
                Gb(_dbgk)

        def emit_q_pair(hp):
            for qb in range(NQB):
                if hp == 0:
                    emit_bounce("q", qb)
                buf = qb % 2
                kmm = [0, 0]
                elu_mx = [0]

                def _y(y, hp=hp, qb=qb, buf=buf):
                    cT.wait(y, fr[f"xt{buf}"])
                    for bpar, bval in xblk_w[("q", qb)]:
                        cBXS[bpar].wait(y, bval)
                    ins = y.dma_start(out=xt[buf][:, :, :], in_=xqblk[qb])
                    cXT[buf].tick(ins, 16)
                    cV.wait(y, fr[f"cs{buf}"])
                    ins = y.dma_start(out=csb[buf][:, :, :],
                                      in_=csq[ts(hp, 128), :, ts(qb, 512)])
                    cCS[buf].tick(ins, 16)
                Yb(_y)

                def _t(t, hp=hp, qb=qb, buf=buf, kmm=kmm):
                    cXT[buf].wait(t)
                    cCS[buf].wait(t)
                    cWQ.wait(t)
                    if qb % 2 == 0:
                        cV.wait(t, fr["pA01"])
                    else:
                        cV.wait(t, max(fr["pA23"], fr["pvb0"], fr["pvb1"]))
                    qbk = (pA[0], pA[1]) if qb % 2 == 0 else (pA[2], pA[3])
                    for d in range(2):
                        ins = None
                        for mc in range(MC):
                            ins = t.matmul(qbk[d][:, 0:512],
                                           wslq[:, mc, ts(d, 128)],
                                           xt[buf][:, mc, :],
                                           start=(mc == 0), stop=(mc == MC - 1))
                        kmm[d] = cT.tick(ins)
                    fr[f"xt{buf}"] = cT.n
                Tb(_t)

                def _v(v, hp=hp, qb=qb, buf=buf, kmm=kmm, elu_mx=elu_mx):
                    sl = ts(qb, 512)
                    cT.wait(v, kmm[1])
                    qbk = (pA[0], pA[1]) if qb % 2 == 0 else (pA[2], pA[3])
                    emit_rope(v, buf, qt[:, 0, sl], qt[:, 1, sl],
                              qbk[0], qbk[1],
                              "pA01" if qb % 2 == 0 else "pA23")
                    ins = None
                    for h in range(2):
                        v.tensor_scalar_min(esc[0][:, h, :], qt[:, h, sl], 0.0)
                        ins = v.tensor_scalar_max(esc[1][:, h, :], qt[:, h, sl],
                                                  0.0)
                    elu_mx[0] = cV.tick(ins)
                Vb(_v)

                def _a(s, qb=qb, elu_mx=elu_mx):
                    cV.wait(s, elu_mx[0])
                    ins = s.activation(esc[2][:, :, :], esc[0][:, :, :], AF.Exp)
                    cA.tick(ins)
                Ab(_a)

                def _vq(v, qb=qb):
                    cA.wait(v)
                    ins = None
                    for h in range(2):
                        ins = v.tensor_tensor(qft[:, h, ts(qb, 512)],
                                              esc[2][:, h, :],
                                              esc[1][:, h, :], ALU.add)
                    cV.tick(ins)
                Vb(_vq)

            if dbg & 2 and hp == 0:
                def _dbgq(g):
                    cA.wait(g)
                    cV.wait(g)
                    cDBG.tick(g.dma_start(out=dout["d_qt"], in_=qt[:, :, :]), 16)
                Gb(_dbgq)

        def emit_attn_pair(hp):
            GRP = [3] * 21 + [1]
            NKG3 = len(GRP)
            goff = [0]
            for gsz in GRP:
                goff.append(goff[-1] + gsz)
            for h in range(2):
                hg = hp + 8 * h
                j0 = h * 1024 + hp * 128
                for qb in range(NQB):
                    sl = ts(qb, 512)
                    sc_done = {}
                    pv_done = {}
                    mm_marks = {}
                    cA_base = [0]
                    cV_base = [0]

                    def _t(t, hp=hp, h=h, qb=qb, sl=sl, cA_base=cA_base,
                           cV_base=cV_base, sc_done=sc_done, pv_done=pv_done,
                           mm_marks=mm_marks):
                        cA_base[0] = cA.n
                        cV_base[0] = cV.n
                        vb = cV_base[0]
                        for kg in range(NKG3):
                            ps = pS3[kg % 2]
                            cA.wait(t, cA_base[0] + max(0, kg - 1))
                            if kg == 0:
                                cV.wait(t, max(fr["rbev"], fr["pA01"],
                                               fr["pvb2"]))
                            elif kg == 1:
                                cV.wait(t, max(fr["pvb0"], fr["pvb1"],
                                               fr["pvb3"], fr["pA23"]))
                            ins = None
                            for j in range(GRP[kg]):
                                kti = goff[kg] + j
                                ins = t.matmul(ps[:, ts(j, 512)],
                                               kt[:, h, ts(kti, 128)],
                                               qt[:, h, sl],
                                               start=True, stop=True)
                            sc_done[kg] = cT.tick(ins)
                            if kg >= 1:
                                kgp = kg - 1
                                cA.wait(t, cA_base[0] + kgp + 1)
                                if kgp == 0:
                                    cV.wait(t, max(fr["pvfree"], fr["mem"]))
                                ins = None
                                for j in range(GRP[kgp]):
                                    kti = goff[kgp] + j
                                    ins = t.matmul(
                                        pP[0][:, 0:512],
                                        va[:, h, kti, 0:128],
                                        pt[kgp % 2][:, ts(j, 512)],
                                        start=(kti == 0),
                                        stop=(kti == NKT - 1))
                                pv_done[kgp] = cT.tick(ins)
                        kgp = NKG3 - 1
                        cA.wait(t, cA_base[0] + kgp + 1)
                        ins = t.matmul(pP[0][:, 0:512],
                                       va[:, h, goff[kgp], 0:128],
                                       pt[kgp % 2][:, 0:512],
                                       start=False, stop=True)
                        pv_done[kgp] = cT.tick(ins)
                        cV.wait(t, vb + 65)
                        cA.wait(t, cA_base[0] + NKG3 - 1)
                        ins = t.matmul(pS3[0][0:1, 0:512], onescol[:, 0:1],
                                       sacc[:, :], start=True, stop=True)
                        mm_marks["srow"] = cT.tick(ins)
                        ins = t.matmul(pS3[1][0:1, 0:512], mem[:, h, 128:129],
                                       qft[:, h, sl], start=True, stop=True)
                        mm_marks["lrow"] = cT.tick(ins)
                        cV.wait(t, max(fr["epi_done"], fr["mem"]))
                        ins = t.matmul(pP[1][:, 0:512], mem[:, h, 0:128],
                                       qft[:, h, sl], start=True, stop=True)
                        mm_marks["mo"] = cT.tick(ins)
                        cV.wait(t, vb + 66)
                        ins = t.matmul(pS3[0][:, 512:1024], ones1f[0:1, :],
                                       rs[0][0:1, 0:512], start=True, stop=True)
                        ins = t.matmul(pS3[1][:, 512:1024], ones1f[0:1, :],
                                       rs[1][0:1, 0:512], start=True,
                                       stop=True)
                        mm_marks["bc"] = cT.tick(ins)
                    Tb(_t)

                    def _ae(s, hp=hp, h=h, qb=qb, cA_base=cA_base,
                            cV_base=cV_base, sc_done=sc_done,
                            pv_done=pv_done):
                        for kg in range(NKG3):
                            cT.wait(s, sc_done[kg])
                            if kg >= 2:
                                cT.wait(s, pv_done[kg - 2])
                                cV.wait(s, cV_base[0] + 1 + 3 * (kg - 1))
                            ps = pS3[kg % 2]
                            w = GRP[kg] * 512
                            ins = s.activation(pt[kg % 2][:, 0:w], ps[:, 0:w],
                                               AF.Exp, scale=SCALE)
                            got = cA.tick(ins)
                            assert got == cA_base[0] + kg + 1
                    Ab(_ae)

                    def _v(v, hp=hp, h=h, qb=qb, hg=hg, cA_base=cA_base,
                           cV_base=cV_base, sc_done=sc_done, pv_done=pv_done,
                           mm_marks=mm_marks):
                        vb = cV_base[0]
                        tmp = esc[0][:, 0, :]
                        ins = v.memset(sacc[:, :], 0.0)
                        got = cV.tick(ins)
                        assert got == vb + 1
                        for kg in range(NKG3):
                            cA.wait(v, cA_base[0] + kg + 1)
                            pb = pt[kg % 2]
                            if GRP[kg] == 3:
                                v.wait_ge(cV.h, cV.n)
                                ins = v.tensor_tensor(tmp, pb[:, 0:512],
                                                      pb[:, 512:1024], ALU.add)
                                v.wait_ge(cV.h, cV.tick(ins))
                                ins = v.tensor_tensor(tmp, tmp,
                                                      pb[:, 1024:1536], ALU.add)
                                v.wait_ge(cV.h, cV.tick(ins))
                                ins = v.tensor_tensor(sacc[:, :], sacc[:, :],
                                                      tmp, ALU.add)
                                cV.tick(ins)
                            else:
                                v.wait_ge(cV.h, cV.n)
                                ins = v.tensor_tensor(sacc[:, :], sacc[:, :],
                                                      pb[:, 0:512], ALU.add)
                                cV.tick(ins)
                        assert cV.n == vb + 65
                        cT.wait(v, mm_marks["srow"])
                        cT.wait(v, mm_marks["lrow"])
                        v.reciprocal(rs[0][0:1, 0:512], pS3[0][0:1, 0:512])
                        ins = v.reciprocal(rs[1][0:1, 0:512],
                                           pS3[1][0:1, 0:512])
                        got = cV.tick(ins)
                        assert got == vb + 66
                        cT.wait(v, mm_marks["bc"])
                        v.tensor_copy(rb[:, 0, :], pS3[0][:, 512:1024])
                        ins = v.tensor_copy(rb[:, 1, :], pS3[1][:, 512:1024])
                        got = cV.tick(ins)
                        fr["rbev"] = got
                        assert got == vb + 67
                        cT.wait(v, pv_done[NKG3 - 1])
                        cT.wait(v, mm_marks["mo"])
                        cCB[qb % 2].wait(v, fr["cb4"][qb % 2])
                        v.wait_ge(cV.h, vb + 67)
                        v.scalar_tensor_tensor(rs[0][:, :], pP[0][:, 0:512],
                                               gcols[:, 16 + hg:17 + hg],
                                               rb[:, 0, :], ALU.mult, ALU.mult)
                        ins = v.scalar_tensor_tensor(rs[1][:, :],
                                                     pP[1][:, 0:512],
                                                     gcols[:, hg:hg + 1],
                                                     rb[:, 1, :], ALU.mult,
                                                     ALU.mult)
                        got = cV.tick(ins)
                        assert got == vb + 68
                        v.wait_ge(cV.h, vb + 68)
                        ins = v.tensor_tensor(cbt[:, qb % 2, :], rs[0][:, :],
                                              rs[1][:, :], ALU.add)
                        got = cV.tick(ins)
                        fr["pvfree"] = got
                        fr["epi_done"] = got
                        assert got == vb + 69
                    Vb(_v)

                    def _y(y, hp=hp, h=h, qb=qb, j0=j0):
                        cV.wait(y)
                        ins = y.dma_start(
                            out=combT[j0:j0 + 128, ts(qb, 512)],
                            in_=cbt[:, qb % 2, :])
                        fr["cb4"][qb % 2] = cCB[qb % 2].tick(ins, 16)
                    Yb(_y)

        kv_state = {}
        prefetched_pairs = set()

        def emit_xt_prefetch(hp):
            prefetched_pairs.add(hp)

            def _yp(y, hp=hp):
                for sb in (0, 1):
                    cT.wait(y, fr[f"xt{sb}"])
                    for bpar, bval in xblk_w[("k", sb)]:
                        cBXS[bpar].wait(y, bval)
                    ins = y.dma_start(out=xt[sb][:, :, :], in_=xblk[sb])
                    cXT[sb].tick(ins, 16)
                    cV.wait(y, fr[f"cs{sb}"])
                    ins = y.dma_start(out=csb[sb][:, :, :],
                                      in_=csk[ts(hp, 128), :, ts(sb, 512)])
                    cCS[sb].tick(ins, 16)
            Yb(_yp)

        if stages != "p2":
            emit_gw(0)
        for hp in range(npairs if stages != "p2" else 0):
            emit_kv_pair(hp)
            if stages == "kv":
                emit_kv_tail(hp)
                break
            emit_q_pair(hp)
            emit_kv_tail(hp)
            wsl_used[hp] = cT.n
            if hp + 1 < npairs:
                emit_gw(hp + 1, "q" if hp == 0 else "both")
            if stages == "kvq":
                break
            if hp + 1 < npairs:
                emit_xt_prefetch(hp + 1)
            emit_attn_pair(hp)

        if stages in ("kv", "kvq"):
            def _fin(g):
                cV.wait(g)
                cA.wait(g)
                cKFB[0].wait(g)
                cKFB[1].wait(g)
                cDBG.wait(g)
                ins = g.dma_start(out=out, in_=xch)
                cOUT.tick(ins, 16)
                cOUT.wait(g)
            Gb(_fin)
            p1.close()
            return nc, dout

        if dbg & 4:
            def _dbgc(g):
                cCB[0].wait(g)
                cCB[1].wait(g)
                for hp in range(npairs):
                    for h in range(2):
                        j0 = h * 1024 + hp * 128
                        ins = g.dma_start(out=dout["d_comb"][j0:j0 + 128, :],
                                          in_=combT[j0:j0 + 128, :])
                        cDBG.tick(ins, 16)
            Gb(_dbgc)

        if stages == "attn":
            def _fin(g):
                cCB[0].wait(g)
                cCB[1].wait(g)
                cDBG.wait(g)
                ins = g.dma_start(out=out, in_=xch)
                cOUT.tick(ins, 16)
                cOUT.wait(g)
            Gb(_fin)
            p1.close()
            return nc, dout

        p1.close()

        def emit_ln(nt, src_psum, xcol_ap, zs, s48, lncol, sqscr, x2f_ap,
                    x2b_ap, done_dict, war_ctr=None, war_val=0):
            vtick = [0]

            def _v(v):
                ins = None
                for mb in range(4):
                    ins = v.scalar_tensor_tensor(
                        zs[:, ts(mb, 512)], src_psum[mb][:, :], 1.0,
                        xcol_ap[:, ts(mb, 512)], ALU.mult, ALU.add,
                        accum_out=s48[:, mb:mb + 1])
                v.wait_ge(cV.h, cV.tick(ins))
                ins = v.tensor_reduce(
                    lncol[:, 0:1].rearrange("p (a o) -> p a o", o=1),
                    s48[:, 0:4].rearrange("p (a f) -> p a f", a=1),
                    AX.X, ALU.add)
                v.wait_ge(cV.h, cV.tick(ins))
                ins = v.tensor_scalar_mul(lncol[:, 1:2], lncol[:, 0:1], 1.0 / H)
                vtick[0] = cV.tick(ins)
            Vb(_v)

            def _aq(s):
                cV.wait(s, vtick[0])
                for mb in range(4):
                    ins = s.activation(sqscr[:, ts(mb, 512)],
                                       zs[:, ts(mb, 512)], AF.Square,
                                       accum_out=s48[:, 4 + mb:5 + mb])
                cA.tick(ins)
            Ab(_aq)

            def _v2(v):
                cA.wait(v)
                ins = v.tensor_reduce(
                    lncol[:, 2:3].rearrange("p (a o) -> p a o", o=1),
                    s48[:, 4:8].rearrange("p (a f) -> p a f", a=1),
                    AX.X, ALU.add)
                v.wait_ge(cV.h, cV.tick(ins))
                ins = v.tensor_scalar_mul(lncol[:, 2:3], lncol[:, 2:3], 1.0 / H)
                w = cV.tick(ins)
                ins = v.tensor_tensor(lncol[:, 3:4], lncol[:, 1:2],
                                      lncol[:, 1:2], ALU.mult)
                v.wait_ge(cV.h, cV.tick(ins))
                ins = v.tensor_tensor(lncol[:, 3:4], lncol[:, 2:3],
                                      lncol[:, 3:4], ALU.subtract)
                v.wait_ge(cV.h, cV.tick(ins))
                ins = v.tensor_scalar_add(lncol[:, 3:4], lncol[:, 3:4], EPS)
                cV.tick(ins)
            Vb(_v2)

            def _a2(s):
                cV.wait(s)
                ins = s.activation(lncol[:, 4:5], lncol[:, 3:4], AF.Sqrt)
                cA.tick(ins)
            Ab(_a2)

            def _v3(v):
                cA.wait(v)
                if war_ctr is not None:
                    war_ctr.wait(v, war_val)
                ins = v.reciprocal(lncol[:, 5:6], lncol[:, 4:5])
                v.wait_ge(cV.h, cV.tick(ins))
                ins = None
                for mb in range(4):
                    ins = v.tensor_scalar(x2f_ap[:, ts(mb, 512)],
                                          zs[:, ts(mb, 512)],
                                          lncol[:, 1:2], lncol[:, 5:6],
                                          ALU.subtract, ALU.mult)
                if x2b_ap is not None:
                    v.wait_ge(cV.h, cV.tick(ins))
                    ins = v.tensor_copy(x2b_ap[:, :], x2f_ap[:, :])
                done_dict[nt] = cV.tick(ins)
            Vb(_v3)

        w_es = ExitStack()
        wo_sb = w_es.enter_context(nc.sbuf_tensor("wo_sb", [128, MC, H], BF16))
        ct = w_es.enter_context(nc.sbuf_tensor("ct", [128, 2, MC, 512], BF16))
        xc = w_es.enter_context(nc.sbuf_tensor("xc", [128, 2, H], F32))
        zs = w_es.enter_context(nc.sbuf_tensor("zs", [128, H], F32))
        sqs = w_es.enter_context(nc.sbuf_tensor("sqs", [128, H], BF16))
        x2f = w_es.enter_context(nc.sbuf_tensor("x2f", [128, H], F32))
        x2b = w_es.enter_context(nc.sbuf_tensor("x2b", [128, H], BF16))
        lncol = w_es.enter_context(nc.sbuf_tensor("lncol", [128, 6], F32))
        s48 = w_es.enter_context(nc.sbuf_tensor("s48", [128, 8], F32))

        def _gwo(g):
            cT.wait(g)
            cV.wait(g)
            cA.wait(g)
            cCB[0].wait(g)
            cCB[1].wait(g)
            for mc in range(MC):
                ins = g.dma_start(out=wo_sb[:, mc, :], in_=woT[ts(mc, 128), :])
                cWO.tick(ins, 16)
        Gb(_gwo)

        ct_ready = {}
        wo_done = {}
        ln1_done = {}
        xc_ready = {}
        x2w_mark = {}

        for nb in range(NQB):
            def _yc(y, nb=nb):
                if nb == 0:
                    cCB[0].wait(y)
                    cCB[1].wait(y)
                    cT.wait(y)
                    cV.wait(y)
                    cA.wait(y)
                if nb >= 2:
                    cT.wait(y, wo_done.get((nb - 2, 3, 3), 0))
                ins = None
                for jc in range(MC):
                    ins = y.dma_start(
                        out=ct[:, nb % 2, jc, :],
                        in_=combT[ts(jc, 128), ts(nb, 512)])
                    cCT[nb % 2].tick(ins, 16)
                ct_ready[nb] = cCT[nb % 2].n
            Yb(_yc)

            for nt4 in range(4):
                nt = nb * 4 + nt4

                def _yx(y, nb=nb, nt=nt):
                    cV.wait(y, ln1_done.get(nt - 2, 0))
                    ins = y.dma_start(out=xc[:, nt % 2, :],
                                      in_=xch[ts(nt, 128), :])
                    xc_ready[nt] = cXC[nt % 2].tick(ins, 16)
                Yb(_yx)

                wb = [pS3[nt % 2][:, 0:512], pS3[nt % 2][:, 512:1024],
                      pS3[nt % 2][:, 1024:1536], pP[nt % 2][:, 0:512]]

                def _tw(t, nb=nb, nt=nt, nt4=nt4, wb=wb):
                    cCT[nb % 2].wait(t, ct_ready[nb])
                    cWO.wait(t)
                    if nt <= 1:
                        cV.wait(t)
                    cV.wait(t, ln1_done.get(nt - 2, 0))
                    for mb in range(4):
                        ins = None
                        for jc in range(MC):
                            ins = t.matmul(wb[mb][:, :],
                                           ct[:, nb % 2, jc, ts(nt4, 128)],
                                           wo_sb[:, jc, ts(mb, 512)],
                                           start=(jc == 0), stop=(jc == MC - 1))
                        wo_done[(nb, nt4, mb)] = cT.tick(ins)
                Tb(_tw)

                def _vpre(v, nb=nb, nt=nt, nt4=nt4):
                    cT.wait(v, wo_done[(nb, nt4, 3)])
                    cXC[nt % 2].wait(v, xc_ready[nt])
                Vb(_vpre)

                emit_ln(nt, wb, xc[:, nt % 2, :], zs, s48, lncol, sqs,
                        x2f, x2b, ln1_done, war_ctr=cX2W,
                        war_val=x2w_mark.get(nt - 1, 0))

                def _yw(g, nt=nt):
                    cV.wait(g, ln1_done[nt])
                    ins = g.dma_start(out=x2f32[ts(nt, 128), :], in_=x2f[:, :])
                    cX2W.tick(ins, 16)
                    ins = g.dma_start(out=x2b16[ts(nt, 128), :], in_=x2b[:, :])
                    x2w_mark[nt] = cX2W.tick(ins, 16)
                Gb(_yw)

        x2_written = cX2W.n
        w_es.close()

        if dbg & 8:
            def _dbgx(g):
                cX2W.wait(g, x2_written)
                ins = g.dma_start(out=dout["d_x2"], in_=x2f32)
                cDBG.tick(ins, 16)
            Gb(_dbgx)

        if stages == "ln1":
            def _fin(g):
                cX2W.wait(g, x2_written)
                cDBG.wait(g)
                ins = g.dma_start(out=out, in_=x2f32)
                cOUT.tick(ins, 16)
                cOUT.wait(g)
            Gb(_fin)
            return nc, dout

        scl = C(nc.sbuf_tensor("scl", [128, MC, 8], F32))
        x2T = C(nc.sbuf_tensor("x2T", [128, MC, TOK], BF16))

        g_es = ExitStack()
        gw_sb = g_es.enter_context(nc.sbuf_tensor("gw_sb", [128, MC, E], BF16))
        lgt = g_es.enter_context(nc.sbuf_tensor("lgt", [128, E], F32))
        gx = g_es.enter_context(nc.sbuf_tensor("gx", [128, E], F32))
        gsc = g_es.enter_context(nc.sbuf_tensor("gsc", [128, E], F32))
        m8 = g_es.enter_context(nc.sbuf_tensor("m8", [128, 8], F32))
        gcol2 = g_es.enter_context(nc.sbuf_tensor("gcol2", [128, 4], F32))
        wet = g_es.enter_context(nc.sbuf_tensor("wet", [64, 128], F32))
        ivb = g_es.enter_context(nc.sbuf_tensor("ivb", [64, 64], F32))
        poh = g_es.enter_context(nc.sbuf_tensor("poh", [64, 64], F32))
        ivr = g_es.enter_context(nc.sbuf_tensor("ivr", [1, 64], F32))
        ivri = g_es.enter_context(nc.sbuf_tensor("ivri", [1, 64], I32))
        fl = g_es.enter_context(nc.sbuf_tensor("fl", [128, E], F32))
        exs = g_es.enter_context(nc.sbuf_tensor("exs", [128, E], F32))
        sd8 = g_es.enter_context(nc.sbuf_tensor("sd8", [128, 8], F32))
        sfl = g_es.enter_context(nc.sbuf_tensor("sfl", [128, E], F32))
        sft_nt = g_es.enter_context(nc.sbuf_tensor("sft_nt", [8, G, 128], BF16))

        def _gg(g):
            cT.wait(g)
            cV.wait(g)
            cX2W.wait(g, x2_written)
            for mc in range(MC):
                ins = g.dma_start(out=gw_sb[:, mc, :], in_=gwT[ts(mc, 128), :])
                cGW.tick(ins, 16)
            ins = g.dma_start(out=ivri[:, :],
                              in_=invm.rearrange("(o a) b -> o (a b)", o=1))
            cGW.tick(ins, 16)
        Gb(_gg)

        def _yx2(y):
            cX2W.wait(y, x2_written)
            cT.wait(y)
            cV.wait(y)
            ins = None
            for nb in range(NQB):
                for mc in range(MC):
                    ins = y.dma_start_transpose(
                        x2T[:, mc, ts(nb, 512)],
                        x2b16[ts(nb, 512), ts(mc, 128)])
                    cX2T.tick(ins, 16)
        Yb(_yx2)
        x2T_ready = cX2T.n

        def _vp(v):
            cGW.wait(v)
            ins = v.tensor_copy(ivr[:, :], ivri[:, :])
            cV.tick(ins)
        Vb(_vp)

        def _tp(t):
            cV.wait(t)
            ins = t.matmul(pS3[0][0:64, 0:64], ones1f[0:1, 0:64], ivr[0:1, :],
                           start=True, stop=True)
            cT.tick(ins)
        Tb(_tp)

        def _vp2(v):
            cT.wait(v)
            ins = v.tensor_copy(ivb[0:64, :], pS3[0][0:64, 0:64])
            v.wait_ge(cV.h, cV.tick(ins))
            ins = v.tensor_scalar(poh[0:64, :], ivb[0:64, :], iotc[0:64, 0:1],
                                  None, ALU.is_equal)
            cV.tick(ins)
        Vb(_vp2)

        sft_w = {}
        for nt in range(MC):
            nb, nt4 = nt // 4, nt % 4
            lgmm = [0]
            flmm = [0]
            wemm = [0]
            sftr = {}

            def _tg(t, nt=nt, nb=nb, nt4=nt4, lgmm=lgmm):
                cX2T.wait(t, x2T_ready)
                cGW.wait(t)
                cV.wait(t)
                ins = None
                for mc in range(MC):
                    ins = t.matmul(pP[0][:, 0:64],
                                   x2T[:, mc, ts(nt, 128)],
                                   gw_sb[:, mc, :],
                                   start=(mc == 0), stop=(mc == MC - 1))
                lgmm[0] = cT.tick(ins)
            Tb(_tg)

            def _vg(v, nt=nt, lgmm=lgmm):
                cT.wait(v, lgmm[0])
                ins = v.tensor_copy(lgt[:, :], pP[0][:, 0:64])
                v.wait_ge(cV.h, cV.tick(ins))
                ins = v.reduce_max(
                    gcol2[:, 0:1].rearrange("p (a o) -> p a o", o=1),
                    lgt[:, :].rearrange("p (a f) -> p a f", a=1), AX.X)
                v.wait_ge(cV.h, cV.tick(ins))
                ins = v.tensor_scalar_mul(gcol2[:, 1:2], gcol2[:, 0:1], -1.0)
                cV.tick(ins)
            Vb(_vg)

            def _ag(s, nt=nt):
                cV.wait(s)
                ins = s.activation(gx[:, :], lgt[:, :], AF.Exp,
                                   bias=gcol2[:, 1:2],
                                   accum_out=gcol2[:, 2:3])
                cA.tick(ins)
            Ab(_ag)

            def _vg2(v, nt=nt):
                cA.wait(v)
                ins = v.reciprocal(gcol2[:, 3:4], gcol2[:, 2:3])
                v.wait_ge(cV.h, cV.tick(ins))
                ins = v.tensor_scalar(gsc[:, :], gx[:, :], gcol2[:, 3:4], None,
                                      ALU.mult)
                v.wait_ge(cV.h, cV.tick(ins))
                ins = v.max(m8[:, :], gsc[:, :])
                v.wait_ge(cV.h, cV.tick(ins))
                ins = v.tensor_scalar(lgt[:, :], gsc[:, :], m8[:, 7:8], None,
                                      ALU.is_ge)
                v.wait_ge(cV.h, cV.tick(ins))
                ins = v.tensor_tensor(gx[:, :], gsc[:, :], lgt[:, :], ALU.mult)
                cV.tick(ins)
            Vb(_vg2)

            def _tg2(t, nt=nt, wemm=wemm):
                cV.wait(t)
                ins = t.transpose(pS3[0][0:64, 0:128], gx[:, :], identf[:, :])
                wemm[0] = cT.tick(ins)
            Tb(_tg2)

            def _vg3(v, nt=nt, wemm=wemm):
                cT.wait(v, wemm[0])
                ins = v.tensor_copy(wet[0:64, :], pS3[0][0:64, 0:128])
                cV.tick(ins)
            Vb(_vg3)

            def _tg3(t, nt=nt, flmm=flmm):
                cV.wait(t)
                ins = t.matmul(pP[1][:, 0:64], wet[0:64, :], poh[0:64, :],
                               start=True, stop=True)
                flmm[0] = cT.tick(ins)
            Tb(_tg3)

            def _vg4(v, nt=nt, flmm=flmm):
                cT.wait(v, flmm[0])
                ins = v.tensor_copy(fl[:, :], pP[1][:, 0:64])
                v.wait_ge(cV.h, cV.tick(ins))
                v.tensor_reduce(
                    scl[:, nt, :].rearrange("p (g o) -> p g o", o=1),
                    fl[:, :].rearrange("p (g s) -> p g s", g=G),
                    AX.X, ALU.add)
                ins = v.tensor_scalar(lgt[:, :], fl[:, :], 0.0, None,
                                      ALU.is_equal)
                v.wait_ge(cV.h, cV.tick(ins))
                ins = v.scalar_tensor_tensor(lgt[:, :], lgt[:, :], NEG,
                                             fl[:, :], ALU.mult, ALU.add)
                cV.tick(ins)
            Vb(_vg4)

            def _ag2(s, nt=nt):
                cV.wait(s)
                ins = s.activation(exs[:, :], lgt[:, :], AF.Exp)
                cA.tick(ins)
            Ab(_ag2)

            def _vg5(v, nt=nt):
                cA.wait(v)
                ins = v.tensor_reduce(
                    sd8[:, :].rearrange("p (g o) -> p g o", o=1),
                    exs[:, :].rearrange("p (g s) -> p g s", g=G),
                    AX.X, ALU.add)
                v.wait_ge(cV.h, cV.tick(ins))
                ins = v.tensor_scalar_max(sd8[:, :], sd8[:, :], 1e-30)
                v.wait_ge(cV.h, cV.tick(ins))
                ins = v.reciprocal(sd8[:, :], sd8[:, :])
                v.wait_ge(cV.h, cV.tick(ins))
                ins = None
                for g in range(G):
                    ins = v.tensor_scalar(sfl[:, ts(g, GS)], exs[:, ts(g, GS)],
                                          sd8[:, g:g + 1], None, ALU.mult)
                cV.tick(ins)
            Vb(_vg5)

            vg6_base = cV.n
            sftw_prev = sft_w.get(nt - 1, 0)

            def _tg4(t, nt=nt, sftr=sftr, vg6_base=vg6_base):
                cV.wait(t, vg6_base)
                for g in range(G):
                    reg = (pS3[1][0:8, 0:128] if g % 2 == 0
                           else pS3[1][0:8, 512:640])
                    if g >= 2:
                        cV.wait(t, vg6_base + g - 1)
                    ins = t.transpose(reg, sfl[:, ts(g, GS)], identf[:, :])
                    sftr[g] = cT.tick(ins)
            Tb(_tg4)

            def _vg6(v, nt=nt, sftr=sftr, vg6_base=vg6_base,
                     sftw_prev=sftw_prev):
                cSFT.wait(v, sftw_prev)
                for g in range(G):
                    reg = (pS3[1][0:8, 0:128] if g % 2 == 0
                           else pS3[1][0:8, 512:640])
                    cT.wait(v, sftr[g])
                    ins = v.tensor_copy(sft_nt[0:8, g, :], reg)
                    got = cV.tick(ins)
                    assert got == vg6_base + g + 1
            Vb(_vg6)

            def _yg6(y, nt=nt):
                cV.wait(y)
                ins = y.dma_start(
                    out=sftd[:, :, ts(nt, 128)].rearrange("g s c -> s g c"),
                    in_=sft_nt[0:8, :, :])
                sft_w[nt] = cSFT.tick(ins, 16)
            Yb(_yg6)

        sft_written = cSFT.n
        g_es.close()

        if dbg & 16:
            def _dbgg(g):
                cV.wait(g)
                cSFT.wait(g, sft_written)
                cDBG.tick(g.dma_start(out=dout["d_scl"], in_=scl[:, :, :]), 16)
                cDBG.tick(g.dma_start(out=dout["d_sft"], in_=sftd), 16)
            Gb(_dbgg)

        f_es = ExitStack()
        aT = f_es.enter_context(nc.sbuf_tensor("aT", [128, IC, TOK], BF16))
        dn_sb = f_es.enter_context(nc.sbuf_tensor("dn_sb", [128, IC, H], BF16))
        guw = [f_es.enter_context(
            nc.sbuf_tensor(f"guw{i}", [128, MC, 2, 128], BF16))
            for i in range(2)]
        mun = f_es.enter_context(nc.sbuf_tensor("mun", [128, MC, GS], BF16))
        sft_g = f_es.enter_context(nc.sbuf_tensor("sft_g", [8, TOK], BF16))
        wmg = f_es.enter_context(nc.sbuf_tensor("wmg", [8, 128], BF16))
        wmu = f_es.enter_context(nc.sbuf_tensor("wmu", [8, 128], BF16))
        sil = f_es.enter_context(nc.sbuf_tensor("sil", [128, 512], BF16))
        dsc = f_es.enter_context(nc.sbuf_tensor("dsc", [128, 2, H], F32))

        guw_ready = {}
        guw_used = {}
        dn_ready = {}
        dn_used = {}
        mun_ready = {}
        yacc = {}
        yfin = {}

        for g in range(G):
            tgu_base = [0]

            def _gf(gg, g=g, tgu_base=tgu_base):
                tgu_base[0] = cT.n
                cSFT.wait(gg, sft_written)
                if g == 0:
                    cT.wait(gg)
                    cV.wait(gg)
                if g >= 1:
                    cT.wait(gg, guw_used.get((g - 1, "mun"), 0))
                    cT.wait(gg, guw_used[(g - 1, IC - 1)])
                ins = gg.dma_start(out=sft_g[0:8, :], in_=sftd[g])
                cMU2.tick(ins, 16)
                ins = gg.dma_start(
                    out=mun[:, :, :],
                    in_=muo[g].rearrange("(c p) s -> p c s", p=128))
                mun_ready[g] = cMU2.tick(ins, 16)
                for ic in range(IC):
                    slot = ic % 2
                    if ic >= 2:
                        cT.wait(gg, tgu_base[0] + 9 * (ic - 1))
                    elif g >= 1:
                        cT.wait(gg, guw_used[(g - 1, IC - 2 + ic)])
                    for half, w in ((0, gpT), (1, upT)):
                        for q in range(4):
                            ins = gg.dma_start(
                                out=guw[slot][:, ts(q, 4), half, :],
                                in_=w[g][:, ts(ic, 128)].rearrange(
                                    "(c p) i -> p c i", p=128)[:, ts(q, 4), :])
                            cGU[slot].tick(ins, 16)
                    guw_ready[(g, ic)] = cGU[slot].n
                if g >= 1:
                    cT.wait(gg, dn_used[g - 1])
                for q in range(4):
                    sl = bass.ds(q * 3, 3) if q < 3 else bass.ds(9, 2)
                    ins = gg.dma_start(
                        out=dn_sb[:, sl, :],
                        in_=dnT[g].rearrange("(c p) h -> p c h",
                                             p=128)[:, sl, :])
                    cDN.tick(ins, 16)
                dn_ready[g] = cDN.n
            Gb(_gf)

            gu_mm = {}
            wm_mm = {}
            agu_base = [0]
            vgu_base = [0]

            def _tgu(t, g=g, gu_mm=gu_mm, wm_mm=wm_mm, agu_base=agu_base,
                     vgu_base=vgu_base):
                agu_base[0] = cA.n
                vgu_base[0] = cV.n
                cMU2.wait(t, mun_ready[g])
                for ic in range(IC):
                    cGU[ic % 2].wait(t, guw_ready[(g, ic)])
                    if ic >= 1:
                        cV.wait(t, vgu_base[0] + (ic - 1) * 5 + 1)
                    ins = None
                    for half in range(2):
                        reg = (pS3[0][0:8, 0:128] if half == 0
                               else pS3[0][0:8, 512:640])
                        for mc in range(MC):
                            ins = t.matmul(reg, mun[:, mc, :],
                                           guw[ic % 2][:, mc, half, :],
                                           start=(mc == 0), stop=(mc == MC - 1))
                    wm_mm[ic] = cT.tick(ins)
                    for nb in range(NQB):
                        cA.wait(t, agu_base[0] + ic * 4 + nb)
                        cV.wait(t, vgu_base[0] + ic * 5 + (1 if nb == 0
                                                           else nb + 1))
                        ins = None
                        for mc in range(MC):
                            ins = t.matmul(pP[0][:, :],
                                           guw[ic % 2][:, mc, 0, :],
                                           x2T[:, mc, ts(nb, 512)],
                                           start=(mc == 0), stop=False)
                        ins = t.matmul(pP[0][:, :], wmg[0:8, :],
                                       sft_g[0:8, ts(nb, 512)],
                                       start=False, stop=True)
                        gmm = cT.tick(ins)
                        ins = None
                        for mc in range(MC):
                            ins = t.matmul(pP[1][:, :],
                                           guw[ic % 2][:, mc, 1, :],
                                           x2T[:, mc, ts(nb, 512)],
                                           start=(mc == 0), stop=False)
                        ins = t.matmul(pP[1][:, :], wmu[0:8, :],
                                       sft_g[0:8, ts(nb, 512)],
                                       start=False, stop=True)
                        umm = cT.tick(ins)
                        gu_mm[(ic, nb)] = (gmm, umm)
                    guw_used[(g, ic)] = cT.n
                    assert cT.n == tgu_base[0] + 9 * (ic + 1)
                guw_used[(g, "mun")] = cT.n
            Tb(_tgu)

            def _agu(s, g=g, gu_mm=gu_mm, agu_base=agu_base,
                     vgu_base=vgu_base):
                for ic in range(IC):
                    for nb in range(NQB):
                        gmm, umm = gu_mm[(ic, nb)]
                        cT.wait(s, gmm)
                        cV.wait(s, vgu_base[0] + (ic * 5 if nb == 0
                                                  else ic * 5 + nb + 1))
                        ins = s.activation(sil[:, :], pP[0][:, :], AF.Silu)
                        got = cA.tick(ins)
                        assert got == agu_base[0] + ic * 4 + nb + 1
            Ab(_agu)

            def _vgu(v, g=g, gu_mm=gu_mm, wm_mm=wm_mm, agu_base=agu_base,
                     vgu_base=vgu_base):
                if g >= 1:
                    cT.wait(v, dn_used[g - 1])
                for ic in range(IC):
                    cT.wait(v, wm_mm[ic])
                    if ic >= 1:
                        cT.wait(v, gu_mm[(ic - 1, NQB - 1)][1])
                    v.tensor_copy(wmg[0:8, :], pS3[0][0:8, 0:128])
                    ins = v.tensor_copy(wmu[0:8, :], pS3[0][0:8, 512:640])
                    got = cV.tick(ins)
                    assert got == vgu_base[0] + ic * 5 + 1, (got, vgu_base, ic)
                    for nb in range(NQB):
                        gmm, umm = gu_mm[(ic, nb)]
                        cT.wait(v, umm)
                        cA.wait(v, agu_base[0] + ic * 4 + nb + 1)
                        ins = v.tensor_tensor(aT[:, ic, ts(nb, 512)],
                                              sil[:, :], pP[1][:, :], ALU.mult)
                        got = cV.tick(ins)
                        assert got == vgu_base[0] + ic * 5 + nb + 2
            Vb(_vgu)


            dmm = {}
            vdn_base = [0]

            def _tdn(t, g=g, dmm=dmm, vdn_base=vdn_base):
                vdn_base[0] = cV.n
                cDN.wait(t, dn_ready[g])
                cV.wait(t)
                for nt in range(MC):
                    for hb in range(4):
                        cV.wait(t, vdn_base[0] + max(0, (nt - 1) * 4 + hb + 1))
                        dreg = (pS3[1][:, ts(hb, 512)] if hb < 3
                                else pS3[0][:, 1024:1536])
                        ins = None
                        for ic in range(IC):
                            ins = t.matmul(dreg,
                                           aT[:, ic, ts(nt, 128)],
                                           dn_sb[:, ic, ts(hb, 512)],
                                           start=(ic == 0), stop=(ic == IC - 1))
                        dmm[(nt, hb)] = cT.tick(ins)
                dn_used[g] = cT.n
            Tb(_tdn)

            def _vdn(v, g=g, dmm=dmm, vdn_base=vdn_base):
                ybase = (cY[0].n, cY[1].n)
                for nt in range(MC):
                    par = nt % 2
                    if nt >= 2:
                        cY[par].wait(v, ybase[par] + 16 * (nt // 2))
                    else:
                        cY[par].wait(v, ybase[par])
                    for hb in range(4):
                        cT.wait(v, dmm[(nt, hb)])
                        dreg = (pS3[1][:, ts(hb, 512)] if hb < 3
                                else pS3[0][:, 1024:1536])
                        ins = v.tensor_scalar(dsc[:, nt % 2, ts(hb, 512)],
                                              dreg,
                                              scl[:, nt, g:g + 1], None,
                                              ALU.mult)
                        got = cV.tick(ins)
                        assert got == vdn_base[0] + nt * 4 + hb + 1
            Vb(_vdn)

            def _gdn(gg, g=g, vdn_base=vdn_base):
                ybase2 = (cY[0].n, cY[1].n)
                for nt in range(MC):
                    cV.wait(gg, vdn_base[0] + nt * 4 + 4)
                    if g >= 1:
                        cY[nt % 2].wait(gg, yfin[g - 1][nt % 2])
                    ins = gg.dma_start(out=y32[ts(nt, 128), :],
                                       in_=dsc[:, nt % 2, :],
                                       accum_op=(ALU.bypass if g == 0
                                                 else ALU.add))
                    yacc[(g, nt)] = cY[nt % 2].tick(ins, 16)
                    assert yacc[(g, nt)] == ybase2[nt % 2] + 16 * (nt // 2 + 1)
                yfin[g] = (cY[0].n, cY[1].n)
            Gb(_gdn)

        f_es.close()

        if dbg & 32:
            def _dbgy(g):
                cY[0].wait(g)
                cY[1].wait(g)
                ins = g.dma_start(out=dout["d_y"], in_=y32)
                cDBG.tick(ins, 16)
            Gb(_dbgy)

        with (nc.sbuf_tensor("yl", [128, 2, H], F32) as yl,
              nc.sbuf_tensor("x2l", [128, 2, H], F32) as x2l,
              nc.sbuf_tensor("z2", [128, 2, H], F32) as z2,
              nc.sbuf_tensor("sq2", [128, 2, H], BF16) as sq2,
              nc.sbuf_tensor("ocol", [128, 2, 6], F32) as ocol,
              nc.sbuf_tensor("o48", [128, 2, 8], F32) as o48,
              nc.sbuf_tensor("ob", [128, 2, H], F32) as ob):
            yload = {}
            xload = {}
            oln = {}
            out_mark = {}
            amark = {}
            a2mark = {}

            for nt in range(MC):
                pb = nt % 2

                def _yl(y, nt=nt, pb=pb):
                    if nt == 0:
                        cT.wait(y)
                        cV.wait(y)
                    cY[0].wait(y)
                    cY[1].wait(y)
                    cV.wait(y, oln.get(nt - 2, 0))
                    ins = y.dma_start(out=yl[:, pb, :],
                                      in_=y32[ts(nt, 128), :])
                    yload[nt] = cYL[pb].tick(ins, 16)
                Yb(_yl)

                def _axl(a, nt=nt, pb=pb):
                    if nt == 0:
                        cV.wait(a)
                    cV.wait(a, oln.get(nt - 2, 0))
                    ins = a.dma_start(out=x2l[:, pb, :],
                                      in_=x2f32[ts(nt, 128), :])
                    xload[nt] = cXL[pb].tick(ins, 16)
                Ab(_axl)

                vt = [0]

                def _vl(v, nt=nt, pb=pb, vt=vt):
                    cYL[pb].wait(v, yload[nt])
                    cXL[pb].wait(v, xload[nt])
                    cA.wait(v, a2mark.get(nt - 2, 0))
                    ins = None
                    for mb in range(4):
                        ins = v.scalar_tensor_tensor(
                            z2[:, pb, ts(mb, 512)],
                            yl[:, pb, ts(mb, 512)], 1.0,
                            x2l[:, pb, ts(mb, 512)], ALU.mult, ALU.add,
                            accum_out=o48[:, pb, mb:mb + 1])
                    v.wait_ge(cV.h, cV.tick(ins))
                    ins = v.tensor_reduce(
                        ocol[:, pb, 0:1].rearrange("p (a o) -> p a o", o=1),
                        o48[:, pb, 0:4].rearrange("p (a f) -> p a f", a=1),
                        AX.X, ALU.add)
                    v.wait_ge(cV.h, cV.tick(ins))
                    ins = v.tensor_scalar_mul(ocol[:, pb, 1:2],
                                              ocol[:, pb, 0:1], 1.0 / H)
                    vt[0] = cV.tick(ins)
                Vb(_vl)

                def _al(s, nt=nt, pb=pb, vt=vt):
                    cV.wait(s, vt[0])
                    ins = None
                    for mb in range(4):
                        ins = s.activation(sq2[:, pb, ts(mb, 512)],
                                           z2[:, pb, ts(mb, 512)], AF.Square,
                                           accum_out=o48[:, pb, 4 + mb:5 + mb])
                    amark[nt] = cA.tick(ins)
                Ab(_al)

                def _vl2(v, nt=nt, pb=pb):
                    cA.wait(v, amark[nt])
                    ins = v.tensor_reduce(
                        ocol[:, pb, 2:3].rearrange("p (a o) -> p a o", o=1),
                        o48[:, pb, 4:8].rearrange("p (a f) -> p a f", a=1),
                        AX.X, ALU.add)
                    v.wait_ge(cV.h, cV.tick(ins))
                    ins = v.tensor_scalar_mul(ocol[:, pb, 2:3],
                                              ocol[:, pb, 2:3], 1.0 / H)
                    cV.tick(ins)
                    ins = v.tensor_tensor(ocol[:, pb, 3:4], ocol[:, pb, 1:2],
                                          ocol[:, pb, 1:2], ALU.mult)
                    v.wait_ge(cV.h, cV.tick(ins))
                    ins = v.tensor_tensor(ocol[:, pb, 3:4], ocol[:, pb, 2:3],
                                          ocol[:, pb, 3:4], ALU.subtract)
                    v.wait_ge(cV.h, cV.tick(ins))
                    ins = v.tensor_scalar_add(ocol[:, pb, 3:4],
                                              ocol[:, pb, 3:4], EPS)
                    cV.tick(ins)
                Vb(_vl2)

                def _al2(s, nt=nt, pb=pb):
                    cV.wait(s)
                    ins = s.activation(ocol[:, pb, 4:5], ocol[:, pb, 3:4],
                                       AF.Sqrt)
                    a2mark[nt] = cA.tick(ins)
                Ab(_al2)

                def _vl3(v, nt=nt, pb=pb):
                    cA.wait(v, a2mark[nt])
                    cOB[pb].wait(v, out_mark.get(nt - 2, 0))
                    ins = v.reciprocal(ocol[:, pb, 5:6], ocol[:, pb, 4:5])
                    v.wait_ge(cV.h, cV.tick(ins))
                    ins = None
                    for mb in range(4):
                        ins = v.tensor_scalar(ob[:, pb, ts(mb, 512)],
                                              z2[:, pb, ts(mb, 512)],
                                              ocol[:, pb, 1:2],
                                              ocol[:, pb, 5:6],
                                              ALU.subtract, ALU.mult)
                    oln[nt] = cV.tick(ins)
                Vb(_vl3)

                def _yo(g, nt=nt, pb=pb):
                    cV.wait(g, oln[nt])
                    ins = g.dma_start(out=out[ts(nt, 128), :], in_=ob[:, pb, :])
                    out_mark[nt] = cOB[pb].tick(ins, 16)
                Gb(_yo)

            def _fin(y):
                cOB[0].wait(y)
                cOB[1].wait(y)
                if dbg:
                    cDBG.wait(y)
            Yb(_fin)

    return nc, dout



_NC_CACHE = {}


def _get_nc(dbg=0, npairs=NPAIR, stages="full"):
    key = (dbg, npairs, stages)
    if key not in _NC_CACHE:
        _NC_CACHE[key] = build(dbg, npairs, stages)
    return _NC_CACHE[key]


def host_prep(inputs):
    f32 = np.float32
    x = np.asarray(inputs["L0_attn_hidden_states"], dtype=f32)
    wq = np.asarray(inputs["L0_attn_Wq"], dtype=f32)
    wk = np.asarray(inputs["L0_attn_Wk"], dtype=f32)
    wv = np.asarray(inputs["L0_attn_Wv"], dtype=f32)
    wo = np.asarray(inputs["L0_attn_Wo"], dtype=f32)
    gate = np.asarray(inputs["L0_attn_gate"], dtype=f32).reshape(1, NH)
    gw = np.asarray(inputs["L0_ffn_gate_w"], dtype=f32)
    mu = np.asarray(inputs["L0_ffn_mask_up_w"], dtype=f32)
    gp = np.asarray(inputs["L0_ffn_gate_proj_w"], dtype=f32)
    up = np.asarray(inputs["L0_ffn_up_proj_w"], dtype=f32)
    dn = np.asarray(inputs["L0_ffn_down_proj_w"], dtype=f32)
    inv = np.asarray(inputs["L0_ffn_inv_map"], dtype=np.int32)

    ac = np.ascontiguousarray
    xT = ac(x.transpose(0, 2, 1))
    shared = dict(
        wqT=ac(wq.T), wkT=ac(wk.T), wvT=ac(wv.T), woT=ac(wo.T),
        gt2=ac(gate), gwT=ac(gw.T), muo=ac(mu),
        gpT=ac(gp.transpose(0, 2, 1)), upT=ac(up.transpose(0, 2, 1)),
        dnT=ac(dn.transpose(0, 2, 1)), invm=ac(inv),
        iot=np.arange(E, dtype=f32).reshape(E, 1),
        idn=np.eye(128, dtype=f32),
    )
    inv_freq = (1.0 / np.float32(10000.0) ** (np.arange(0, H, 2,
                dtype=f32) / np.float32(H))).astype(f32)
    t = np.arange(S, dtype=f32)
    ang = (inv_freq[:, None] * t[None, :]).astype(f32)
    cosk = np.cos(ang).astype(f32)
    sink = np.sin(ang).astype(f32)
    shared["csk"] = ac(np.stack([cosk, sink], axis=1))

    in_maps = []
    for c in range(8):
        b, ch = c // 4, c % 4
        m = dict(shared)
        m["xTb"] = ac(xT[b])
        m["xqT"] = ac(xT[b][:, ch * TOK:(ch + 1) * TOK])
        m["xch"] = ac(x[b, ch * TOK:(ch + 1) * TOK, :])
        m["csq"] = ac(shared["csk"][:, :, ch * TOK:(ch + 1) * TOK])
        in_maps.append(m)
    return in_maps


def kernel(**inputs):
    nc, _ = _get_nc()
    in_maps = host_prep(inputs)
    res = run_bass_kernel_spmd(nc, in_maps, core_ids=list(range(8)))
    outp = np.empty((B, S, H), dtype=np.float32)
    for c in range(8):
        b, ch = c // 4, c % 4
        outp[b, ch * TOK:(ch + 1) * TOK, :] = res.results[c]["out"]
    return outp
